# version 04
# baseline (speedup 1.0000x reference)
import contextlib
import numpy as np
import ml_dtypes
import concourse.bass as bass
import concourse.mybir as mybir
from concourse.bass_utils import run_bass_kernel_spmd
from concourse.dve_op import DveOp
from concourse.dve_spec import Spec, Src0, Src1, lower, C0, C1, C2, eq, select
import concourse.dve_ops as dve_ops

F32 = mybir.dt.float32
F16 = mybir.dt.float16
BF16 = mybir.dt.bfloat16
FP8 = mybir.dt.float8e4
AF = mybir.ActivationFunctionType
ALU = mybir.AluOpType
AX = mybir.AxisListType
LN2 = 0.6931471805599453

S = 2048
D = 1024
NT = 16
NCH = 8
NH = 16
NP = 8
DH = 64
HF = 4096
NHF = 32
GQ = 512
NG = S // GQ
USOFT = 12.0
EPS = 1e-5

_OPS = {}


def _fdt(x):
    return np.float64 if isinstance(x, np.ndarray) and x.dtype == np.float64 else np.float32


def _reg(name, spec_body, reference, row):
    spec = Spec(body=spec_body)
    sha = lower(spec, ver="v3", perf_en=False, name=name).sha("v3")
    op = DveOp(name, spec, subdim=False, uops_sha={"v3": sha}, reference=reference)
    if not any(o.name == name for o in dve_ops.OPS):
        dve_ops.OPS.append(op)
        dve_ops._SUB_OPCODE_FOR_NAME[name] = row
        dve_ops.CUSTOM_DVE_SPECS[name] = spec
        dve_ops.CUSTOM_DVE_REFS[name] = op.ref
    return op


def _register_ops():
    if _OPS:
        return
    b4 = ((Src0 >= C0) + (Src0 >= C1)) + ((Src0 >= C2) + (Src0 >= C0 * C2))

    def r4(in0, in1, s0, s1, imm2):
        t = _fdt(in0); x = in0.astype(t)
        return ((x >= s0).astype(t) + (x >= s1).astype(t)
                + (x >= imm2).astype(t) + (x >= s0 * imm2).astype(t))

    def r4b(in0, in1, s0, s1, imm2):
        return r4(in0, in1, s0, s1, imm2) + in1.astype(_fdt(in0))

    def r1b(in0, in1, s0, s1, imm2):
        t = _fdt(in0); x = in0.astype(t)
        return (x >= s0).astype(t) + in1.astype(t)

    t = Src0 * Src1 + C0
    t = t - (t >= C1) * C1
    t = t - (t >= C2) * C2

    def rmp2(in0, in1, s0, s1, imm2):
        tt = _fdt(in0)
        x = in0.astype(tt) * in1.astype(tt) + s0
        x = x - (x >= s1).astype(tt) * s1
        return x - (x >= imm2).astype(tt) * imm2

    p = Src0 + C0
    p = p - (p >= C1) * C1
    p = p - (p >= C2) * C2

    def rp2(in0, in1, s0, s1, imm2):
        tt = _fdt(in0)
        x = in0.astype(tt) + s0
        x = x - (x >= s1).astype(tt) * s1
        return x - (x >= imm2).astype(tt) * imm2

    q = Src0
    q = q - (q >= C0) * C0
    qw = (q - C1) * Src1

    def rpw(in0, in1, s0, s1, imm2):
        tt = _fdt(in0)
        x = in0.astype(tt)
        x = x - (x >= s0).astype(tt) * s0
        return (x - s1) * in1.astype(tt)

    qe = Src0
    qe = qe - (qe >= C0) * C0
    qwe = (qe - C1 + eq(Src0, C0)) * Src1

    def rpwe(in0, in1, s0, s1, imm2):
        tt = _fdt(in0)
        x = in0.astype(tt)
        r = x - (x >= s0).astype(tt) * s0
        return (r - s1 + (x == s0).astype(tt)) * in1.astype(tt)

    b0t = Src0 >= C0
    qwt = (((Src0 - b0t * C0) - C1) + (b0t - (Src0 >= C2))) * Src1

    def rpwt(in0, in1, s0, s1, imm2):
        tt = _fdt(in0)
        x = in0.astype(tt)
        r = x - (x >= s0).astype(tt) * s0
        return (r - s1 + (x >= s0).astype(tt) * (x < imm2).astype(tt)) \
            * in1.astype(tt)

    r_ = Src0
    r_ = r_ - (r_ >= C0) * C0
    r_ = r_ - (r_ >= C2) * C2
    rw = (r_ - C1) * Src1

    def rp2w(in0, in1, s0, s1, imm2):
        tt = _fdt(in0)
        x = in0.astype(tt)
        x = x - (x >= s0).astype(tt) * s0
        x = x - (x >= imm2).astype(tt) * imm2
        return (x - s1) * in1.astype(tt)

    _OPS["LAD4"] = _reg("QLAD4", b4, r4, 30)
    _OPS["LAD4B"] = _reg("QLAD4B", b4 + Src1, r4b, 31)
    _OPS["LAD1B"] = _reg("QLAD1B", (Src0 >= C0) + Src1, r1b, 1)
    _OPS["MULPEEL2"] = _reg("QMULPEEL2", t, rmp2, 2)
    _OPS["PEEL2"] = _reg("QPEEL2", p, rp2, 3)
    _OPS["PEELW"] = _reg("QPEELW", qw, rpw, 4)
    _OPS["PEEL2W"] = _reg("QPEEL2W", rw, rp2w, 5)
    _OPS["PEELWE"] = _reg("QPEELWE", qwe, rpwe, 6)

    hv1 = select(Src0 >= C0, Src1 * C2, Src1)
    hv2 = select(Src0 >= C1, hv1 * C2, hv1)

    def rhalve2(in0, in1, s0, s1, imm2):
        tt = _fdt(in0)
        x = in0.astype(tt)
        t = np.where(x >= s0, in1.astype(tt) * imm2, in1.astype(tt))
        return np.where(x >= s1, t * imm2, t)

    sp = Src0 * C0 + C1
    sp = sp - (sp >= C2) * C2

    def rscpeel(in0, in1, s0, s1, imm2):
        tt = _fdt(in0)
        u = in0.astype(tt) * s0 + s1
        return u - (u >= imm2).astype(tt) * imm2

    _OPS["HALVE2"] = _reg("QHALVE2", hv2, rhalve2, 7)
    _OPS["SCPEEL"] = _reg("QSCPEEL", sp, rscpeel, 8)
    _OPS["PEELWET"] = _reg("QPEELWET", qwt, rpwt, 9)

    pc = Src0 + C0
    pc = pc - (pc >= C1) * C1
    pc = pc - (pc >= C2) * C1

    def rp2c(in0, in1, s0, s1, imm2):
        tt = _fdt(in0)
        x = in0.astype(tt) + s0
        x = x - (x >= s1).astype(tt) * s1
        return x - (x >= imm2).astype(tt) * s1

    _OPS["PEEL2C"] = _reg("QPEEL2C", pc, rp2c, 10)


_register_ops()

_HI = float(np.float32(LN2))
_LO = float(np.float32(LN2 - _HI))
_B1_8 = float(np.float32(9 * _HI))
_B2_8 = float(np.float32(9 * LN2 - _B1_8))
_B1_16 = float(np.float32(24 * _HI))
_B2_16 = float(np.float32(24 * LN2 - _B1_16))
CST = {
    "is8": 9 * LN2, "sc8": -9 * LN2, "is16": 24 * LN2, "sc16": -24 * LN2,
    "pln2": LN2, "nln2": -LN2, "eighth": 0.125, "negu": -USOFT, "eps": EPS,
    "neghalf": -0.5, "one": 1.0,
    "nhi": -_HI, "phi": _HI, "nlo": -_LO, "plo": _LO,
    "i8h": _B1_8, "i8l": _B2_8, "s8h": -_B1_8, "s8l": -_B2_8,
    "i16h": _B1_16, "i16l": _B2_16, "s16h": -_B1_16, "s16l": -_B2_16,
}
CSTI = {k: i for i, k in enumerate(CST)}
NCST = len(CST)

ENGINES = ("sync", "tensor", "vector", "scalar", "gpsimd")
SEMS = ("din", "dg", "dz", "dwr", "dout", "t", "v", "s", "g")


class Prog:

    def __init__(self):
        self.counts = {sn: 0 for sn in SEMS}
        self.ops = {e: [] for e in ENGINES}
        self.known = {e: {} for e in ENGINES}

    def now(self, name):
        return self.counts[name]

    def op(self, eng, emit, wait=(), inc=()):
        w = []
        for sname, val in wait:
            if val <= 0:
                continue
            if self.known[eng].get(sname, 0) >= val:
                continue
            self.known[eng][sname] = val
            w.append((sname, val))
        incs = []
        for sname, amt in inc:
            self.counts[sname] += amt
            incs.append((sname, amt))
            self.known[eng][sname] = max(self.known[eng].get(sname, 0),
                                         self.counts[sname])
        self.ops[eng].append([w, emit, incs])

    def bump(self, eng, sname, amt=1):
        self.counts[sname] += amt
        self.ops[eng][-1][2].append((sname, amt))
        self.known[eng][sname] = max(self.known[eng].get(sname, 0),
                                     self.counts[sname])
        return self.counts[sname]


def build_nc(debug=None):
    nc = bass.Bass("TRN2", target_bir_lowering=False, detect_race_conditions=False)

    xin = nc.dram_tensor("xin", [S, D], F32, kind="ExternalInput")
    wqp = nc.dram_tensor("wqp", [NP, D, 128], F32, kind="ExternalInput")
    wkp = nc.dram_tensor("wkp", [NP, D, 128], F32, kind="ExternalInput")
    wvp = nc.dram_tensor("wvp", [NP, D, 128], F32, kind="ExternalInput")
    wop = nc.dram_tensor("wop", [NCH, 128, D], F32, kind="ExternalInput")
    w1t = nc.dram_tensor("w1t", [NHF, NCH, 128, 128], F32, kind="ExternalInput")
    w2t = nc.dram_tensor("w2t", [NHF, 128, D], F32, kind="ExternalInput")
    b1in = nc.dram_tensor("b1in", [128, NHF], F32, kind="ExternalInput")
    boutbc_in = nc.dram_tensor("boutbc_in", [128, D], F32, kind="ExternalInput")
    b2bc_in = nc.dram_tensor("b2bc_in", [128, D], F32, kind="ExternalInput")
    cst_in = nc.dram_tensor("cst_in", [128, NCST], F32, kind="ExternalInput")
    ident_in = nc.dram_tensor("ident_in", [128, 128], BF16, kind="ExternalInput")
    trimask_in = nc.dram_tensor("trimask_in", [128, 128], BF16, kind="ExternalInput")
    bcast64_in = nc.dram_tensor("bcast64_in", [128, 64], BF16, kind="ExternalInput")
    xout = nc.dram_tensor("xout", [S, D], F32, kind="ExternalOutput")
    w2q_dram = nc.dram_tensor("w2q_dram", [NHF, 128, D], FP8, kind="Internal")

    P = Prog()

    with contextlib.ExitStack() as est:
        E = est.enter_context
        sems = {sn: E(nc.semaphore(sn)) for sn in SEMS}

        x_tm = E(nc.sbuf_tensor("x_tm", [128, NT * D], F32))
        ln_fm = E(nc.sbuf_tensor("ln_fm", [128, NCH * S], BF16))
        cstb = E(nc.sbuf_tensor("cstb", [128, NCST], F32))
        ident = E(nc.sbuf_tensor("ident", [128, 128], BF16))
        trimask = E(nc.sbuf_tensor("trimask", [128, 128], BF16))
        bcast64 = E(nc.sbuf_tensor("bcast64", [128, 64], BF16))
        b1_sb = E(nc.sbuf_tensor("b1_sb", [128, NHF], F32))
        boutbc = E(nc.sbuf_tensor("boutbc", [128, D], F32))
        b2bc = E(nc.sbuf_tensor("b2bc", [128, D], F32))
        bqbc = E(nc.sbuf_tensor("bqbc", [128, D], F32))
        stat = E(nc.sbuf_tensor("stat", [128, 128], F32))
        scratch = E(nc.sbuf_tensor("scratch", [128, 2048], F32))
        sq = scratch[:, 0:1024]

        scope = {}

        def open_scope(name, specs):
            stk = contextlib.ExitStack()
            scope[name] = stk
            return {nm: stk.enter_context(nc.sbuf_tensor(f"{name}_{nm}", sh, dt))
                    for nm, sh, dt in specs}

        def close_scope(name):
            scope.pop(name).close()

        def close_all_scopes():
            for name in reversed(list(scope)):
                scope.pop(name).close()

        ps_a = E(nc.psum_tensor("ps_a", [128, 512], F32))
        ps_b = E(nc.psum_tensor("ps_b", [128, 512], F32))
        ps_z = E(nc.psum_tensor("ps_z", [128, 512], F32))
        ps_r = E(nc.psum_tensor("ps_r", [128, 512], F32))

        def cc(name, lo=0, hi=128):
            return cstb[lo:hi, CSTI[name]:CSTI[name] + 1]

        def _ld(dst_ap, src_ap):
            return lambda e, d=dst_ap, s=src_ap: e.dma_start(d, s)

        def _ms(ap, val):
            return lambda e, a=ap, vv=val: e.memset(a, vv)

        def ts(out, in0, s1, s2, op0, op1=None):
            if op1 is None:
                return lambda e: e.tensor_scalar(out, in0, s1, None, op0)
            return lambda e: e.tensor_scalar(out, in0, s1, s2, op0, op1)

        def tt(out, in0, in1, op):
            return lambda e: e.tensor_tensor(out, in0, in1, op)

        def sttf(out, in0, sc, in1, op0, op1, accum=None):
            return lambda e: e.scalar_tensor_tensor(out, in0, sc, in1, op0, op1,
                                                    accum_out=accum)

        def act(out, in_, func, bias=0.0, scale=1.0):
            return lambda e: e.activation(out, in_, func, bias=bias, scale=scale)

        def cdve(opn, out, in0, in1=None, **kw):
            if in1 is None:
                return lambda e: e._custom_dve(_OPS[opn], out=out, in0=in0, **kw)
            return lambda e: e._custom_dve(_OPS[opn], out=out, in0=in0, in1=in1, **kw)

        def mm(out, lhsT, rhs, start, stop):
            return lambda e: e.matmul(out, lhsT, rhs, start=start, stop=stop)

        def tpz(out, in_):
            return lambda e: e.transpose(out, in_, ident[:])

        def red(out, in_, op):
            return lambda e: e.tensor_reduce(out, in_, AX.X, op)

        def cpy(out, in_):
            return lambda e: (e.tensor_copy(out, in_) if hasattr(e, "tensor_copy")
                              else e.copy(out, in_))

        P.op("sync", _ld(bass.AP(x_tm, 0, [[NT * D, 128], [D, NT], [1, D]]),
                         bass.AP(xin, 0, [[D, 128], [128 * D, NT], [1, D]])),
             inc=[("din", 16)])
        for src, dst in ((cst_in, cstb), (b1in, b1_sb), (boutbc_in, boutbc),
                         (b2bc_in, b2bc), (ident_in, ident), (trimask_in, trimask),
                         (bcast64_in, bcast64)):
            P.op("sync", _ld(dst[:], src[:]), inc=[("din", 16)])
        PRO_DIN = P.now("din")

        if debug == "x0":
            for i in range(NT):
                P.op("sync", _ld(bass.AP(xout, i * 128 * D, [[D, 128], [1, D]]),
                                 x_tm[:, i * D:(i + 1) * D]),
                     wait=[("din", PRO_DIN)], inc=[("dout", 16)])
            P.op("sync", None, wait=[("dout", P.now("dout"))])
            return _emit_and_finish(nc, P, sems)

        qlast = {"v": 0}
        lnlast = {"s": 0}

        L16 = [("LAD4", dict(s0=2.0 ** -13, s1=2.0 ** -12, imm2=8.0)),
               ("LAD4B", dict(s0=2.0 ** -11, s1=2.0 ** -8, imm2=4.0)),
               ("LAD4B", dict(s0=2.0 ** -7, s1=2.0 ** -4, imm2=2.0)),
               ("LAD4B", dict(s0=2.0 ** -2, s1=2.0 ** -5, imm2=2.0 ** -1)),
               ("LAD1B", dict(s0=1.0))]
        L8 = [("LAD4", dict(s0=2.0 ** -5, s1=2.0 ** -4, imm2=4.0)),
              ("LAD4B", dict(s0=2.0 ** -2, s1=1.0, imm2=2.0))]
        L8X = L8 + [("LAD4B", dict(s0=8.0, s1=16.0, imm2=32.0))]

        ETA = 2.0 ** -22
        CATCH = 2.0 - ETA

        def walk_quant(x_ap, n, man16, signed, xwaits, out_slices, ext=False,
                       he=False):
            lads = L16 if man16 else (L8X if ext else L8)
            ih, il = ("i16h", "i16l") if man16 else ("i8h", "i8l")
            sh, sl_ = ("s16h", "s16l") if man16 else ("s8h", "s8l")
            guard = [("v", qlast["v"])]
            if signed:
                P.op("scalar", act(wabs[:, 0:n], x_ap, AF.Abs),
                     wait=list(xwaits) + guard)
                P.op("scalar", act(wsgn[:, 0:n], x_ap, AF.Sign))
                sg = P.bump("scalar", "s")
                ax_ap = wabs[:, 0:n]
                lw = [("s", sg)]
            else:
                ax_ap = x_ap
                lw = list(xwaits) + guard
            for i, (opn, kw) in enumerate(lads):
                if i == 0:
                    P.op("vector", cdve(opn, wacc[:, 0:n], ax_ap, **kw), wait=lw)
                else:
                    P.op("vector", cdve(opn, wacc[:, 0:n], ax_ap, wacc[:, 0:n], **kw))
            vw = P.bump("vector", "v")
            P.op("scalar", act(wis[:, 0:n], wacc[:, 0:n], AF.Exp,
                               bias=cc(ih), scale=cc("nhi")), wait=[("v", vw)])
            P.op("scalar", act(wlo[:, 0:n], wacc[:, 0:n], AF.Exp,
                               bias=cc(il), scale=cc("nlo")))
            P.op("scalar", act(wsc[:, 0:n], wacc[:, 0:n], AF.Exp,
                               bias=cc(sh), scale=cc("phi")))
            P.op("scalar", act(wacc[:, 0:n], wacc[:, 0:n], AF.Exp,
                               bias=cc(sl_), scale=cc("plo")))
            sw = P.bump("scalar", "s")
            P.op("gpsimd", tt(wis[:, 0:n], wis[:, 0:n], wlo[:, 0:n], ALU.mult),
                 wait=[("s", sw)])
            P.op("gpsimd", tt(wsc[:, 0:n], wsc[:, 0:n], wacc[:, 0:n], ALU.mult))
            gw = P.bump("gpsimd", "g")
            nh = (n + 511) // 512
            for hh in range(nh):
                lo, hi = hh * 512, min(n, hh * 512 + 512)
                m = hi - lo
                sl = slice(lo, hi)
                w = [("g", gw)] if hh == 0 else ()
                if man16:
                    P.op("vector", cdve("MULPEEL2", scratch[:, 0:m], ax_ap[:, sl],
                                        wis[:, sl], s0=0.5, s1=2048.0, imm2=1024.0),
                         wait=w)
                    P.op("vector", cdve("PEEL2", scratch[:, 512:512 + m], scratch[:, 0:m],
                                        s0=0.0, s1=512.0, imm2=256.0))
                    P.op("vector", cdve("PEEL2", scratch[:, 0:m], scratch[:, 512:512 + m],
                                        s0=0.0, s1=128.0, imm2=64.0))
                    P.op("vector", cdve("PEEL2", scratch[:, 512:512 + m], scratch[:, 0:m],
                                        s0=0.0, s1=32.0, imm2=16.0))
                    P.op("vector", cdve("PEEL2", scratch[:, 0:m], scratch[:, 512:512 + m],
                                        s0=0.0, s1=8.0, imm2=4.0))
                    if he:
                        P.op("vector", cdve("PEEL2C", scratch[:, 512:512 + m],
                                            scratch[:, 0:m], s0=0.0, s1=2.0,
                                            imm2=CATCH))
                        P.op("vector", cdve("PEELWET", scratch[:, 1024:1024 + m],
                                            scratch[:, 512:512 + m], wsc[:, sl],
                                            s0=1.0, s1=0.5, imm2=1.0 + ETA))
                    else:
                        P.op("vector", cdve("PEEL2W", scratch[:, 1024:1024 + m], scratch[:, 0:m],
                                            wsc[:, sl], s0=2.0, s1=0.5, imm2=1.0))
                else:
                    P.op("vector", cdve("MULPEEL2", scratch[:, 0:m], ax_ap[:, sl],
                                        wis[:, sl], s0=0.5, s1=16.0, imm2=8.0), wait=w)
                    P.op("vector", cdve("PEEL2", scratch[:, 512:512 + m], scratch[:, 0:m],
                                        s0=0.0, s1=4.0, imm2=2.0))
                    if he:
                        P.op("vector", cdve("PEEL2C", scratch[:, 0:m],
                                            scratch[:, 512:512 + m], s0=0.0, s1=2.0,
                                            imm2=CATCH))
                        P.op("vector", cdve("PEELWET", scratch[:, 1024:1024 + m],
                                            scratch[:, 0:m], wsc[:, sl],
                                            s0=1.0, s1=0.5, imm2=1.0 + ETA))
                    else:
                        P.op("vector", cdve("PEELW", scratch[:, 1024:1024 + m],
                                            scratch[:, 512:512 + m],
                                            wsc[:, sl], s0=1.0, s1=0.5))
                if signed:
                    P.op("vector", tt(scratch[:, 1536:1536 + m], ax_ap[:, sl],
                                      scratch[:, 1024:1024 + m], ALU.subtract))
                    P.op("vector", tt(out_slices[hh], scratch[:, 1536:1536 + m], wsgn[:, sl],
                                      ALU.mult))
                else:
                    P.op("vector", tt(out_slices[hh], x_ap[:, sl], scratch[:, 1024:1024 + m],
                                      ALU.subtract))
            vend = P.bump("vector", "v")
            qlast["v"] = vend
            return vend

        cbufs = open_scope("chain", [("wabs", [128, D], F32), ("wsgn", [128, D], F32),
                                     ("wacc", [128, D], F32), ("wis", [128, D], F32),
                                     ("wsc", [128, D], F32), ("wlo", [128, D], F32)])
        wabs, wsgn, wacc, wis, wsc, wlo = (cbufs[n] for n in
                                           ("wabs", "wsgn", "wacc", "wis", "wsc", "wlo"))
        wb = open_scope("w2st", [("w2st", [128, 2 * D], F32),
                                 ("w2qs", [128, 2 * D], FP8)])
        w2st, w2qs = wb["w2st"], wb["w2qs"]
        walk_quant(b2bc[:], D, True, True, [("din", PRO_DIN)],
                   [bqbc[:, 0:512], bqbc[:, 512:1024]], he=True)

        for c in range(NHF):
            slot = (c % 2) * D
            w = [("dwr", 16 * (c - 1))] if c >= 2 else []
            P.op("sync", _ld(w2st[:, slot:slot + D],
                             bass.AP(w2t, c * 128 * D, [[D, 128], [1, D]])),
                 wait=w, inc=[("din", 16)])
            din_c = P.now("din")
            vend = walk_quant(w2st[:, slot:slot + D], D, False, True,
                              [("din", din_c)],
                              [w2qs[:, slot:slot + 512],
                               w2qs[:, slot + 512:slot + D]], he=True)
            P.op("gpsimd", _ld(bass.AP(w2q_dram, c * 128 * D, [[D, 128], [1, D]]),
                               w2qs[:, slot:slot + D]),
                 wait=[("v", vend)], inc=[("dwr", 16)])
        W2_DWR = P.now("dwr")
        if debug in ("w2q", "w2qb"):
            cofs = 0 if debug == "w2q" else 16
            for cx in range(16):
                c = cx + cofs
                P.op("sync", _ld(w2qs[:, 0:D],
                                 bass.AP(w2q_dram, c * 128 * D, [[D, 128], [1, D]])),
                     wait=[("dwr", W2_DWR), ("dg", P.now("dg"))], inc=[("din", 16)])
                P.op("gpsimd", _ld(bass.AP(xout, cx * 1024,
                                           [[16384, 128], [1, 1024]]),
                                   w2qs[:, 0:D]),
                     wait=[("din", P.now("din"))], inc=[("dg", 16)])
            P.op("sync", None, wait=[("dg", P.now("dg"))])
            close_all_scopes()
            return _emit_and_finish(nc, P, sems)
        if debug == "bq":
            P.op("sync", _ld(bass.AP(xout, 0, [[D, 128], [1, D]]), bqbc[:]),
                 wait=[("v", P.now("v"))], inc=[("dout", 16)])
            P.op("sync", None, wait=[("dout", P.now("dout"))])
            close_all_scopes()
            return _emit_and_finish(nc, P, sems)
        close_scope("w2st")
        close_scope("chain")

        def walk_ln_stats_apply(src_t, src_ofs, dst_ap, xwaits):
            full = src_t[:, src_ofs:src_ofs + D]
            h0 = src_t[:, src_ofs:src_ofs + 512]
            h1 = src_t[:, src_ofs + 512:src_ofs + 1024]
            P.op("vector", red(stat[:, 112:113], full, ALU.add),
                 wait=list(xwaits) + [("s", lnlast["s"])])
            P.op("vector", tt(sq[:, 0:512], h0, h0, ALU.mult))
            P.op("vector", tt(sq[:, 512:1024], h1, h1, ALU.mult))
            P.op("vector", red(stat[:, 113:114], sq[:], ALU.add))
            vw = P.bump("vector", "v")
            P.op("scalar", act(stat[:, 115:116], stat[:, 112:113], AF.Identity,
                               scale=-1.0 / D), wait=[("v", vw)])
            P.op("scalar", act(stat[:, 116:117], stat[:, 113:114], AF.Identity,
                               scale=1.0 / D, bias=cc("eps")))
            P.op("scalar", act(stat[:, 122:123], stat[:, 112:113], AF.Identity))
            P.op("scalar", act(stat[:, 123:124], stat[:, 112:113], AF.Identity))
            P.op("scalar", act(stat[:, 124:125], stat[:, 112:113], AF.Identity))
            P.op("scalar", act(stat[:, 117:118], stat[:, 112:113], AF.Square,
                               scale=1.0 / D))
            P.op("scalar", act(stat[:, 118:119], stat[:, 117:118], AF.Ln, scale=-1.0,
                               bias=stat[:, 116:117]))
            P.op("scalar", act(stat[:, 119:120], stat[:, 118:119], AF.Exp,
                               scale=cc("neghalf")))
            sw = P.bump("scalar", "s")
            va = None
            if dst_ap is not None:
                P.op("vector", ts(dst_ap, full, stat[:, 115:116], stat[:, 119:120],
                                  ALU.add, ALU.mult), wait=[("s", sw)])
                va = P.bump("vector", "v")
            lnlast["s"] = sw
            return sw, va

        def walk_ln_phase(ln_tm, ps_tr, xwaits_fn):
            cS, cQ, cM3, cMQ, cM2, cL, cR = 0, 16, 32, 48, 64, 80, 96
            for i in range(NT):
                full = x_tm[:, i * D:(i + 1) * D]
                h0 = x_tm[:, i * D:i * D + 512]
                h1 = x_tm[:, i * D + 512:(i + 1) * D]
                w = list(xwaits_fn(i))
                if i == 0:
                    w += [("s", lnlast["s"])]
                P.op("vector", red(stat[:, cS + i:cS + i + 1], full, ALU.add),
                     wait=w)
                P.op("vector", tt(sq[:, 0:512], h0, h0, ALU.mult))
                P.op("vector", tt(sq[:, 512:1024], h1, h1, ALU.mult))
                P.op("vector", red(stat[:, cQ + i:cQ + i + 1], sq[:], ALU.add))
            vw = P.bump("vector", "v")
            for i in range(NT):
                P.op("scalar", act(stat[:, cM3 + i:cM3 + i + 1],
                                   stat[:, cS + i:cS + i + 1],
                                   AF.Identity, scale=-1.0 / D),
                     wait=[("v", vw)] if i == 0 else ())
            for i in range(NT):
                P.op("scalar", act(stat[:, cMQ + i:cMQ + i + 1],
                                   stat[:, cQ + i:cQ + i + 1],
                                   AF.Identity, scale=1.0 / D, bias=cc("eps")))
            for i in range(NT):
                P.op("scalar", act(stat[:, cM2 + i:cM2 + i + 1],
                                   stat[:, cS + i:cS + i + 1],
                                   AF.Square, scale=1.0 / D))
            for i in range(NT):
                P.op("scalar", act(stat[:, cL + i:cL + i + 1],
                                   stat[:, cM2 + i:cM2 + i + 1],
                                   AF.Ln, scale=-1.0,
                                   bias=stat[:, cMQ + i:cMQ + i + 1]))
            for i in range(NT):
                P.op("scalar", act(stat[:, cR + i:cR + i + 1],
                                   stat[:, cL + i:cL + i + 1],
                                   AF.Exp, scale=cc("neghalf")))
            sw = P.bump("scalar", "s")
            lnlast["s"] = sw
            for i in range(NT):
                P.op("vector", ts(ln_tm[:, i * D:(i + 1) * D],
                                  x_tm[:, i * D:(i + 1) * D],
                                  stat[:, cM3 + i:cM3 + i + 1],
                                  stat[:, cR + i:cR + i + 1],
                                  ALU.add, ALU.mult),
                     wait=[("s", sw)] if i == 0 else ())
                va = P.bump("vector", "v")
                for half in range(2):
                    for cq in range(4):
                        c = half * 4 + cq
                        P.op("tensor",
                             tpz(ps_tr[:, cq * 128:(cq + 1) * 128],
                                 ln_tm[:, i * D + c * 128: i * D + (c + 1) * 128]),
                             wait=[("v", va), ("s", P.now("s"))] if cq == 0 else [])
                    tw = P.bump("tensor", "t")
                    for cq in range(4):
                        c = half * 4 + cq
                        P.op("scalar",
                             cpy(ln_fm[:, c * S + i * 128: c * S + (i + 1) * 128],
                                 ps_tr[:, cq * 128:(cq + 1) * 128]),
                             wait=[("t", tw)] if cq == 0 else [])
                    se = P.bump("scalar", "s")
                    P.op("tensor", None, wait=[("s", se)])
            return P.now("s")

        zs = open_scope("zscope", [("zfm", [128, NCH * S], BF16)])
        zfm = zs["zfm"]
        lt = open_scope("lntm", [("ln_tm", [128, NT * D], BF16)])
        ps_tr = scope["lntm"].enter_context(nc.psum_tensor("ps_tr1", [128, 512], BF16))
        LN1_S = walk_ln_phase(lt["ln_tm"], ps_tr, lambda i: [("din", PRO_DIN)])
        close_scope("lntm")
        if debug == "ln1":
            P.op("gpsimd", _ld(bass.AP(xout, 0, [[16384, 128], [1, 16384]]),
                               ln_fm[:]),
                 wait=[("s", LN1_S)], inc=[("dout", 16)])
            P.op("sync", None, wait=[("dout", P.now("dout"))])
            close_all_scopes()
            return _emit_and_finish(nc, P, sems)
        att = open_scope("attn", [("qh", [128, 2 * S], BF16), ("kh", [128, 2 * S], BF16),
                                  ("wstr", [128, 6 * 1024], BF16),
                                  ("vpair", [128, NT * 130], BF16),
                                  ("probsT", [128, 16 * GQ], BF16),
                                  ("zstage", [64, S], BF16),
                                  ("recst", [128, GQ], BF16),
                                  ("recsb", [64, GQ], F32)])
        qh, kh, wstr, vpair, probsT, zstage, recst, recsb = (
            att[n] for n in ("qh", "kh", "wstr", "vpair", "probsT", "zstage",
                             "recst", "recsb"))
        ps_s = scope["attn"].enter_context(nc.psum_tensor("ps_s", [128, 2048], F32))
        T_LN1 = P.now("t")
        P.op("vector", _ms(qh[64:128, 0:S], 0.0),
             wait=[("din", PRO_DIN), ("t", T_LN1)])
        P.op("vector", _ms(qh[0:64, S:2 * S], 0.0))
        P.op("vector", _ms(kh[64:128, 0:S], 0.0))
        P.op("vector", _ms(kh[0:64, S:2 * S], 0.0))
        P.op("vector", _ms(vpair[:], 1.0))
        P.op("vector", _ms(recst[:], 0.0))
        INIT_V = P.bump("vector", "v")

        for p in range(NP):
            base = (p % 2) * 3 * 1024
            for idx, wsrc in enumerate((wqp, wkp, wvp)):
                P.op("gpsimd",
                     _ld(bass.AP(wstr, base + idx * 1024,
                                 [[6 * 1024, 128], [128, NCH], [1, 128]]),
                         bass.AP(wsrc, p * D * 128,
                                 [[128, 128], [128 * 128, NCH], [1, 128]])),
                     wait=[("t", P.now("t"))] if p >= 2 else [("t", T_LN1)],
                     inc=[("dg", 16)])
            DG_P = P.now("dg")

            evs = {}
            for which, wof, dst, scale in (("q", 0, qh, True), ("k", 1024, kh, False)):
                for g4 in range(4):
                    ps = ps_a if g4 % 2 == 0 else ps_b
                    for c in range(NCH):
                        P.op("tensor",
                             mm(ps[:, 0:512],
                                wstr[:, base + wof + c * 128: base + wof + (c + 1) * 128],
                                ln_fm[:, c * S + g4 * 512: c * S + (g4 + 1) * 512],
                                start=(c == 0), stop=(c == NCH - 1)),
                             wait=([("dg", DG_P), ("s", P.now("s"))] if c == 0 else []))
                    tw = P.bump("tensor", "t")
                    col = g4 * 512
                    nm = "eighth" if scale else "one"
                    P.op("scalar", act(dst[0:64, col:col + 512], ps[0:64, 0:512],
                                       AF.Copy, scale=cc(nm, 0, 64)),
                         wait=[("t", tw)])
                    P.op("scalar", act(dst[64:128, S + col:S + col + 512],
                                       ps[64:128, 0:512], AF.Copy, scale=cc(nm, 64, 128)))
                    P.bump("scalar", "s")
            for i in range(NT):
                ps = ps_a if i % 2 == 0 else ps_b
                for c in range(NCH):
                    P.op("tensor",
                         mm(ps[:, 0:128],
                            ln_fm[:, c * S + i * 128: c * S + (i + 1) * 128],
                            wstr[:, base + 2048 + c * 128: base + 2048 + (c + 1) * 128],
                            start=(c == 0), stop=(c == NCH - 1)),
                         wait=[("s", P.now("s"))] if c == 0 else [])
                tw = P.bump("tensor", "t")
                P.op("scalar",
                     act(bass.AP(vpair, i * 130, [[NT * 130, 128], [65, 2], [1, 64]]),
                         bass.AP(ps, 0, [[512, 128], [64, 2], [1, 64]]),
                         AF.Copy),
                     wait=[("t", tw)])
                P.bump("scalar", "s")
            VS = P.now("s")

            for hloc in range(2):
                qcol = hloc * S
                for g in range(NG):
                    njc = 4 * g + 4
                    for jb in range(0, njc, 4):
                        jhi = min(njc, jb + 4)
                        for j in range(jb, jhi):
                            if j == jb:
                                jw = ([("s", VS), ("v", INIT_V)] if jb == 0
                                      else [("s", P.now("s"))])
                            else:
                                jw = []
                            P.op("tensor",
                                 mm(ps_s[:, (j - jb) * 512:(j - jb + 1) * 512],
                                    kh[:, qcol + j * 128: qcol + (j + 1) * 128],
                                    qh[:, qcol + g * GQ: qcol + (g + 1) * GQ],
                                    start=True, stop=True),
                                 wait=jw)
                        tw = P.bump("tensor", "t")
                        P.op("scalar",
                             act(probsT[:, jb * GQ: jhi * GQ],
                                 ps_s[:, 0:(jhi - jb) * 512], AF.Exp,
                                 bias=cc("negu")),
                             wait=[("t", tw)])
                        P.bump("scalar", "s")
                    SEXP = P.now("s")
                    for dj in range(4):
                        j = 4 * g + dj
                        o = 128 * dj
                        if o > 0:
                            P.op("vector", _ms(probsT[:, j * GQ: j * GQ + o], 0.0),
                                 wait=[("s", SEXP)])
                        P.op("vector",
                             tt(probsT[:, j * GQ + o: j * GQ + o + 128],
                                probsT[:, j * GQ + o: j * GQ + o + 128],
                                trimask[:], ALU.mult),
                             wait=[("s", SEXP)])
                    VMASK = P.bump("vector", "v")
                    for j in range(njc):
                        P.op("tensor",
                             mm(ps_z[0:65, 0:GQ],
                                bass.AP(vpair, j * 130 + hloc * 65,
                                        [[NT * 130, 128], [1, 65]]),
                                probsT[:, j * GQ:(j + 1) * GQ],
                                start=(j == 0), stop=(j == njc - 1)),
                             wait=[("v", VMASK), ("s", P.now("s"))] if j == 0 else [])
                    TPV = P.bump("tensor", "t")
                    def _recip(e):
                        with nc.allow_low_precision(reason="recip feeds bf16 mm operand"):
                            return e.reciprocal(recst[0:1, 0:GQ], ps_z[64:65, 0:GQ])
                    P.op("vector", _recip, wait=[("t", TPV)])
                    VREC = P.bump("vector", "v")
                    P.op("tensor", mm(ps_r[0:64, 0:GQ], bcast64[:], recst[:, 0:GQ],
                                      start=True, stop=True),
                         wait=[("v", VREC), ("s", P.now("s"))])
                    TREC = P.bump("tensor", "t")
                    P.op("vector", cpy(recsb[0:64, 0:GQ], ps_r[0:64, 0:GQ]),
                         wait=[("t", TREC)])
                    SREC = P.bump("vector", "v")
                    if hloc == 0:
                        zdst = zfm[0:64, p * S + g * GQ: p * S + (g + 1) * GQ]
                    else:
                        zdst = zstage[0:64, g * GQ:(g + 1) * GQ]
                    zw = [("dwr", W2_DWR)]
                    if hloc == 1 and g == 0:
                        zw.append(("dz", P.now("dz")))
                    P.op("vector", tt(zdst, ps_z[0:64, 0:GQ], recsb[0:64, 0:GQ],
                                      ALU.mult), wait=zw)
                    P.bump("vector", "v")
                if hloc == 1:
                    P.op("sync", _ld(zfm[64:128, p * S:(p + 1) * S], zstage[0:64, 0:S]),
                         wait=[("v", P.now("v"))], inc=[("dz", 16)])
        ATT_DZ = P.now("dz")
        ATT_V = P.now("v")
        close_scope("attn")
        ob = open_scope("outp", [("woutb", [128, NCH * D], BF16),
                                 ("attn_t", [128, D], F32)])
        woutb, attn_t = ob["woutb"], ob["attn_t"]

        P.op("gpsimd", _ld(bass.AP(woutb, 0, [[NCH * D, 128], [D, NCH], [1, D]]),
                           bass.AP(wop, 0, [[D, 128], [128 * D, NCH], [1, D]])),
             wait=[("t", P.now("t")), ("v", ATT_V), ("s", P.now("s"))],
             inc=[("dg", 16)])
        WOUT_DG = P.now("dg")

        for i in range(NT):
            for half in range(2):
                ps = ps_a if half == 0 else ps_b
                for c in range(NCH):
                    P.op("tensor",
                         mm(ps[:, 0:512],
                            zfm[:, c * S + i * 128: c * S + (i + 1) * 128],
                            woutb[:, c * D + half * 512: c * D + half * 512 + 512],
                            start=(c == 0), stop=(c == NCH - 1)),
                         wait=([("dg", WOUT_DG), ("dz", ATT_DZ), ("v", P.now("v")),
                                ("s", P.now("s"))] if c == 0 else []))
                tw = P.bump("tensor", "t")
                P.op("vector",
                     sttf(attn_t[:, half * 512: half * 512 + 512], ps[:, 0:512],
                          1.0, boutbc[:, half * 512: half * 512 + 512],
                          ALU.mult, ALU.add),
                     wait=[("t", tw)])
                P.bump("vector", "v")
            sw, _ = walk_ln_stats_apply(attn_t, 0, None, [])
            P.op("vector", ts(attn_t[:], attn_t[:], stat[:, 115:116], stat[:, 119:120],
                              ALU.add, ALU.mult), wait=[("s", sw)])
            P.op("vector", tt(x_tm[:, i * D:(i + 1) * D],
                              x_tm[:, i * D:(i + 1) * D], attn_t[:], ALU.add))
            P.bump("vector", "v")
        OUTPROJ_V = P.now("v")
        close_scope("outp")
        close_scope("zscope")

        if debug == "x2":
            for i in range(NT):
                P.op("sync", _ld(bass.AP(xout, i * 128 * D, [[D, 128], [1, D]]),
                                 x_tm[:, i * D:(i + 1) * D]),
                     wait=[("v", OUTPROJ_V)], inc=[("dout", 16)])
            P.op("sync", None, wait=[("dout", P.now("dout"))])
            debug = "STOP"

        if debug == "STOP":
            close_all_scopes()
            return _emit_and_finish(nc, P, sems)
        lt2 = open_scope("lntm2", [("ln_tm", [128, NT * D], BF16)])
        ps_tr = scope["lntm2"].enter_context(nc.psum_tensor("ps_tr2", [128, 512], BF16))
        T_OUTP = P.now("t")
        LN2_S = walk_ln_phase(lt2["ln_tm"], ps_tr,
                              lambda i: [("v", OUTPROJ_V), ("t", T_OUTP)])
        close_scope("lntm2")
        if debug == "ln2":
            P.op("gpsimd", _ld(bass.AP(xout, 0, [[16384, 128], [1, 16384]]),
                               ln_fm[:]),
                 wait=[("s", LN2_S)], inc=[("dout", 16)])
            P.op("sync", None, wait=[("dout", P.now("dout"))])
            close_all_scopes()
            return _emit_and_finish(nc, P, sems)
        fbufs = open_scope("ffn", [("hq8", [128, NHF * 1024], FP8),
                                   ("w1str", [128, 2 * 1024], BF16),
                                   ("w2qs", [128, 2 * D], FP8),
                                   ("xr_sb", [128, 512], F32),
                                   ("ffo", [128, D], F32),
                                   ("ffq", [128, D], F16)])
        hq8, w1str, w2qs, xr_sb, ffo, ffq = (
            fbufs[n] for n in ("hq8", "w1str", "w2qs", "xr_sb", "ffo", "ffq"))
        cb2 = open_scope("chain2", [("wabs", [128, D], F32), ("wsgn", [128, D], F32),
                                    ("wacc", [128, D], F32), ("wis", [128, D], F32),
                                    ("wsc", [128, D], F32), ("wlo", [128, D], F32)])
        wabs, wsgn, wacc, wis, wsc, wlo = (cb2[n] for n in
                                           ("wabs", "wsgn", "wacc", "wis", "wsc", "wlo"))

        first_w2_load = [True]
        w2_tc = []
        for sg in range(2):
            col0 = sg * 1024
            for m in range(NHF):
                slot = (m % 2) * 1024
                P.op("gpsimd",
                     _ld(bass.AP(w1str, slot, [[2 * 1024, 128], [128, NCH], [1, 128]]),
                         bass.AP(w1t, m * NCH * 128 * 128,
                                 [[128, 128], [128 * 128, NCH], [1, 128]])),
                     wait=[("t", P.now("t"))] if (sg > 0 or m >= 2)
                     else [("t", P.now("t")), ("s", LN2_S)],
                     inc=[("dg", 16)])
                dgm = P.now("dg")
                for g2 in range(2):
                    ps = ps_a if g2 == 0 else ps_b
                    for c in range(NCH):
                        P.op("tensor",
                             mm(ps[:, 0:512],
                                w1str[:, slot + c * 128: slot + (c + 1) * 128],
                                ln_fm[:, c * S + col0 + g2 * 512:
                                      c * S + col0 + g2 * 512 + 512],
                                start=(c == 0), stop=(c == NCH - 1)),
                             wait=([("dg", dgm), ("s", P.now("s"))] if c == 0 else []))
                    tw = P.bump("tensor", "t")
                    P.op("vector", ts(xr_sb[:], ps[:, 0:512], b1_sb[:, m:m + 1],
                                      0.0, ALU.add, ALU.max),
                         wait=[("t", tw), ("v", qlast["v"])])
                    P.bump("vector", "v")
                    walk_quant(xr_sb[:], 512, False, False, [],
                               [hq8[:, m * 1024 + g2 * 512: m * 1024 + g2 * 512 + 512]],
                               ext=True)
            HQ_V = P.now("v")
            if debug and debug.startswith("ffacc") and sg == 0:
                nacc = int(debug[5:])
                for c in range(nacc):
                    slot = (c % 2) * D
                    P.op("sync", _ld(w2qs[:, slot:slot + D],
                                     bass.AP(w2q_dram, c * 128 * D,
                                             [[D, 128], [1, D]])),
                         wait=[("v", HQ_V), ("dwr", W2_DWR),
                               ("t", P.now("t"))], inc=[("din", 16)])
                    P.op("tensor", mm(ps_a[:, 0:512], hq8[:, c * 1024: c * 1024 + 128],
                                      w2qs[:, slot:slot + 512],
                                      start=(c == 0), stop=(c == nacc - 1)),
                         wait=[("din", P.now("din")), ("v", HQ_V),
                               ("s", P.now("s"))])
                    P.bump("tensor", "t")
                P.op("scalar", cpy(ffo[:, 0:512], ps_a[:, 0:512]),
                     wait=[("t", P.now("t"))])
                s1 = P.bump("scalar", "s")
                P.op("sync", _ld(bass.AP(xout, 0, [[D, 128], [1, 512]]),
                                 ffo[:, 0:512]),
                     wait=[("s", s1)], inc=[("dout", 16)])
                P.op("sync", None, wait=[("dout", P.now("dout"))])
                close_all_scopes()
                return _emit_and_finish(nc, P, sems)
            if debug in ("hqa", "hqb") and sg == 0:
                hofs = 0 if debug == "hqa" else 16384
                P.op("gpsimd", _ld(bass.AP(xout, 0, [[16384, 128], [1, 16384]]),
                                   hq8[:, hofs:hofs + 16384]),
                     wait=[("v", HQ_V)], inc=[("dout", 16)])
                P.op("sync", None, wait=[("dout", P.now("dout"))])
                close_all_scopes()
                return _emit_and_finish(nc, P, sems)
            for stp in range(4):
                sts = (col0 // 128) + stp * 2
                psl = (ps_a, ps_b, ps_z, ps_r)
                for c in range(NHF):
                    slot = (c % 2) * D
                    lw = [("v", HQ_V)]
                    if first_w2_load[0]:
                        lw.append(("dwr", W2_DWR))
                        first_w2_load[0] = False
                    if len(w2_tc) >= 2:
                        lw.append(("t", w2_tc[-2]))
                    P.op("sync", _ld(w2qs[:, slot:slot + D],
                                     bass.AP(w2q_dram, c * 128 * D, [[D, 128], [1, D]])),
                         wait=lw, inc=[("din", 16)])
                    dinc = P.now("din")
                    if debug == "ffo1" and sg == 0 and stp == 0 and c == 0:
                        P.op("gpsimd", _ld(bass.AP(xout, 128 * D, [[D, 128], [1, D]]),
                                           w2qs[:, 0:D]),
                             wait=[("din", dinc)], inc=[("dg", 16)])
                        P.op("gpsimd", _ld(bass.AP(xout, 256 * D, [[D, 128], [1, 128]]),
                                           hq8[:, 0:128]),
                             wait=[("v", HQ_V)], inc=[("dg", 16)])
                        P.op("tensor", mm(ps_a[:, 0:512], hq8[:, 0:128],
                                          w2qs[:, 0:512], start=True, stop=True),
                             wait=[("din", dinc), ("v", HQ_V), ("s", P.now("s"))])
                        t1 = P.bump("tensor", "t")
                        P.op("scalar", cpy(ffo[:, 0:512], ps_a[:, 0:512]),
                             wait=[("t", t1)])
                        s1 = P.bump("scalar", "s")
                        P.op("sync", _ld(bass.AP(xout, 384 * D, [[D, 128], [1, 512]]),
                                         ffo[:, 0:512]),
                             wait=[("s", s1)], inc=[("dout", 16)])
                        P.op("sync", None, wait=[("dout", P.now("dout")),
                                                 ("dg", P.now("dg"))])
                        close_all_scopes()
                        return _emit_and_finish(nc, P, sems)
                    for sti in range(2):
                        for half in range(2):
                            ps = psl[sti * 2 + half]
                            P.op("tensor",
                                 mm(ps[:, 0:512],
                                    hq8[:, c * 1024 + (stp * 2 + sti) * 128:
                                        c * 1024 + (stp * 2 + sti + 1) * 128],
                                    w2qs[:, slot + half * 512: slot + half * 512 + 512],
                                    start=(c == 0), stop=(c == NHF - 1)),
                                 wait=([("din", dinc)]
                                       + ([("v", P.now("v")), ("s", P.now("s"))]
                                          if c == 0 else []))
                                 if (sti == 0 and half == 0) or c == 0 else [])
                    w2_tc.append(P.bump("tensor", "t"))
                tws = w2_tc[-1]
                if debug == "ffo2" and sg == 0 and stp == 0:
                    P.op("scalar", cpy(ffo[:, 0:512], ps_a[:, 0:512]),
                         wait=[("t", tws)])
                    s1 = P.bump("scalar", "s")
                    P.op("sync", _ld(bass.AP(xout, 0, [[D, 128], [1, 512]]),
                                     ffo[:, 0:512]),
                         wait=[("s", s1)], inc=[("dout", 16)])
                    P.op("sync", None, wait=[("dout", P.now("dout"))])
                    close_all_scopes()
                    return _emit_and_finish(nc, P, sems)
                for sti in range(2):
                    i = sts + sti
                    for half in range(2):
                        ps = psl[sti * 2 + half]
                        P.op("vector",
                             sttf(ffo[:, half * 512: half * 512 + 512], ps[:, 0:512],
                                  1.0, bqbc[:, half * 512: half * 512 + 512],
                                  ALU.mult, ALU.add),
                             wait=[("t", tws)])
                    vffo = P.bump("vector", "v")
                    if debug == "ffo0":
                        P.op("sync", _ld(bass.AP(xout, 0, [[D, 128], [1, D]]),
                                         ffo[:]),
                             wait=[("v", vffo)], inc=[("dout", 16)])
                        P.op("sync", None, wait=[("dout", P.now("dout"))])
                        close_all_scopes()
                        return _emit_and_finish(nc, P, sems)
                    qv = walk_quant(ffo[:], D, True, True, [("v", vffo)],
                                    [ffq[:, 0:512], ffq[:, 512:1024]], he=True)
                    if debug == "ffx" and i == 5:
                        P.op("sync", _ld(bass.AP(xout, 0, [[D, 128], [1, D]]),
                                         ffo[:]),
                             wait=[("v", qv)], inc=[("dout", 16)])
                        P.op("gpsimd", _ld(bass.AP(xout, 128 * D, [[D, 128], [1, D]]),
                                           ffq[:]),
                             wait=[("v", qv)], inc=[("dg", 16)])
                        P.op("sync", None, wait=[("dout", P.now("dout")),
                                                 ("dg", P.now("dg"))])
                        close_all_scopes()
                        return _emit_and_finish(nc, P, sems)
                    sw, _ = walk_ln_stats_apply(ffq, 0, None, [])
                    P.op("vector", ts(ffo[:], ffq[:], stat[:, 115:116], stat[:, 119:120],
                                      ALU.add, ALU.mult), wait=[("s", sw)])
                    P.op("vector", tt(x_tm[:, i * D:(i + 1) * D],
                                      x_tm[:, i * D:(i + 1) * D], ffo[:], ALU.add))
                    vres = P.bump("vector", "v")
                    P.op("sync", _ld(bass.AP(xout, i * 128 * D, [[D, 128], [1, D]]),
                                     x_tm[:, i * D:(i + 1) * D]),
                         wait=[("v", vres)], inc=[("dout", 16)])

        P.op("sync", None, wait=[("dout", P.now("dout"))])
        close_scope("chain2")
        close_scope("ffn")

        return _emit_and_finish(nc, P, sems)


def _emit_and_finish(nc, P, sems):
    with nc.Block() as blk:
        def make_body(eng_name):
            def body(eng):
                for waits, emit, incs in P.ops[eng_name]:
                    for sname, val in waits:
                        eng.wait_ge(sems[sname], val)
                    if emit is None:
                        assert not incs
                        continue
                    inst = emit(eng)
                    for sname, amt in incs:
                        inst.then_inc(sems[sname], amt)
            return body
        blk.sync(make_body("sync"))
        blk.tensor(make_body("tensor"))
        blk.vector(make_body("vector"))
        blk.scalar(make_body("scalar"))
        blk.gpsimd(make_body("gpsimd"))
    mybir.codegen_inst_isa_subclasses(nc)
    return nc


def _host_inputs(L0_attn_resid, L0_attn_W_Q, L0_attn_W_K, L0_attn_W_V,
                 L0_attn_W_out, L0_attn_b_out, L0_ffn_w1, L0_ffn_b1,
                 L0_ffn_w2, L0_ffn_b2):
    bf = ml_dtypes.bfloat16
    x = np.ascontiguousarray(np.asarray(L0_attn_resid, dtype=np.float32))
    wq = np.asarray(L0_attn_W_Q, dtype=np.float32).reshape(D, NH * DH)
    wk = np.asarray(L0_attn_W_K, dtype=np.float32).reshape(D, NH * DH)
    wv = np.asarray(L0_attn_W_V, dtype=np.float32).reshape(D, NH * DH)
    wqp = np.ascontiguousarray(wq.reshape(D, NP, 128).transpose(1, 0, 2))
    wkp = np.ascontiguousarray(wk.reshape(D, NP, 128).transpose(1, 0, 2))
    wvp = np.ascontiguousarray(wv.reshape(D, NP, 128).transpose(1, 0, 2))
    wo = np.asarray(L0_attn_W_out, dtype=np.float32).transpose(1, 0, 2).reshape(NH * DH, D)
    wop = np.ascontiguousarray(wo.reshape(NCH, 128, D))
    w1 = np.asarray(L0_ffn_w1, dtype=np.float32)
    w1t = np.ascontiguousarray(w1.reshape(NHF, 128, NCH, 128).transpose(0, 2, 3, 1))
    w2 = np.asarray(L0_ffn_w2, dtype=np.float32)
    w2t = np.ascontiguousarray(w2.T.reshape(NHF, 128, D))
    b1 = np.asarray(L0_ffn_b1, dtype=np.float32)
    b1in = np.ascontiguousarray(b1.reshape(NHF, 128).T)
    boutbc = np.ascontiguousarray(
        np.broadcast_to(np.asarray(L0_attn_b_out, dtype=np.float32), (128, D)))
    b2bc = np.ascontiguousarray(
        np.broadcast_to(np.asarray(L0_ffn_b2, dtype=np.float32), (128, D)))
    cstv = np.zeros((128, NCST), dtype=np.float32)
    for k, v in CST.items():
        cstv[:, CSTI[k]] = v
    identv = np.eye(128, dtype=np.float32).astype(bf)
    tri = np.triu(np.ones((128, 128), dtype=np.float32)).astype(bf)
    bc64 = np.zeros((128, 64), dtype=np.float32)
    bc64[0, :] = 1.0
    bc64 = bc64.astype(bf)
    common = dict(wqp=wqp, wkp=wkp, wvp=wvp, wop=wop, w1t=w1t, w2t=w2t,
                  b1in=b1in, boutbc_in=boutbc, b2bc_in=b2bc, cst_in=cstv,
                  ident_in=identv, trimask_in=tri, bcast64_in=bc64)
    in_maps = []
    for b in range(8):
        m = dict(common)
        m["xin"] = np.ascontiguousarray(x[b])
        in_maps.append(m)
    return in_maps


def kernel(L0_attn_resid, L0_attn_W_Q, L0_attn_W_K, L0_attn_W_V, L0_attn_W_out,
           L0_attn_b_out, L0_ffn_w1, L0_ffn_b1, L0_ffn_w2, L0_ffn_b2):
    in_maps = _host_inputs(L0_attn_resid, L0_attn_W_Q, L0_attn_W_K, L0_attn_W_V,
                           L0_attn_W_out, L0_attn_b_out, L0_ffn_w1, L0_ffn_b1,
                           L0_ffn_w2, L0_ffn_b2)
    nc = build_nc()
    res = run_bass_kernel_spmd(nc, in_maps, core_ids=list(range(8)))
    out = np.stack([np.asarray(r["xout"]) for r in res.results], axis=0)
    return out


# version 05
# speedup vs baseline: 1.0137x; 1.0137x over previous
import contextlib
import numpy as np
import ml_dtypes
import concourse.bass as bass
import concourse.mybir as mybir
from concourse.bass_utils import run_bass_kernel_spmd
from concourse.dve_op import DveOp
from concourse.dve_spec import Spec, Src0, Src1, lower, C0, C1, C2, eq, select
import concourse.dve_ops as dve_ops

F32 = mybir.dt.float32
F16 = mybir.dt.float16
BF16 = mybir.dt.bfloat16
FP8 = mybir.dt.float8e4
AF = mybir.ActivationFunctionType
ALU = mybir.AluOpType
AX = mybir.AxisListType
LN2 = 0.6931471805599453

S = 2048
D = 1024
NT = 16
NCH = 8
NH = 16
NP = 8
DH = 64
HF = 4096
NHF = 32
GQ = 512
NG = S // GQ
USOFT = 12.0
EPS = 1e-5

_OPS = {}


def _fdt(x):
    return np.float64 if isinstance(x, np.ndarray) and x.dtype == np.float64 else np.float32


def _reg(name, spec_body, reference, row):
    spec = Spec(body=spec_body)
    sha = lower(spec, ver="v3", perf_en=False, name=name).sha("v3")
    op = DveOp(name, spec, subdim=False, uops_sha={"v3": sha}, reference=reference)
    if not any(o.name == name for o in dve_ops.OPS):
        dve_ops.OPS.append(op)
        dve_ops._SUB_OPCODE_FOR_NAME[name] = row
        dve_ops.CUSTOM_DVE_SPECS[name] = spec
        dve_ops.CUSTOM_DVE_REFS[name] = op.ref
    return op


def _register_ops():
    if _OPS:
        return
    b4 = ((Src0 >= C0) + (Src0 >= C1)) + ((Src0 >= C2) + (Src0 >= C0 * C2))

    def r4(in0, in1, s0, s1, imm2):
        t = _fdt(in0); x = in0.astype(t)
        return ((x >= s0).astype(t) + (x >= s1).astype(t)
                + (x >= imm2).astype(t) + (x >= s0 * imm2).astype(t))

    def r4b(in0, in1, s0, s1, imm2):
        return r4(in0, in1, s0, s1, imm2) + in1.astype(_fdt(in0))

    def r1b(in0, in1, s0, s1, imm2):
        t = _fdt(in0); x = in0.astype(t)
        return (x >= s0).astype(t) + in1.astype(t)

    t = Src0 * Src1 + C0
    t = t - (t >= C1) * C1
    t = t - (t >= C2) * C2

    def rmp2(in0, in1, s0, s1, imm2):
        tt = _fdt(in0)
        x = in0.astype(tt) * in1.astype(tt) + s0
        x = x - (x >= s1).astype(tt) * s1
        return x - (x >= imm2).astype(tt) * imm2

    p = Src0 + C0
    p = p - (p >= C1) * C1
    p = p - (p >= C2) * C2

    def rp2(in0, in1, s0, s1, imm2):
        tt = _fdt(in0)
        x = in0.astype(tt) + s0
        x = x - (x >= s1).astype(tt) * s1
        return x - (x >= imm2).astype(tt) * imm2

    q = Src0
    q = q - (q >= C0) * C0
    qw = (q - C1) * Src1

    def rpw(in0, in1, s0, s1, imm2):
        tt = _fdt(in0)
        x = in0.astype(tt)
        x = x - (x >= s0).astype(tt) * s0
        return (x - s1) * in1.astype(tt)

    qe = Src0
    qe = qe - (qe >= C0) * C0
    qwe = (qe - C1 + eq(Src0, C0)) * Src1

    def rpwe(in0, in1, s0, s1, imm2):
        tt = _fdt(in0)
        x = in0.astype(tt)
        r = x - (x >= s0).astype(tt) * s0
        return (r - s1 + (x == s0).astype(tt)) * in1.astype(tt)

    b0t = Src0 >= C0
    qwt = (((Src0 - b0t * C0) - C1) + (b0t - (Src0 >= C2))) * Src1

    def rpwt(in0, in1, s0, s1, imm2):
        tt = _fdt(in0)
        x = in0.astype(tt)
        r = x - (x >= s0).astype(tt) * s0
        return (r - s1 + (x >= s0).astype(tt) * (x < imm2).astype(tt)) \
            * in1.astype(tt)

    r_ = Src0
    r_ = r_ - (r_ >= C0) * C0
    r_ = r_ - (r_ >= C2) * C2
    rw = (r_ - C1) * Src1

    def rp2w(in0, in1, s0, s1, imm2):
        tt = _fdt(in0)
        x = in0.astype(tt)
        x = x - (x >= s0).astype(tt) * s0
        x = x - (x >= imm2).astype(tt) * imm2
        return (x - s1) * in1.astype(tt)

    _OPS["LAD4"] = _reg("QLAD4", b4, r4, 30)
    _OPS["LAD4B"] = _reg("QLAD4B", b4 + Src1, r4b, 31)
    _OPS["LAD1B"] = _reg("QLAD1B", (Src0 >= C0) + Src1, r1b, 1)
    _OPS["MULPEEL2"] = _reg("QMULPEEL2", t, rmp2, 2)
    _OPS["PEEL2"] = _reg("QPEEL2", p, rp2, 3)
    _OPS["PEELW"] = _reg("QPEELW", qw, rpw, 4)
    _OPS["PEEL2W"] = _reg("QPEEL2W", rw, rp2w, 5)
    _OPS["PEELWE"] = _reg("QPEELWE", qwe, rpwe, 6)

    hv1 = select(Src0 >= C0, Src1 * C2, Src1)
    hv2 = select(Src0 >= C1, hv1 * C2, hv1)

    def rhalve2(in0, in1, s0, s1, imm2):
        tt = _fdt(in0)
        x = in0.astype(tt)
        t = np.where(x >= s0, in1.astype(tt) * imm2, in1.astype(tt))
        return np.where(x >= s1, t * imm2, t)

    sp = Src0 * C0 + C1
    sp = sp - (sp >= C2) * C2

    def rscpeel(in0, in1, s0, s1, imm2):
        tt = _fdt(in0)
        u = in0.astype(tt) * s0 + s1
        return u - (u >= imm2).astype(tt) * imm2

    _OPS["HALVE2"] = _reg("QHALVE2", hv2, rhalve2, 7)
    _OPS["SCPEEL"] = _reg("QSCPEEL", sp, rscpeel, 8)
    _OPS["PEELWET"] = _reg("QPEELWET", qwt, rpwt, 9)

    pc = Src0 + C0
    pc = pc - (pc >= C1) * C1
    pc = pc - (pc >= C2) * C1

    def rp2c(in0, in1, s0, s1, imm2):
        tt = _fdt(in0)
        x = in0.astype(tt) + s0
        x = x - (x >= s1).astype(tt) * s1
        return x - (x >= imm2).astype(tt) * s1

    _OPS["PEEL2C"] = _reg("QPEEL2C", pc, rp2c, 10)


_register_ops()

_HI = float(np.float32(LN2))
_LO = float(np.float32(LN2 - _HI))
_B1_8 = float(np.float32(9 * _HI))
_B2_8 = float(np.float32(9 * LN2 - _B1_8))
_B1_16 = float(np.float32(24 * _HI))
_B2_16 = float(np.float32(24 * LN2 - _B1_16))
CST = {
    "is8": 9 * LN2, "sc8": -9 * LN2, "is16": 24 * LN2, "sc16": -24 * LN2,
    "pln2": LN2, "nln2": -LN2, "eighth": 0.125, "negu": -USOFT, "eps": EPS,
    "neghalf": -0.5, "one": 1.0,
    "nhi": -_HI, "phi": _HI, "nlo": -_LO, "plo": _LO,
    "i8h": _B1_8, "i8l": _B2_8, "s8h": -_B1_8, "s8l": -_B2_8,
    "i16h": _B1_16, "i16l": _B2_16, "s16h": -_B1_16, "s16l": -_B2_16,
}
CSTI = {k: i for i, k in enumerate(CST)}
NCST = len(CST)

ENGINES = ("sync", "tensor", "vector", "scalar", "gpsimd")
SEMS = ("din", "dg", "dz", "dwr", "dout", "t", "v", "s", "g")


class Prog:

    def __init__(self):
        self.counts = {sn: 0 for sn in SEMS}
        self.ops = {e: [] for e in ENGINES}
        self.known = {e: {} for e in ENGINES}

    def now(self, name):
        return self.counts[name]

    def op(self, eng, emit, wait=(), inc=()):
        w = []
        for sname, val in wait:
            if val <= 0:
                continue
            if self.known[eng].get(sname, 0) >= val:
                continue
            self.known[eng][sname] = val
            w.append((sname, val))
        incs = []
        for sname, amt in inc:
            self.counts[sname] += amt
            incs.append((sname, amt))
            self.known[eng][sname] = max(self.known[eng].get(sname, 0),
                                         self.counts[sname])
        self.ops[eng].append([w, emit, incs])

    def bump(self, eng, sname, amt=1):
        self.counts[sname] += amt
        self.ops[eng][-1][2].append((sname, amt))
        self.known[eng][sname] = max(self.known[eng].get(sname, 0),
                                     self.counts[sname])
        return self.counts[sname]


def build_nc(debug=None):
    nc = bass.Bass("TRN2", target_bir_lowering=False, detect_race_conditions=False)

    xin = nc.dram_tensor("xin", [S, D], F32, kind="ExternalInput")
    wqp = nc.dram_tensor("wqp", [NP, D, 128], F32, kind="ExternalInput")
    wkp = nc.dram_tensor("wkp", [NP, D, 128], F32, kind="ExternalInput")
    wvp = nc.dram_tensor("wvp", [NP, D, 128], F32, kind="ExternalInput")
    wop = nc.dram_tensor("wop", [NCH, 128, D], F32, kind="ExternalInput")
    w1t = nc.dram_tensor("w1t", [NHF, NCH, 128, 128], F32, kind="ExternalInput")
    w2t = nc.dram_tensor("w2t", [NHF, 128, D], F32, kind="ExternalInput")
    b1in = nc.dram_tensor("b1in", [128, NHF], F32, kind="ExternalInput")
    boutbc_in = nc.dram_tensor("boutbc_in", [128, D], F32, kind="ExternalInput")
    b2bc_in = nc.dram_tensor("b2bc_in", [128, D], F32, kind="ExternalInput")
    cst_in = nc.dram_tensor("cst_in", [128, NCST], F32, kind="ExternalInput")
    ident_in = nc.dram_tensor("ident_in", [128, 128], BF16, kind="ExternalInput")
    trimask_in = nc.dram_tensor("trimask_in", [128, 128], BF16, kind="ExternalInput")
    bcast64_in = nc.dram_tensor("bcast64_in", [128, 64], BF16, kind="ExternalInput")
    xout = nc.dram_tensor("xout", [S, D], F32, kind="ExternalOutput")
    w2q_dram = nc.dram_tensor("w2q_dram", [NHF, 128, D], FP8, kind="Internal")

    P = Prog()

    with contextlib.ExitStack() as est:
        E = est.enter_context
        sems = {sn: E(nc.semaphore(sn)) for sn in SEMS}

        x_tm = E(nc.sbuf_tensor("x_tm", [128, NT * D], F32))
        ln_fm = E(nc.sbuf_tensor("ln_fm", [128, NCH * S], BF16))
        cstb = E(nc.sbuf_tensor("cstb", [128, NCST], F32))
        ident = E(nc.sbuf_tensor("ident", [128, 128], BF16))
        trimask = E(nc.sbuf_tensor("trimask", [128, 128], BF16))
        bcast64 = E(nc.sbuf_tensor("bcast64", [128, 64], BF16))
        b1_sb = E(nc.sbuf_tensor("b1_sb", [128, NHF], F32))
        boutbc = E(nc.sbuf_tensor("boutbc", [128, D], F32))
        b2bc = E(nc.sbuf_tensor("b2bc", [128, D], F32))
        bqbc = E(nc.sbuf_tensor("bqbc", [128, D], F32))
        stat = E(nc.sbuf_tensor("stat", [128, 128], F32))
        scratch = E(nc.sbuf_tensor("scratch", [128, 2048], F32))
        sq = scratch[:, 0:1024]

        scope = {}

        def open_scope(name, specs):
            stk = contextlib.ExitStack()
            scope[name] = stk
            return {nm: stk.enter_context(nc.sbuf_tensor(f"{name}_{nm}", sh, dt))
                    for nm, sh, dt in specs}

        def close_scope(name):
            scope.pop(name).close()

        def close_all_scopes():
            for name in reversed(list(scope)):
                scope.pop(name).close()

        ps_a = E(nc.psum_tensor("ps_a", [128, 512], F32))
        ps_b = E(nc.psum_tensor("ps_b", [128, 512], F32))
        ps_z = E(nc.psum_tensor("ps_z", [128, 512], F32))
        ps_r = E(nc.psum_tensor("ps_r", [128, 512], F32))

        def cc(name, lo=0, hi=128):
            return cstb[lo:hi, CSTI[name]:CSTI[name] + 1]

        def _ld(dst_ap, src_ap):
            return lambda e, d=dst_ap, s=src_ap: e.dma_start(d, s)

        def _ms(ap, val):
            return lambda e, a=ap, vv=val: e.memset(a, vv)

        def ts(out, in0, s1, s2, op0, op1=None):
            if op1 is None:
                return lambda e: e.tensor_scalar(out, in0, s1, None, op0)
            return lambda e: e.tensor_scalar(out, in0, s1, s2, op0, op1)

        def tt(out, in0, in1, op):
            return lambda e: e.tensor_tensor(out, in0, in1, op)

        def sttf(out, in0, sc, in1, op0, op1, accum=None):
            return lambda e: e.scalar_tensor_tensor(out, in0, sc, in1, op0, op1,
                                                    accum_out=accum)

        def act(out, in_, func, bias=0.0, scale=1.0):
            return lambda e: e.activation(out, in_, func, bias=bias, scale=scale)

        def cdve(opn, out, in0, in1=None, **kw):
            if in1 is None:
                return lambda e: e._custom_dve(_OPS[opn], out=out, in0=in0, **kw)
            return lambda e: e._custom_dve(_OPS[opn], out=out, in0=in0, in1=in1, **kw)

        def mm(out, lhsT, rhs, start, stop):
            return lambda e: e.matmul(out, lhsT, rhs, start=start, stop=stop)

        def tpz(out, in_):
            return lambda e: e.transpose(out, in_, ident[:])

        def red(out, in_, op):
            return lambda e: e.tensor_reduce(out, in_, AX.X, op)

        def cpy(out, in_):
            return lambda e: (e.tensor_copy(out, in_) if hasattr(e, "tensor_copy")
                              else e.copy(out, in_))

        P.op("sync", _ld(bass.AP(x_tm, 0, [[NT * D, 128], [D, NT], [1, D]]),
                         bass.AP(xin, 0, [[D, 128], [128 * D, NT], [1, D]])),
             inc=[("din", 16)])
        for src, dst in ((cst_in, cstb), (b1in, b1_sb), (boutbc_in, boutbc),
                         (b2bc_in, b2bc), (ident_in, ident), (trimask_in, trimask),
                         (bcast64_in, bcast64)):
            P.op("sync", _ld(dst[:], src[:]), inc=[("din", 16)])
        PRO_DIN = P.now("din")

        if debug == "x0":
            for i in range(NT):
                P.op("sync", _ld(bass.AP(xout, i * 128 * D, [[D, 128], [1, D]]),
                                 x_tm[:, i * D:(i + 1) * D]),
                     wait=[("din", PRO_DIN)], inc=[("dout", 16)])
            P.op("sync", None, wait=[("dout", P.now("dout"))])
            return _emit_and_finish(nc, P, sems)

        qlast = {"v": 0}
        lnlast = {"s": 0}

        L16 = [("LAD4", dict(s0=2.0 ** -13, s1=2.0 ** -12, imm2=8.0)),
               ("LAD4B", dict(s0=2.0 ** -11, s1=2.0 ** -8, imm2=4.0)),
               ("LAD4B", dict(s0=2.0 ** -7, s1=2.0 ** -4, imm2=2.0)),
               ("LAD4B", dict(s0=2.0 ** -2, s1=2.0 ** -5, imm2=2.0 ** -1)),
               ("LAD1B", dict(s0=1.0))]
        L8 = [("LAD4", dict(s0=2.0 ** -5, s1=2.0 ** -4, imm2=4.0)),
              ("LAD4B", dict(s0=2.0 ** -2, s1=1.0, imm2=2.0))]
        L8X = L8 + [("LAD4B", dict(s0=8.0, s1=16.0, imm2=32.0))]

        ETA = 2.0 ** -22
        CATCH = 2.0 - ETA

        qlastb = {}

        def walk_quant2(bufs, x_ap, n, man16, signed, xwaits, out_slices,
                        ext=False, he=False):
            wabs_, wsgn_, wacc_, wis_, wsc_, wlo_, key = bufs
            lads = L16 if man16 else (L8X if ext else L8)
            ih, il = ("i16h", "i16l") if man16 else ("i8h", "i8l")
            sh, sl_ = ("s16h", "s16l") if man16 else ("s8h", "s8l")
            guard = [("v", qlastb.get(key, 0))]
            if signed:
                P.op("scalar", act(wabs_, x_ap, AF.Abs),
                     wait=list(xwaits) + guard)
                P.op("scalar", act(wsgn_, x_ap, AF.Sign))
                sg = P.bump("scalar", "s")
                ax_ap = wabs_
                lw = [("s", sg)]
            else:
                ax_ap = x_ap
                lw = list(xwaits) + guard
            for i, (opn, kw) in enumerate(lads):
                if i == 0:
                    P.op("vector", cdve(opn, wacc_, ax_ap, **kw), wait=lw)
                else:
                    P.op("vector", cdve(opn, wacc_, ax_ap, wacc_, **kw))
            vw = P.bump("vector", "v")
            P.op("scalar", act(wis_, wacc_, AF.Exp,
                               bias=cc(ih), scale=cc("nhi")), wait=[("v", vw)])
            P.op("scalar", act(wlo_, wacc_, AF.Exp,
                               bias=cc(il), scale=cc("nlo")))
            P.op("scalar", act(wsc_, wacc_, AF.Exp,
                               bias=cc(sh), scale=cc("phi")))
            P.op("scalar", act(wacc_, wacc_, AF.Exp,
                               bias=cc(sl_), scale=cc("plo")))
            sw = P.bump("scalar", "s")
            P.op("gpsimd", tt(wis_, wis_, wlo_, ALU.mult), wait=[("s", sw)])
            P.op("gpsimd", tt(wsc_, wsc_, wacc_, ALU.mult))
            gw = P.bump("gpsimd", "g")

            def finish():
                nh = (n + 511) // 512
                for hh in range(nh):
                    lo, hi = hh * 512, min(n, hh * 512 + 512)
                    m = hi - lo
                    sl = slice(lo, hi)
                    w = [("g", gw)] if hh == 0 else ()
                    if man16:
                        P.op("vector", cdve("MULPEEL2", scratch[:, 0:m], ax_ap[:, sl],
                                            wis_[:, sl], s0=0.5, s1=2048.0, imm2=1024.0),
                             wait=w)
                        P.op("vector", cdve("PEEL2", scratch[:, 512:512 + m], scratch[:, 0:m],
                                            s0=0.0, s1=512.0, imm2=256.0))
                        P.op("vector", cdve("PEEL2", scratch[:, 0:m], scratch[:, 512:512 + m],
                                            s0=0.0, s1=128.0, imm2=64.0))
                        P.op("vector", cdve("PEEL2", scratch[:, 512:512 + m], scratch[:, 0:m],
                                            s0=0.0, s1=32.0, imm2=16.0))
                        P.op("vector", cdve("PEEL2", scratch[:, 0:m], scratch[:, 512:512 + m],
                                            s0=0.0, s1=8.0, imm2=4.0))
                        if he:
                            P.op("vector", cdve("PEEL2C", scratch[:, 512:512 + m],
                                                scratch[:, 0:m], s0=0.0, s1=2.0,
                                                imm2=CATCH))
                            P.op("vector", cdve("PEELWET", scratch[:, 1024:1024 + m],
                                                scratch[:, 512:512 + m], wsc_[:, sl],
                                                s0=1.0, s1=0.5, imm2=1.0 + ETA))
                        else:
                            P.op("vector", cdve("PEEL2W", scratch[:, 1024:1024 + m],
                                                scratch[:, 0:m],
                                                wsc_[:, sl], s0=2.0, s1=0.5, imm2=1.0))
                    else:
                        P.op("vector", cdve("MULPEEL2", scratch[:, 0:m], ax_ap[:, sl],
                                            wis_[:, sl], s0=0.5, s1=16.0, imm2=8.0),
                             wait=w)
                        P.op("vector", cdve("PEEL2", scratch[:, 512:512 + m], scratch[:, 0:m],
                                            s0=0.0, s1=4.0, imm2=2.0))
                        if he:
                            P.op("vector", cdve("PEEL2C", scratch[:, 0:m],
                                                scratch[:, 512:512 + m], s0=0.0, s1=2.0,
                                                imm2=CATCH))
                            P.op("vector", cdve("PEELWET", scratch[:, 1024:1024 + m],
                                                scratch[:, 0:m], wsc_[:, sl],
                                                s0=1.0, s1=0.5, imm2=1.0 + ETA))
                        else:
                            P.op("vector", cdve("PEELW", scratch[:, 1024:1024 + m],
                                                scratch[:, 512:512 + m],
                                                wsc_[:, sl], s0=1.0, s1=0.5))
                    if signed:
                        P.op("vector", tt(scratch[:, 1536:1536 + m], ax_ap[:, sl],
                                          scratch[:, 1024:1024 + m], ALU.subtract))
                        P.op("vector", tt(out_slices[hh], scratch[:, 1536:1536 + m],
                                          wsgn_[:, sl], ALU.mult))
                    else:
                        P.op("vector", tt(out_slices[hh], x_ap[:, sl],
                                          scratch[:, 1024:1024 + m], ALU.subtract))
                vend = P.bump("vector", "v")
                qlastb[key] = vend
                qlast["v"] = vend
                return vend
            return finish

        def walk_quant(x_ap, n, man16, signed, xwaits, out_slices, ext=False,
                       he=False):
            bufs = (wabs[:, 0:n], wsgn[:, 0:n], wacc[:, 0:n], wis[:, 0:n],
                    wsc[:, 0:n], wlo[:, 0:n], "serial")
            qlastb["serial"] = qlast["v"]
            return walk_quant2(bufs, x_ap, n, man16, signed, xwaits,
                               out_slices, ext=ext, he=he)()

        cbufs = open_scope("chain", [("wabs", [128, D], F32), ("wsgn", [128, D], F32),
                                     ("wacc", [128, D], F32), ("wis", [128, D], F32),
                                     ("wsc", [128, D], F32), ("wlo", [128, D], F32),
                                     ("wabsB", [128, D], F32), ("wsgnB", [128, D], F32),
                                     ("waccB", [128, D], F32), ("wisB", [128, D], F32),
                                     ("wscB", [128, D], F32), ("wloB", [128, D], F32)])
        wabs, wsgn, wacc, wis, wsc, wlo = (cbufs[n] for n in
                                           ("wabs", "wsgn", "wacc", "wis", "wsc", "wlo"))
        wsetA = (wabs[:], wsgn[:], wacc[:], wis[:], wsc[:], wlo[:], "w2A")
        wsetB = (cbufs["wabsB"][:], cbufs["wsgnB"][:], cbufs["waccB"][:],
                 cbufs["wisB"][:], cbufs["wscB"][:], cbufs["wloB"][:], "w2B")
        wb = open_scope("w2st", [("w2st", [128, 2 * D], F32),
                                 ("w2qs", [128, 2 * D], FP8)])
        w2st, w2qs = wb["w2st"], wb["w2qs"]
        walk_quant(b2bc[:], D, True, True, [("din", PRO_DIN)],
                   [bqbc[:, 0:512], bqbc[:, 512:1024]], he=True)
        qlastb["w2A"] = qlast["v"]

        wpend = [None]
        for c in range(NHF):
            slot = (c % 2) * D
            w = [("dwr", 16 * (c - 1))] if c >= 2 else []
            P.op("sync", _ld(w2st[:, slot:slot + D],
                             bass.AP(w2t, c * 128 * D, [[D, 128], [1, D]])),
                 wait=w, inc=[("din", 16)])
            din_c = P.now("din")
            fin = walk_quant2(wsetA if c % 2 == 0 else wsetB,
                              w2st[:, slot:slot + D], D, False, True,
                              [("din", din_c)],
                              [w2qs[:, slot:slot + 512],
                               w2qs[:, slot + 512:slot + D]], he=True)

            def wfin(f=fin, c=c, slot=slot):
                vend = f()
                P.op("gpsimd", _ld(bass.AP(w2q_dram, c * 128 * D,
                                           [[D, 128], [1, D]]),
                                   w2qs[:, slot:slot + D]),
                     wait=[("v", vend)], inc=[("dwr", 16)])
            if wpend[0] is not None:
                wpend[0]()
            wpend[0] = wfin
        wpend[0]()
        W2_DWR = P.now("dwr")
        if debug in ("w2q", "w2qb"):
            cofs = 0 if debug == "w2q" else 16
            for cx in range(16):
                c = cx + cofs
                P.op("sync", _ld(w2qs[:, 0:D],
                                 bass.AP(w2q_dram, c * 128 * D, [[D, 128], [1, D]])),
                     wait=[("dwr", W2_DWR), ("dg", P.now("dg"))], inc=[("din", 16)])
                P.op("gpsimd", _ld(bass.AP(xout, cx * 1024,
                                           [[16384, 128], [1, 1024]]),
                                   w2qs[:, 0:D]),
                     wait=[("din", P.now("din"))], inc=[("dg", 16)])
            P.op("sync", None, wait=[("dg", P.now("dg"))])
            close_all_scopes()
            return _emit_and_finish(nc, P, sems)
        if debug == "bq":
            P.op("sync", _ld(bass.AP(xout, 0, [[D, 128], [1, D]]), bqbc[:]),
                 wait=[("v", P.now("v"))], inc=[("dout", 16)])
            P.op("sync", None, wait=[("dout", P.now("dout"))])
            close_all_scopes()
            return _emit_and_finish(nc, P, sems)
        close_scope("w2st")
        close_scope("chain")

        def walk_ln_stats_apply(src_t, src_ofs, dst_ap, xwaits):
            full = src_t[:, src_ofs:src_ofs + D]
            h0 = src_t[:, src_ofs:src_ofs + 512]
            h1 = src_t[:, src_ofs + 512:src_ofs + 1024]
            P.op("vector", red(stat[:, 112:113], full, ALU.add),
                 wait=list(xwaits) + [("s", lnlast["s"])])
            P.op("vector", tt(sq[:, 0:512], h0, h0, ALU.mult))
            P.op("vector", tt(sq[:, 512:1024], h1, h1, ALU.mult))
            P.op("vector", red(stat[:, 113:114], sq[:], ALU.add))
            vw = P.bump("vector", "v")
            P.op("scalar", act(stat[:, 115:116], stat[:, 112:113], AF.Identity,
                               scale=-1.0 / D), wait=[("v", vw)])
            P.op("scalar", act(stat[:, 116:117], stat[:, 113:114], AF.Identity,
                               scale=1.0 / D, bias=cc("eps")))
            P.op("scalar", act(stat[:, 122:123], stat[:, 112:113], AF.Identity))
            P.op("scalar", act(stat[:, 123:124], stat[:, 112:113], AF.Identity))
            P.op("scalar", act(stat[:, 124:125], stat[:, 112:113], AF.Identity))
            P.op("scalar", act(stat[:, 117:118], stat[:, 112:113], AF.Square,
                               scale=1.0 / D))
            P.op("scalar", act(stat[:, 118:119], stat[:, 117:118], AF.Ln, scale=-1.0,
                               bias=stat[:, 116:117]))
            P.op("scalar", act(stat[:, 119:120], stat[:, 118:119], AF.Exp,
                               scale=cc("neghalf")))
            sw = P.bump("scalar", "s")
            va = None
            if dst_ap is not None:
                P.op("vector", ts(dst_ap, full, stat[:, 115:116], stat[:, 119:120],
                                  ALU.add, ALU.mult), wait=[("s", sw)])
                va = P.bump("vector", "v")
            lnlast["s"] = sw
            return sw, va

        def walk_ln_phase(ln_tm, ps_tr, xwaits_fn):
            cS, cQ, cM3, cMQ, cM2, cL, cR = 0, 16, 32, 48, 64, 80, 96
            for i in range(NT):
                full = x_tm[:, i * D:(i + 1) * D]
                h0 = x_tm[:, i * D:i * D + 512]
                h1 = x_tm[:, i * D + 512:(i + 1) * D]
                w = list(xwaits_fn(i))
                if i == 0:
                    w += [("s", lnlast["s"])]
                P.op("vector", red(stat[:, cS + i:cS + i + 1], full, ALU.add),
                     wait=w)
                P.op("vector", tt(sq[:, 0:512], h0, h0, ALU.mult))
                P.op("vector", tt(sq[:, 512:1024], h1, h1, ALU.mult))
                P.op("vector", red(stat[:, cQ + i:cQ + i + 1], sq[:], ALU.add))
            vw = P.bump("vector", "v")
            for i in range(NT):
                P.op("scalar", act(stat[:, cM3 + i:cM3 + i + 1],
                                   stat[:, cS + i:cS + i + 1],
                                   AF.Identity, scale=-1.0 / D),
                     wait=[("v", vw)] if i == 0 else ())
            for i in range(NT):
                P.op("scalar", act(stat[:, cMQ + i:cMQ + i + 1],
                                   stat[:, cQ + i:cQ + i + 1],
                                   AF.Identity, scale=1.0 / D, bias=cc("eps")))
            for i in range(NT):
                P.op("scalar", act(stat[:, cM2 + i:cM2 + i + 1],
                                   stat[:, cS + i:cS + i + 1],
                                   AF.Square, scale=1.0 / D))
            for i in range(NT):
                P.op("scalar", act(stat[:, cL + i:cL + i + 1],
                                   stat[:, cM2 + i:cM2 + i + 1],
                                   AF.Ln, scale=-1.0,
                                   bias=stat[:, cMQ + i:cMQ + i + 1]))
            for i in range(NT):
                P.op("scalar", act(stat[:, cR + i:cR + i + 1],
                                   stat[:, cL + i:cL + i + 1],
                                   AF.Exp, scale=cc("neghalf")))
            sw = P.bump("scalar", "s")
            lnlast["s"] = sw
            for i in range(NT):
                P.op("vector", ts(ln_tm[:, i * D:(i + 1) * D],
                                  x_tm[:, i * D:(i + 1) * D],
                                  stat[:, cM3 + i:cM3 + i + 1],
                                  stat[:, cR + i:cR + i + 1],
                                  ALU.add, ALU.mult),
                     wait=[("s", sw)] if i == 0 else ())
                va = P.bump("vector", "v")
                for half in range(2):
                    for cq in range(4):
                        c = half * 4 + cq
                        P.op("tensor",
                             tpz(ps_tr[:, cq * 128:(cq + 1) * 128],
                                 ln_tm[:, i * D + c * 128: i * D + (c + 1) * 128]),
                             wait=[("v", va), ("s", P.now("s"))] if cq == 0 else [])
                    tw = P.bump("tensor", "t")
                    for cq in range(4):
                        c = half * 4 + cq
                        P.op("scalar",
                             cpy(ln_fm[:, c * S + i * 128: c * S + (i + 1) * 128],
                                 ps_tr[:, cq * 128:(cq + 1) * 128]),
                             wait=[("t", tw)] if cq == 0 else [])
                    se = P.bump("scalar", "s")
                    P.op("tensor", None, wait=[("s", se)])
            return P.now("s")

        zs = open_scope("zscope", [("zfm", [128, NCH * S], BF16)])
        zfm = zs["zfm"]
        lt = open_scope("lntm", [("ln_tm", [128, NT * D], BF16)])
        ps_tr = scope["lntm"].enter_context(nc.psum_tensor("ps_tr1", [128, 512], BF16))
        LN1_S = walk_ln_phase(lt["ln_tm"], ps_tr, lambda i: [("din", PRO_DIN)])
        close_scope("lntm")
        if debug == "ln1":
            P.op("gpsimd", _ld(bass.AP(xout, 0, [[16384, 128], [1, 16384]]),
                               ln_fm[:]),
                 wait=[("s", LN1_S)], inc=[("dout", 16)])
            P.op("sync", None, wait=[("dout", P.now("dout"))])
            close_all_scopes()
            return _emit_and_finish(nc, P, sems)
        att = open_scope("attn", [("qh", [128, 2 * S], BF16), ("kh", [128, 2 * S], BF16),
                                  ("wstr", [128, 6 * 1024], BF16),
                                  ("vpair", [128, NT * 130], BF16),
                                  ("probsT", [128, 16 * GQ], BF16),
                                  ("zstage", [64, S], BF16),
                                  ("recst", [128, GQ], BF16),
                                  ("recsb", [64, GQ], F32)])
        qh, kh, wstr, vpair, probsT, zstage, recst, recsb = (
            att[n] for n in ("qh", "kh", "wstr", "vpair", "probsT", "zstage",
                             "recst", "recsb"))
        ps_s = scope["attn"].enter_context(nc.psum_tensor("ps_s", [128, 2048], F32))
        T_LN1 = P.now("t")
        P.op("vector", _ms(qh[64:128, 0:S], 0.0),
             wait=[("din", PRO_DIN), ("t", T_LN1)])
        P.op("vector", _ms(qh[0:64, S:2 * S], 0.0))
        P.op("vector", _ms(kh[64:128, 0:S], 0.0))
        P.op("vector", _ms(kh[0:64, S:2 * S], 0.0))
        P.op("vector", _ms(vpair[:], 1.0))
        P.op("vector", _ms(recst[:], 0.0))
        INIT_V = P.bump("vector", "v")

        for p in range(NP):
            base = (p % 2) * 3 * 1024
            for idx, wsrc in enumerate((wqp, wkp, wvp)):
                P.op("gpsimd",
                     _ld(bass.AP(wstr, base + idx * 1024,
                                 [[6 * 1024, 128], [128, NCH], [1, 128]]),
                         bass.AP(wsrc, p * D * 128,
                                 [[128, 128], [128 * 128, NCH], [1, 128]])),
                     wait=[("t", P.now("t"))] if p >= 2 else [("t", T_LN1)],
                     inc=[("dg", 16)])
            DG_P = P.now("dg")

            evs = {}
            for which, wof, dst, scale in (("q", 0, qh, True), ("k", 1024, kh, False)):
                for g4 in range(4):
                    ps = ps_a if g4 % 2 == 0 else ps_b
                    for c in range(NCH):
                        P.op("tensor",
                             mm(ps[:, 0:512],
                                wstr[:, base + wof + c * 128: base + wof + (c + 1) * 128],
                                ln_fm[:, c * S + g4 * 512: c * S + (g4 + 1) * 512],
                                start=(c == 0), stop=(c == NCH - 1)),
                             wait=([("dg", DG_P), ("s", P.now("s"))] if c == 0 else []))
                    tw = P.bump("tensor", "t")
                    col = g4 * 512
                    nm = "eighth" if scale else "one"
                    P.op("scalar", act(dst[0:64, col:col + 512], ps[0:64, 0:512],
                                       AF.Copy, scale=cc(nm, 0, 64)),
                         wait=[("t", tw)])
                    P.op("scalar", act(dst[64:128, S + col:S + col + 512],
                                       ps[64:128, 0:512], AF.Copy, scale=cc(nm, 64, 128)))
                    P.bump("scalar", "s")
            for i in range(NT):
                ps = ps_a if i % 2 == 0 else ps_b
                for c in range(NCH):
                    P.op("tensor",
                         mm(ps[:, 0:128],
                            ln_fm[:, c * S + i * 128: c * S + (i + 1) * 128],
                            wstr[:, base + 2048 + c * 128: base + 2048 + (c + 1) * 128],
                            start=(c == 0), stop=(c == NCH - 1)),
                         wait=[("s", P.now("s"))] if c == 0 else [])
                tw = P.bump("tensor", "t")
                P.op("scalar",
                     act(bass.AP(vpair, i * 130, [[NT * 130, 128], [65, 2], [1, 64]]),
                         bass.AP(ps, 0, [[512, 128], [64, 2], [1, 64]]),
                         AF.Copy),
                     wait=[("t", tw)])
                P.bump("scalar", "s")
            VS = P.now("s")

            for hloc in range(2):
                qcol = hloc * S
                for g in range(NG):
                    njc = 4 * g + 4
                    for jb in range(0, njc, 4):
                        jhi = min(njc, jb + 4)
                        for j in range(jb, jhi):
                            if j == jb:
                                jw = ([("s", VS), ("v", INIT_V)] if jb == 0
                                      else [("s", P.now("s"))])
                            else:
                                jw = []
                            P.op("tensor",
                                 mm(ps_s[:, (j - jb) * 512:(j - jb + 1) * 512],
                                    kh[:, qcol + j * 128: qcol + (j + 1) * 128],
                                    qh[:, qcol + g * GQ: qcol + (g + 1) * GQ],
                                    start=True, stop=True),
                                 wait=jw)
                        tw = P.bump("tensor", "t")
                        P.op("scalar",
                             act(probsT[:, jb * GQ: jhi * GQ],
                                 ps_s[:, 0:(jhi - jb) * 512], AF.Exp,
                                 bias=cc("negu")),
                             wait=[("t", tw)])
                        P.bump("scalar", "s")
                    SEXP = P.now("s")
                    for dj in range(4):
                        j = 4 * g + dj
                        o = 128 * dj
                        if o > 0:
                            P.op("vector", _ms(probsT[:, j * GQ: j * GQ + o], 0.0),
                                 wait=[("s", SEXP)])
                        P.op("vector",
                             tt(probsT[:, j * GQ + o: j * GQ + o + 128],
                                probsT[:, j * GQ + o: j * GQ + o + 128],
                                trimask[:], ALU.mult),
                             wait=[("s", SEXP)])
                    VMASK = P.bump("vector", "v")
                    for j in range(njc):
                        P.op("tensor",
                             mm(ps_z[0:65, 0:GQ],
                                bass.AP(vpair, j * 130 + hloc * 65,
                                        [[NT * 130, 128], [1, 65]]),
                                probsT[:, j * GQ:(j + 1) * GQ],
                                start=(j == 0), stop=(j == njc - 1)),
                             wait=[("v", VMASK), ("s", P.now("s"))] if j == 0 else [])
                    TPV = P.bump("tensor", "t")
                    def _recip(e):
                        with nc.allow_low_precision(reason="recip feeds bf16 mm operand"):
                            return e.reciprocal(recst[0:1, 0:GQ], ps_z[64:65, 0:GQ])
                    P.op("vector", _recip, wait=[("t", TPV)])
                    VREC = P.bump("vector", "v")
                    P.op("tensor", mm(ps_r[0:64, 0:GQ], bcast64[:], recst[:, 0:GQ],
                                      start=True, stop=True),
                         wait=[("v", VREC), ("s", P.now("s"))])
                    TREC = P.bump("tensor", "t")
                    P.op("vector", cpy(recsb[0:64, 0:GQ], ps_r[0:64, 0:GQ]),
                         wait=[("t", TREC)])
                    SREC = P.bump("vector", "v")
                    if hloc == 0:
                        zdst = zfm[0:64, p * S + g * GQ: p * S + (g + 1) * GQ]
                    else:
                        zdst = zstage[0:64, g * GQ:(g + 1) * GQ]
                    zw = [("dwr", W2_DWR)]
                    if hloc == 1 and g == 0:
                        zw.append(("dz", P.now("dz")))
                    P.op("vector", tt(zdst, ps_z[0:64, 0:GQ], recsb[0:64, 0:GQ],
                                      ALU.mult), wait=zw)
                    P.bump("vector", "v")
                if hloc == 1:
                    P.op("sync", _ld(zfm[64:128, p * S:(p + 1) * S], zstage[0:64, 0:S]),
                         wait=[("v", P.now("v"))], inc=[("dz", 16)])
        ATT_DZ = P.now("dz")
        ATT_V = P.now("v")
        close_scope("attn")
        ob = open_scope("outp", [("woutb", [128, NCH * D], BF16),
                                 ("attn_t", [128, D], F32)])
        woutb, attn_t = ob["woutb"], ob["attn_t"]

        P.op("gpsimd", _ld(bass.AP(woutb, 0, [[NCH * D, 128], [D, NCH], [1, D]]),
                           bass.AP(wop, 0, [[D, 128], [128 * D, NCH], [1, D]])),
             wait=[("t", P.now("t")), ("v", ATT_V), ("s", P.now("s"))],
             inc=[("dg", 16)])
        WOUT_DG = P.now("dg")

        for i in range(NT):
            for half in range(2):
                ps = ps_a if half == 0 else ps_b
                for c in range(NCH):
                    P.op("tensor",
                         mm(ps[:, 0:512],
                            zfm[:, c * S + i * 128: c * S + (i + 1) * 128],
                            woutb[:, c * D + half * 512: c * D + half * 512 + 512],
                            start=(c == 0), stop=(c == NCH - 1)),
                         wait=([("dg", WOUT_DG), ("dz", ATT_DZ), ("v", P.now("v")),
                                ("s", P.now("s"))] if c == 0 else []))
                tw = P.bump("tensor", "t")
                P.op("vector",
                     sttf(attn_t[:, half * 512: half * 512 + 512], ps[:, 0:512],
                          1.0, boutbc[:, half * 512: half * 512 + 512],
                          ALU.mult, ALU.add),
                     wait=[("t", tw)])
                P.bump("vector", "v")
            sw, _ = walk_ln_stats_apply(attn_t, 0, None, [])
            P.op("vector", ts(attn_t[:], attn_t[:], stat[:, 115:116], stat[:, 119:120],
                              ALU.add, ALU.mult), wait=[("s", sw)])
            P.op("vector", tt(x_tm[:, i * D:(i + 1) * D],
                              x_tm[:, i * D:(i + 1) * D], attn_t[:], ALU.add))
            P.bump("vector", "v")
        OUTPROJ_V = P.now("v")
        close_scope("outp")
        close_scope("zscope")

        if debug == "x2":
            for i in range(NT):
                P.op("sync", _ld(bass.AP(xout, i * 128 * D, [[D, 128], [1, D]]),
                                 x_tm[:, i * D:(i + 1) * D]),
                     wait=[("v", OUTPROJ_V)], inc=[("dout", 16)])
            P.op("sync", None, wait=[("dout", P.now("dout"))])
            debug = "STOP"

        if debug == "STOP":
            close_all_scopes()
            return _emit_and_finish(nc, P, sems)
        lt2 = open_scope("lntm2", [("ln_tm", [128, NT * D], BF16)])
        ps_tr = scope["lntm2"].enter_context(nc.psum_tensor("ps_tr2", [128, 512], BF16))
        T_OUTP = P.now("t")
        LN2_S = walk_ln_phase(lt2["ln_tm"], ps_tr,
                              lambda i: [("v", OUTPROJ_V), ("t", T_OUTP)])
        close_scope("lntm2")
        if debug == "ln2":
            P.op("gpsimd", _ld(bass.AP(xout, 0, [[16384, 128], [1, 16384]]),
                               ln_fm[:]),
                 wait=[("s", LN2_S)], inc=[("dout", 16)])
            P.op("sync", None, wait=[("dout", P.now("dout"))])
            close_all_scopes()
            return _emit_and_finish(nc, P, sems)
        fbufs = open_scope("ffn", [("hq8", [128, NHF * 1024], FP8),
                                   ("w1str", [128, 2 * 1024], BF16),
                                   ("w2qs", [128, 2 * D], FP8),
                                   ("xr_sb", [128, 1024], F32),
                                   ("ffo", [128, D], F32),
                                   ("ffq", [128, D], F16)])
        hq8, w1str, w2qs, xr_sb, ffo, ffq = (
            fbufs[n] for n in ("hq8", "w1str", "w2qs", "xr_sb", "ffo", "ffq"))
        cb2 = open_scope("chain2", [("wabs", [128, D], F32), ("wsgn", [128, D], F32),
                                    ("wacc", [128, D], F32), ("wis", [128, D], F32),
                                    ("wsc", [128, D], F32), ("wlo", [128, D], F32)])
        wabs, wsgn, wacc, wis, wsc, wlo = (cb2[n] for n in
                                           ("wabs", "wsgn", "wacc", "wis", "wsc", "wlo"))

        first_w2_load = [True]
        hpend = [None]
        w2_tc = []
        for sg in range(2):
            col0 = sg * 1024
            for m in range(NHF):
                slot = (m % 2) * 1024
                P.op("gpsimd",
                     _ld(bass.AP(w1str, slot, [[2 * 1024, 128], [128, NCH], [1, 128]]),
                         bass.AP(w1t, m * NCH * 128 * 128,
                                 [[128, 128], [128 * 128, NCH], [1, 128]])),
                     wait=[("t", P.now("t"))] if (sg > 0 or m >= 2)
                     else [("t", P.now("t")), ("s", LN2_S)],
                     inc=[("dg", 16)])
                dgm = P.now("dg")
                for g2 in range(2):
                    ps = ps_a if g2 == 0 else ps_b
                    for c in range(NCH):
                        P.op("tensor",
                             mm(ps[:, 0:512],
                                w1str[:, slot + c * 128: slot + (c + 1) * 128],
                                ln_fm[:, c * S + col0 + g2 * 512:
                                      c * S + col0 + g2 * 512 + 512],
                                start=(c == 0), stop=(c == NCH - 1)),
                             wait=([("dg", dgm), ("s", P.now("s"))] if c == 0 else []))
                    tw = P.bump("tensor", "t")
                    hk = m * 2 + g2
                    o = (hk % 2) * 512
                    hb = (wabs[:, o:o + 512], wsgn[:, o:o + 512],
                          wacc[:, o:o + 512], wis[:, o:o + 512],
                          wsc[:, o:o + 512], wlo[:, o:o + 512], f"h{hk % 2}")
                    P.op("vector", ts(xr_sb[:, o:o + 512], ps[:, 0:512],
                                      b1_sb[:, m:m + 1], 0.0, ALU.add, ALU.max),
                         wait=[("t", tw), ("v", qlastb.get(f"h{hk % 2}", 0))])
                    P.bump("vector", "v")
                    fin = walk_quant2(
                        hb, xr_sb[:, o:o + 512], 512, False, False, [],
                        [hq8[:, m * 1024 + g2 * 512: m * 1024 + g2 * 512 + 512]],
                        ext=True)
                    if hpend[0] is not None:
                        hpend[0]()
                    hpend[0] = fin
            if hpend[0] is not None:
                hpend[0]()
                hpend[0] = None
            HQ_V = P.now("v")
            if debug and debug.startswith("ffacc") and sg == 0:
                nacc = int(debug[5:])
                for c in range(nacc):
                    slot = (c % 2) * D
                    P.op("sync", _ld(w2qs[:, slot:slot + D],
                                     bass.AP(w2q_dram, c * 128 * D,
                                             [[D, 128], [1, D]])),
                         wait=[("v", HQ_V), ("dwr", W2_DWR),
                               ("t", P.now("t"))], inc=[("din", 16)])
                    P.op("tensor", mm(ps_a[:, 0:512], hq8[:, c * 1024: c * 1024 + 128],
                                      w2qs[:, slot:slot + 512],
                                      start=(c == 0), stop=(c == nacc - 1)),
                         wait=[("din", P.now("din")), ("v", HQ_V),
                               ("s", P.now("s"))])
                    P.bump("tensor", "t")
                P.op("scalar", cpy(ffo[:, 0:512], ps_a[:, 0:512]),
                     wait=[("t", P.now("t"))])
                s1 = P.bump("scalar", "s")
                P.op("sync", _ld(bass.AP(xout, 0, [[D, 128], [1, 512]]),
                                 ffo[:, 0:512]),
                     wait=[("s", s1)], inc=[("dout", 16)])
                P.op("sync", None, wait=[("dout", P.now("dout"))])
                close_all_scopes()
                return _emit_and_finish(nc, P, sems)
            if debug in ("hqa", "hqb") and sg == 0:
                hofs = 0 if debug == "hqa" else 16384
                P.op("gpsimd", _ld(bass.AP(xout, 0, [[16384, 128], [1, 16384]]),
                                   hq8[:, hofs:hofs + 16384]),
                     wait=[("v", HQ_V)], inc=[("dout", 16)])
                P.op("sync", None, wait=[("dout", P.now("dout"))])
                close_all_scopes()
                return _emit_and_finish(nc, P, sems)
            for stp in range(4):
                sts = (col0 // 128) + stp * 2
                psl = (ps_a, ps_b, ps_z, ps_r)
                for c in range(NHF):
                    slot = (c % 2) * D
                    lw = [("v", HQ_V)]
                    if first_w2_load[0]:
                        lw.append(("dwr", W2_DWR))
                        first_w2_load[0] = False
                    if len(w2_tc) >= 2:
                        lw.append(("t", w2_tc[-2]))
                    P.op("sync", _ld(w2qs[:, slot:slot + D],
                                     bass.AP(w2q_dram, c * 128 * D, [[D, 128], [1, D]])),
                         wait=lw, inc=[("din", 16)])
                    dinc = P.now("din")
                    if debug == "ffo1" and sg == 0 and stp == 0 and c == 0:
                        P.op("gpsimd", _ld(bass.AP(xout, 128 * D, [[D, 128], [1, D]]),
                                           w2qs[:, 0:D]),
                             wait=[("din", dinc)], inc=[("dg", 16)])
                        P.op("gpsimd", _ld(bass.AP(xout, 256 * D, [[D, 128], [1, 128]]),
                                           hq8[:, 0:128]),
                             wait=[("v", HQ_V)], inc=[("dg", 16)])
                        P.op("tensor", mm(ps_a[:, 0:512], hq8[:, 0:128],
                                          w2qs[:, 0:512], start=True, stop=True),
                             wait=[("din", dinc), ("v", HQ_V), ("s", P.now("s"))])
                        t1 = P.bump("tensor", "t")
                        P.op("scalar", cpy(ffo[:, 0:512], ps_a[:, 0:512]),
                             wait=[("t", t1)])
                        s1 = P.bump("scalar", "s")
                        P.op("sync", _ld(bass.AP(xout, 384 * D, [[D, 128], [1, 512]]),
                                         ffo[:, 0:512]),
                             wait=[("s", s1)], inc=[("dout", 16)])
                        P.op("sync", None, wait=[("dout", P.now("dout")),
                                                 ("dg", P.now("dg"))])
                        close_all_scopes()
                        return _emit_and_finish(nc, P, sems)
                    for sti in range(2):
                        for half in range(2):
                            ps = psl[sti * 2 + half]
                            P.op("tensor",
                                 mm(ps[:, 0:512],
                                    hq8[:, c * 1024 + (stp * 2 + sti) * 128:
                                        c * 1024 + (stp * 2 + sti + 1) * 128],
                                    w2qs[:, slot + half * 512: slot + half * 512 + 512],
                                    start=(c == 0), stop=(c == NHF - 1)),
                                 wait=([("din", dinc)]
                                       + ([("v", P.now("v")), ("s", P.now("s"))]
                                          if c == 0 else []))
                                 if (sti == 0 and half == 0) or c == 0 else [])
                    w2_tc.append(P.bump("tensor", "t"))
                tws = w2_tc[-1]
                if debug == "ffo2" and sg == 0 and stp == 0:
                    P.op("scalar", cpy(ffo[:, 0:512], ps_a[:, 0:512]),
                         wait=[("t", tws)])
                    s1 = P.bump("scalar", "s")
                    P.op("sync", _ld(bass.AP(xout, 0, [[D, 128], [1, 512]]),
                                     ffo[:, 0:512]),
                         wait=[("s", s1)], inc=[("dout", 16)])
                    P.op("sync", None, wait=[("dout", P.now("dout"))])
                    close_all_scopes()
                    return _emit_and_finish(nc, P, sems)
                for sti in range(2):
                    i = sts + sti
                    for half in range(2):
                        ps = psl[sti * 2 + half]
                        P.op("vector",
                             sttf(ffo[:, half * 512: half * 512 + 512], ps[:, 0:512],
                                  1.0, bqbc[:, half * 512: half * 512 + 512],
                                  ALU.mult, ALU.add),
                             wait=[("t", tws)])
                    vffo = P.bump("vector", "v")
                    if debug == "ffo0":
                        P.op("sync", _ld(bass.AP(xout, 0, [[D, 128], [1, D]]),
                                         ffo[:]),
                             wait=[("v", vffo)], inc=[("dout", 16)])
                        P.op("sync", None, wait=[("dout", P.now("dout"))])
                        close_all_scopes()
                        return _emit_and_finish(nc, P, sems)
                    qv = walk_quant(ffo[:], D, True, True, [("v", vffo)],
                                    [ffq[:, 0:512], ffq[:, 512:1024]], he=True)
                    if debug == "ffx" and i == 5:
                        P.op("sync", _ld(bass.AP(xout, 0, [[D, 128], [1, D]]),
                                         ffo[:]),
                             wait=[("v", qv)], inc=[("dout", 16)])
                        P.op("gpsimd", _ld(bass.AP(xout, 128 * D, [[D, 128], [1, D]]),
                                           ffq[:]),
                             wait=[("v", qv)], inc=[("dg", 16)])
                        P.op("sync", None, wait=[("dout", P.now("dout")),
                                                 ("dg", P.now("dg"))])
                        close_all_scopes()
                        return _emit_and_finish(nc, P, sems)
                    sw, _ = walk_ln_stats_apply(ffq, 0, None, [])
                    P.op("vector", ts(ffo[:], ffq[:], stat[:, 115:116], stat[:, 119:120],
                                      ALU.add, ALU.mult), wait=[("s", sw)])
                    P.op("vector", tt(x_tm[:, i * D:(i + 1) * D],
                                      x_tm[:, i * D:(i + 1) * D], ffo[:], ALU.add))
                    vres = P.bump("vector", "v")
                    P.op("sync", _ld(bass.AP(xout, i * 128 * D, [[D, 128], [1, D]]),
                                     x_tm[:, i * D:(i + 1) * D]),
                         wait=[("v", vres)], inc=[("dout", 16)])

        P.op("sync", None, wait=[("dout", P.now("dout"))])
        close_scope("chain2")
        close_scope("ffn")

        return _emit_and_finish(nc, P, sems)


def _emit_and_finish(nc, P, sems):
    with nc.Block() as blk:
        def make_body(eng_name):
            def body(eng):
                for waits, emit, incs in P.ops[eng_name]:
                    for sname, val in waits:
                        eng.wait_ge(sems[sname], val)
                    if emit is None:
                        assert not incs
                        continue
                    inst = emit(eng)
                    for sname, amt in incs:
                        inst.then_inc(sems[sname], amt)
            return body
        blk.sync(make_body("sync"))
        blk.tensor(make_body("tensor"))
        blk.vector(make_body("vector"))
        blk.scalar(make_body("scalar"))
        blk.gpsimd(make_body("gpsimd"))
    mybir.codegen_inst_isa_subclasses(nc)
    return nc


def _host_inputs(L0_attn_resid, L0_attn_W_Q, L0_attn_W_K, L0_attn_W_V,
                 L0_attn_W_out, L0_attn_b_out, L0_ffn_w1, L0_ffn_b1,
                 L0_ffn_w2, L0_ffn_b2):
    bf = ml_dtypes.bfloat16
    x = np.ascontiguousarray(np.asarray(L0_attn_resid, dtype=np.float32))
    wq = np.asarray(L0_attn_W_Q, dtype=np.float32).reshape(D, NH * DH)
    wk = np.asarray(L0_attn_W_K, dtype=np.float32).reshape(D, NH * DH)
    wv = np.asarray(L0_attn_W_V, dtype=np.float32).reshape(D, NH * DH)
    wqp = np.ascontiguousarray(wq.reshape(D, NP, 128).transpose(1, 0, 2))
    wkp = np.ascontiguousarray(wk.reshape(D, NP, 128).transpose(1, 0, 2))
    wvp = np.ascontiguousarray(wv.reshape(D, NP, 128).transpose(1, 0, 2))
    wo = np.asarray(L0_attn_W_out, dtype=np.float32).transpose(1, 0, 2).reshape(NH * DH, D)
    wop = np.ascontiguousarray(wo.reshape(NCH, 128, D))
    w1 = np.asarray(L0_ffn_w1, dtype=np.float32)
    w1t = np.ascontiguousarray(w1.reshape(NHF, 128, NCH, 128).transpose(0, 2, 3, 1))
    w2 = np.asarray(L0_ffn_w2, dtype=np.float32)
    w2t = np.ascontiguousarray(w2.T.reshape(NHF, 128, D))
    b1 = np.asarray(L0_ffn_b1, dtype=np.float32)
    b1in = np.ascontiguousarray(b1.reshape(NHF, 128).T)
    boutbc = np.ascontiguousarray(
        np.broadcast_to(np.asarray(L0_attn_b_out, dtype=np.float32), (128, D)))
    b2bc = np.ascontiguousarray(
        np.broadcast_to(np.asarray(L0_ffn_b2, dtype=np.float32), (128, D)))
    cstv = np.zeros((128, NCST), dtype=np.float32)
    for k, v in CST.items():
        cstv[:, CSTI[k]] = v
    identv = np.eye(128, dtype=np.float32).astype(bf)
    tri = np.triu(np.ones((128, 128), dtype=np.float32)).astype(bf)
    bc64 = np.zeros((128, 64), dtype=np.float32)
    bc64[0, :] = 1.0
    bc64 = bc64.astype(bf)
    common = dict(wqp=wqp, wkp=wkp, wvp=wvp, wop=wop, w1t=w1t, w2t=w2t,
                  b1in=b1in, boutbc_in=boutbc, b2bc_in=b2bc, cst_in=cstv,
                  ident_in=identv, trimask_in=tri, bcast64_in=bc64)
    in_maps = []
    for b in range(8):
        m = dict(common)
        m["xin"] = np.ascontiguousarray(x[b])
        in_maps.append(m)
    return in_maps


def kernel(L0_attn_resid, L0_attn_W_Q, L0_attn_W_K, L0_attn_W_V, L0_attn_W_out,
           L0_attn_b_out, L0_ffn_w1, L0_ffn_b1, L0_ffn_w2, L0_ffn_b2):
    in_maps = _host_inputs(L0_attn_resid, L0_attn_W_Q, L0_attn_W_K, L0_attn_W_V,
                           L0_attn_W_out, L0_attn_b_out, L0_ffn_w1, L0_ffn_b1,
                           L0_ffn_w2, L0_ffn_b2)
    nc = build_nc()
    res = run_bass_kernel_spmd(nc, in_maps, core_ids=list(range(8)))
    out = np.stack([np.asarray(r["xout"]) for r in res.results], axis=0)
    return out


# version 07
# speedup vs baseline: 1.0598x; 1.0454x over previous
import contextlib
import numpy as np
import ml_dtypes
import concourse.bass as bass
import concourse.mybir as mybir
from concourse.bass_utils import run_bass_kernel_spmd
from concourse.dve_op import DveOp
from concourse.dve_spec import Spec, Src0, Src1, lower, C0, C1, C2, eq, select
import concourse.dve_ops as dve_ops

F32 = mybir.dt.float32
F16 = mybir.dt.float16
BF16 = mybir.dt.bfloat16
FP8 = mybir.dt.float8e4
AF = mybir.ActivationFunctionType
ALU = mybir.AluOpType
AX = mybir.AxisListType
LN2 = 0.6931471805599453

S = 2048
D = 1024
NT = 16
NCH = 8
NH = 16
NP = 8
DH = 64
HF = 4096
NHF = 32
GQ = 512
NG = S // GQ
USOFT = 12.0
EPS = 1e-5

_OPS = {}


def _fdt(x):
    return np.float64 if isinstance(x, np.ndarray) and x.dtype == np.float64 else np.float32


def _reg(name, spec_body, reference, row):
    spec = Spec(body=spec_body)
    sha = lower(spec, ver="v3", perf_en=False, name=name).sha("v3")
    op = DveOp(name, spec, subdim=False, uops_sha={"v3": sha}, reference=reference)
    if not any(o.name == name for o in dve_ops.OPS):
        dve_ops.OPS.append(op)
        dve_ops._SUB_OPCODE_FOR_NAME[name] = row
        dve_ops.CUSTOM_DVE_SPECS[name] = spec
        dve_ops.CUSTOM_DVE_REFS[name] = op.ref
    return op


def _register_ops():
    if _OPS:
        return
    b4 = ((Src0 >= C0) + (Src0 >= C1)) + ((Src0 >= C2) + (Src0 >= C0 * C2))

    def r4(in0, in1, s0, s1, imm2):
        t = _fdt(in0); x = in0.astype(t)
        return ((x >= s0).astype(t) + (x >= s1).astype(t)
                + (x >= imm2).astype(t) + (x >= s0 * imm2).astype(t))

    def r4b(in0, in1, s0, s1, imm2):
        return r4(in0, in1, s0, s1, imm2) + in1.astype(_fdt(in0))

    def r1b(in0, in1, s0, s1, imm2):
        t = _fdt(in0); x = in0.astype(t)
        return (x >= s0).astype(t) + in1.astype(t)

    t = Src0 * Src1 + C0
    t = t - (t >= C1) * C1
    t = t - (t >= C2) * C2

    def rmp2(in0, in1, s0, s1, imm2):
        tt = _fdt(in0)
        x = in0.astype(tt) * in1.astype(tt) + s0
        x = x - (x >= s1).astype(tt) * s1
        return x - (x >= imm2).astype(tt) * imm2

    p = Src0 + C0
    p = p - (p >= C1) * C1
    p = p - (p >= C2) * C2

    def rp2(in0, in1, s0, s1, imm2):
        tt = _fdt(in0)
        x = in0.astype(tt) + s0
        x = x - (x >= s1).astype(tt) * s1
        return x - (x >= imm2).astype(tt) * imm2

    q = Src0
    q = q - (q >= C0) * C0
    qw = (q - C1) * Src1

    def rpw(in0, in1, s0, s1, imm2):
        tt = _fdt(in0)
        x = in0.astype(tt)
        x = x - (x >= s0).astype(tt) * s0
        return (x - s1) * in1.astype(tt)

    qe = Src0
    qe = qe - (qe >= C0) * C0
    qwe = (qe - C1 + eq(Src0, C0)) * Src1

    def rpwe(in0, in1, s0, s1, imm2):
        tt = _fdt(in0)
        x = in0.astype(tt)
        r = x - (x >= s0).astype(tt) * s0
        return (r - s1 + (x == s0).astype(tt)) * in1.astype(tt)

    b0t = Src0 >= C0
    qwt = (((Src0 - b0t * C0) - C1) + (b0t - (Src0 >= C2))) * Src1

    def rpwt(in0, in1, s0, s1, imm2):
        tt = _fdt(in0)
        x = in0.astype(tt)
        r = x - (x >= s0).astype(tt) * s0
        return (r - s1 + (x >= s0).astype(tt) * (x < imm2).astype(tt)) \
            * in1.astype(tt)

    r_ = Src0
    r_ = r_ - (r_ >= C0) * C0
    r_ = r_ - (r_ >= C2) * C2
    rw = (r_ - C1) * Src1

    def rp2w(in0, in1, s0, s1, imm2):
        tt = _fdt(in0)
        x = in0.astype(tt)
        x = x - (x >= s0).astype(tt) * s0
        x = x - (x >= imm2).astype(tt) * imm2
        return (x - s1) * in1.astype(tt)

    _OPS["LAD4"] = _reg("QLAD4", b4, r4, 30)
    _OPS["LAD4B"] = _reg("QLAD4B", b4 + Src1, r4b, 31)
    _OPS["LAD1B"] = _reg("QLAD1B", (Src0 >= C0) + Src1, r1b, 1)
    _OPS["MULPEEL2"] = _reg("QMULPEEL2", t, rmp2, 2)
    _OPS["PEEL2"] = _reg("QPEEL2", p, rp2, 3)
    _OPS["PEELW"] = _reg("QPEELW", qw, rpw, 4)
    _OPS["PEEL2W"] = _reg("QPEEL2W", rw, rp2w, 5)
    _OPS["PEELWE"] = _reg("QPEELWE", qwe, rpwe, 6)

    hv1 = select(Src0 >= C0, Src1 * C2, Src1)
    hv2 = select(Src0 >= C1, hv1 * C2, hv1)

    def rhalve2(in0, in1, s0, s1, imm2):
        tt = _fdt(in0)
        x = in0.astype(tt)
        t = np.where(x >= s0, in1.astype(tt) * imm2, in1.astype(tt))
        return np.where(x >= s1, t * imm2, t)

    sp = Src0 * C0 + C1
    sp = sp - (sp >= C2) * C2

    def rscpeel(in0, in1, s0, s1, imm2):
        tt = _fdt(in0)
        u = in0.astype(tt) * s0 + s1
        return u - (u >= imm2).astype(tt) * imm2

    _OPS["HALVE2"] = _reg("QHALVE2", hv2, rhalve2, 7)
    _OPS["SCPEEL"] = _reg("QSCPEEL", sp, rscpeel, 8)
    _OPS["PEELWET"] = _reg("QPEELWET", qwt, rpwt, 9)

    pc = Src0 + C0
    pc = pc - (pc >= C1) * C1
    pc = pc - (pc >= C2) * C1

    def rp2c(in0, in1, s0, s1, imm2):
        tt = _fdt(in0)
        x = in0.astype(tt) + s0
        x = x - (x >= s1).astype(tt) * s1
        return x - (x >= imm2).astype(tt) * s1

    _OPS["PEEL2C"] = _reg("QPEEL2C", pc, rp2c, 10)


_register_ops()

_HI = float(np.float32(LN2))
_LO = float(np.float32(LN2 - _HI))
_B1_8 = float(np.float32(9 * _HI))
_B2_8 = float(np.float32(9 * LN2 - _B1_8))
_B1_16 = float(np.float32(24 * _HI))
_B2_16 = float(np.float32(24 * LN2 - _B1_16))
CST = {
    "is8": 9 * LN2, "sc8": -9 * LN2, "is16": 24 * LN2, "sc16": -24 * LN2,
    "pln2": LN2, "nln2": -LN2, "eighth": 0.125, "negu": -USOFT, "eps": EPS,
    "neghalf": -0.5, "one": 1.0,
    "nhi": -_HI, "phi": _HI, "nlo": -_LO, "plo": _LO,
    "i8h": _B1_8, "i8l": _B2_8, "s8h": -_B1_8, "s8l": -_B2_8,
    "i16h": _B1_16, "i16l": _B2_16, "s16h": -_B1_16, "s16l": -_B2_16,
}
CSTI = {k: i for i, k in enumerate(CST)}
NCST = len(CST)

ENGINES = ("sync", "tensor", "vector", "scalar", "gpsimd")
SEMS = ("din", "dg", "dz", "dwr", "dout", "t", "v", "s", "g")


class Prog:

    def __init__(self):
        self.counts = {sn: 0 for sn in SEMS}
        self.ops = {e: [] for e in ENGINES}
        self.known = {e: {} for e in ENGINES}

    def now(self, name):
        return self.counts[name]

    def op(self, eng, emit, wait=(), inc=()):
        w = []
        for sname, val in wait:
            if val <= 0:
                continue
            if self.known[eng].get(sname, 0) >= val:
                continue
            self.known[eng][sname] = val
            w.append((sname, val))
        incs = []
        for sname, amt in inc:
            self.counts[sname] += amt
            incs.append((sname, amt))
            self.known[eng][sname] = max(self.known[eng].get(sname, 0),
                                         self.counts[sname])
        self.ops[eng].append([w, emit, incs])

    def bump(self, eng, sname, amt=1):
        self.counts[sname] += amt
        self.ops[eng][-1][2].append((sname, amt))
        self.known[eng][sname] = max(self.known[eng].get(sname, 0),
                                     self.counts[sname])
        return self.counts[sname]


def build_nc(debug=None):
    nc = bass.Bass("TRN2", target_bir_lowering=False, detect_race_conditions=False)

    xin = nc.dram_tensor("xin", [S, D], F32, kind="ExternalInput")
    wqp = nc.dram_tensor("wqp", [NP, D, 128], F32, kind="ExternalInput")
    wkp = nc.dram_tensor("wkp", [NP, D, 128], F32, kind="ExternalInput")
    wvp = nc.dram_tensor("wvp", [NP, D, 128], F32, kind="ExternalInput")
    wop = nc.dram_tensor("wop", [NCH, 128, D], F32, kind="ExternalInput")
    w1t = nc.dram_tensor("w1t", [NHF, NCH, 128, 128], F32, kind="ExternalInput")
    w2t = nc.dram_tensor("w2t", [NHF, 128, D], F32, kind="ExternalInput")
    b1in = nc.dram_tensor("b1in", [128, NHF], F32, kind="ExternalInput")
    boutbc_in = nc.dram_tensor("boutbc_in", [128, D], F32, kind="ExternalInput")
    b2bc_in = nc.dram_tensor("b2bc_in", [128, D], F32, kind="ExternalInput")
    cst_in = nc.dram_tensor("cst_in", [128, NCST], F32, kind="ExternalInput")
    ident_in = nc.dram_tensor("ident_in", [128, 128], BF16, kind="ExternalInput")
    trimask_in = nc.dram_tensor("trimask_in", [128, 128], BF16, kind="ExternalInput")
    bcast64_in = nc.dram_tensor("bcast64_in", [128, 64], BF16, kind="ExternalInput")
    xout = nc.dram_tensor("xout", [S, D], F32, kind="ExternalOutput")
    w2q_dram = nc.dram_tensor("w2q_dram", [NHF, 128, D], FP8, kind="Internal")

    P = Prog()

    with contextlib.ExitStack() as est:
        E = est.enter_context
        sems = {sn: E(nc.semaphore(sn)) for sn in SEMS}

        x_tm = E(nc.sbuf_tensor("x_tm", [128, NT * D], F32))
        ln_fm = E(nc.sbuf_tensor("ln_fm", [128, NCH * S], BF16))
        cstb = E(nc.sbuf_tensor("cstb", [128, NCST], F32))
        ident = E(nc.sbuf_tensor("ident", [128, 128], BF16))
        trimask = E(nc.sbuf_tensor("trimask", [128, 128], BF16))
        bcast64 = E(nc.sbuf_tensor("bcast64", [128, 64], BF16))
        b1_sb = E(nc.sbuf_tensor("b1_sb", [128, NHF], F32))
        boutbc = E(nc.sbuf_tensor("boutbc", [128, D], F32))

        bqbc = E(nc.sbuf_tensor("bqbc", [128, D], F16))
        stat = E(nc.sbuf_tensor("stat", [128, 128], F32))
        scratch = E(nc.sbuf_tensor("scratch", [128, 2048], F32))
        sq = scratch[:, 0:1024]

        scope = {}

        def open_scope(name, specs):
            stk = contextlib.ExitStack()
            scope[name] = stk
            return {nm: stk.enter_context(nc.sbuf_tensor(f"{name}_{nm}", sh, dt))
                    for nm, sh, dt in specs}

        def close_scope(name):
            scope.pop(name).close()

        def close_all_scopes():
            for name in reversed(list(scope)):
                scope.pop(name).close()

        ps_a = E(nc.psum_tensor("ps_a", [128, 512], F32))
        ps_b = E(nc.psum_tensor("ps_b", [128, 512], F32))
        ps_z = E(nc.psum_tensor("ps_z", [128, 512], F32))
        ps_r = E(nc.psum_tensor("ps_r", [128, 512], F32))

        def cc(name, lo=0, hi=128):
            return cstb[lo:hi, CSTI[name]:CSTI[name] + 1]

        def _ld(dst_ap, src_ap):
            return lambda e, d=dst_ap, s=src_ap: e.dma_start(d, s)

        def _ms(ap, val):
            return lambda e, a=ap, vv=val: e.memset(a, vv)

        def ts(out, in0, s1, s2, op0, op1=None):
            if op1 is None:
                return lambda e: e.tensor_scalar(out, in0, s1, None, op0)
            return lambda e: e.tensor_scalar(out, in0, s1, s2, op0, op1)

        def tt(out, in0, in1, op):
            return lambda e: e.tensor_tensor(out, in0, in1, op)

        def sttf(out, in0, sc, in1, op0, op1, accum=None):
            return lambda e: e.scalar_tensor_tensor(out, in0, sc, in1, op0, op1,
                                                    accum_out=accum)

        def act(out, in_, func, bias=0.0, scale=1.0):
            return lambda e: e.activation(out, in_, func, bias=bias, scale=scale)

        def cdve(opn, out, in0, in1=None, **kw):
            if in1 is None:
                return lambda e: e._custom_dve(_OPS[opn], out=out, in0=in0, **kw)
            return lambda e: e._custom_dve(_OPS[opn], out=out, in0=in0, in1=in1, **kw)

        def mm(out, lhsT, rhs, start, stop):
            return lambda e: e.matmul(out, lhsT, rhs, start=start, stop=stop)

        def tpz(out, in_):
            return lambda e: e.transpose(out, in_, ident[:])

        def red(out, in_, op):
            return lambda e: e.tensor_reduce(out, in_, AX.X, op)

        def cpy(out, in_):
            return lambda e: (e.tensor_copy(out, in_) if hasattr(e, "tensor_copy")
                              else e.copy(out, in_))

        P.op("sync", _ld(bass.AP(x_tm, 0, [[NT * D, 128], [D, NT], [1, D]]),
                         bass.AP(xin, 0, [[D, 128], [128 * D, NT], [1, D]])),
             inc=[("din", 16)])
        for src, dst in ((cst_in, cstb), (b1in, b1_sb), (boutbc_in, boutbc),
                         (ident_in, ident), (trimask_in, trimask),
                         (bcast64_in, bcast64)):
            P.op("sync", _ld(dst[:], src[:]), inc=[("din", 16)])
        PRO_DIN = P.now("din")

        if debug == "x0":
            for i in range(NT):
                P.op("sync", _ld(bass.AP(xout, i * 128 * D, [[D, 128], [1, D]]),
                                 x_tm[:, i * D:(i + 1) * D]),
                     wait=[("din", PRO_DIN)], inc=[("dout", 16)])
            P.op("sync", None, wait=[("dout", P.now("dout"))])
            return _emit_and_finish(nc, P, sems)

        qlast = {"v": 0}
        lnlast = {"s": 0}

        L16 = [("LAD4", dict(s0=2.0 ** -13, s1=2.0 ** -12, imm2=8.0)),
               ("LAD4B", dict(s0=2.0 ** -11, s1=2.0 ** -8, imm2=4.0)),
               ("LAD4B", dict(s0=2.0 ** -7, s1=2.0 ** -4, imm2=2.0)),
               ("LAD4B", dict(s0=2.0 ** -2, s1=2.0 ** -5, imm2=2.0 ** -1)),
               ("LAD1B", dict(s0=1.0))]
        L8 = [("LAD4", dict(s0=2.0 ** -5, s1=2.0 ** -4, imm2=4.0)),
              ("LAD4B", dict(s0=2.0 ** -2, s1=1.0, imm2=2.0))]
        L8X = L8 + [("LAD4B", dict(s0=8.0, s1=16.0, imm2=32.0))]

        ETA = 2.0 ** -22
        CATCH = 2.0 - ETA

        qlastb = {}

        def walk_quant2(bufs, x_ap, n, man16, signed, xwaits, out_slices,
                        ext=False, he=False):
            wabs_, wsgn_, wacc_, wis_, wsc_, wlo_, key = bufs
            lads = L16 if man16 else (L8X if ext else L8)
            ih, il = ("i16h", "i16l") if man16 else ("i8h", "i8l")
            sh, sl_ = ("s16h", "s16l") if man16 else ("s8h", "s8l")
            guard = [("v", qlastb.get(key, 0))]
            if signed:
                P.op("scalar", act(wabs_, x_ap, AF.Abs),
                     wait=list(xwaits) + guard)
                P.op("scalar", act(wsgn_, x_ap, AF.Sign))
                sg = P.bump("scalar", "s")
                ax_ap = wabs_
                lw = [("s", sg)]
            else:
                ax_ap = x_ap
                lw = list(xwaits) + guard
            for i, (opn, kw) in enumerate(lads):
                if i == 0:
                    P.op("vector", cdve(opn, wacc_, ax_ap, **kw), wait=lw)
                else:
                    P.op("vector", cdve(opn, wacc_, ax_ap, wacc_, **kw))
            vw = P.bump("vector", "v")
            P.op("scalar", act(wis_, wacc_, AF.Exp,
                               bias=cc(ih), scale=cc("nhi")), wait=[("v", vw)])
            P.op("scalar", act(wlo_, wacc_, AF.Exp,
                               bias=cc(il), scale=cc("nlo")))
            P.op("scalar", act(wsc_, wacc_, AF.Exp,
                               bias=cc(sh), scale=cc("phi")))
            P.op("scalar", act(wacc_, wacc_, AF.Exp,
                               bias=cc(sl_), scale=cc("plo")))
            sw = P.bump("scalar", "s")
            P.op("gpsimd", tt(wis_, wis_, wlo_, ALU.mult), wait=[("s", sw)])
            P.op("gpsimd", tt(wsc_, wsc_, wacc_, ALU.mult))
            gw = P.bump("gpsimd", "g")

            def finish():
                nh = (n + 511) // 512
                for hh in range(nh):
                    lo, hi = hh * 512, min(n, hh * 512 + 512)
                    m = hi - lo
                    sl = slice(lo, hi)
                    w = [("g", gw)] if hh == 0 else ()
                    if man16:
                        P.op("vector", cdve("MULPEEL2", scratch[:, 0:m], ax_ap[:, sl],
                                            wis_[:, sl], s0=0.5, s1=2048.0, imm2=1024.0),
                             wait=w)
                        P.op("vector", cdve("PEEL2", scratch[:, 512:512 + m], scratch[:, 0:m],
                                            s0=0.0, s1=512.0, imm2=256.0))
                        P.op("vector", cdve("PEEL2", scratch[:, 0:m], scratch[:, 512:512 + m],
                                            s0=0.0, s1=128.0, imm2=64.0))
                        P.op("vector", cdve("PEEL2", scratch[:, 512:512 + m], scratch[:, 0:m],
                                            s0=0.0, s1=32.0, imm2=16.0))
                        P.op("vector", cdve("PEEL2", scratch[:, 0:m], scratch[:, 512:512 + m],
                                            s0=0.0, s1=8.0, imm2=4.0))
                        if he:
                            P.op("vector", cdve("PEEL2C", scratch[:, 512:512 + m],
                                                scratch[:, 0:m], s0=0.0, s1=2.0,
                                                imm2=CATCH))
                            P.op("vector", cdve("PEELWET", scratch[:, 1024:1024 + m],
                                                scratch[:, 512:512 + m], wsc_[:, sl],
                                                s0=1.0, s1=0.5, imm2=1.0 + ETA))
                        else:
                            P.op("vector", cdve("PEEL2W", scratch[:, 1024:1024 + m],
                                                scratch[:, 0:m],
                                                wsc_[:, sl], s0=2.0, s1=0.5, imm2=1.0))
                    else:
                        P.op("vector", cdve("MULPEEL2", scratch[:, 0:m], ax_ap[:, sl],
                                            wis_[:, sl], s0=0.5, s1=16.0, imm2=8.0),
                             wait=w)
                        P.op("vector", cdve("PEEL2", scratch[:, 512:512 + m], scratch[:, 0:m],
                                            s0=0.0, s1=4.0, imm2=2.0))
                        if he:
                            P.op("vector", cdve("PEEL2C", scratch[:, 0:m],
                                                scratch[:, 512:512 + m], s0=0.0, s1=2.0,
                                                imm2=CATCH))
                            P.op("vector", cdve("PEELWET", scratch[:, 1024:1024 + m],
                                                scratch[:, 0:m], wsc_[:, sl],
                                                s0=1.0, s1=0.5, imm2=1.0 + ETA))
                        else:
                            P.op("vector", cdve("PEELW", scratch[:, 1024:1024 + m],
                                                scratch[:, 512:512 + m],
                                                wsc_[:, sl], s0=1.0, s1=0.5))
                    if signed:
                        P.op("vector", tt(scratch[:, 1536:1536 + m], ax_ap[:, sl],
                                          scratch[:, 1024:1024 + m], ALU.subtract))
                        P.op("vector", tt(out_slices[hh], scratch[:, 1536:1536 + m],
                                          wsgn_[:, sl], ALU.mult))
                    else:
                        P.op("vector", tt(out_slices[hh], x_ap[:, sl],
                                          scratch[:, 1024:1024 + m], ALU.subtract))
                vend = P.bump("vector", "v")
                qlastb[key] = vend
                qlast["v"] = vend
                return vend
            return finish

        def walk_quant(x_ap, n, man16, signed, xwaits, out_slices, ext=False,
                       he=False):
            bufs = (wabs[:, 0:n], wsgn[:, 0:n], wacc[:, 0:n], wis[:, 0:n],
                    wsc[:, 0:n], wlo[:, 0:n], "serial")
            qlastb["serial"] = qlast["v"]
            return walk_quant2(bufs, x_ap, n, man16, signed, xwaits,
                               out_slices, ext=ext, he=he)()

        def walk_ln_stats_apply(src_t, src_ofs, dst_ap, xwaits):
            full = src_t[:, src_ofs:src_ofs + D]
            h0 = src_t[:, src_ofs:src_ofs + 512]
            h1 = src_t[:, src_ofs + 512:src_ofs + 1024]
            P.op("vector", red(stat[:, 112:113], full, ALU.add),
                 wait=list(xwaits) + [("s", lnlast["s"])])
            P.op("vector", tt(sq[:, 0:512], h0, h0, ALU.mult))
            P.op("vector", tt(sq[:, 512:1024], h1, h1, ALU.mult))
            P.op("vector", red(stat[:, 113:114], sq[:], ALU.add))
            vw = P.bump("vector", "v")
            P.op("scalar", act(stat[:, 115:116], stat[:, 112:113], AF.Identity,
                               scale=-1.0 / D), wait=[("v", vw)])
            P.op("scalar", act(stat[:, 116:117], stat[:, 113:114], AF.Identity,
                               scale=1.0 / D, bias=cc("eps")))
            P.op("scalar", act(stat[:, 122:123], stat[:, 112:113], AF.Identity))
            P.op("scalar", act(stat[:, 123:124], stat[:, 112:113], AF.Identity))
            P.op("scalar", act(stat[:, 124:125], stat[:, 112:113], AF.Identity))
            P.op("scalar", act(stat[:, 117:118], stat[:, 112:113], AF.Square,
                               scale=1.0 / D))
            P.op("scalar", act(stat[:, 118:119], stat[:, 117:118], AF.Ln, scale=-1.0,
                               bias=stat[:, 116:117]))
            P.op("scalar", act(stat[:, 119:120], stat[:, 118:119], AF.Exp,
                               scale=cc("neghalf")))
            sw = P.bump("scalar", "s")
            va = None
            if dst_ap is not None:
                P.op("vector", ts(dst_ap, full, stat[:, 115:116], stat[:, 119:120],
                                  ALU.add, ALU.mult), wait=[("s", sw)])
                va = P.bump("vector", "v")
            lnlast["s"] = sw
            return sw, va

        def walk_ln_phase(ln_tm, ps_tr, xwaits_fn):
            cS, cQ, cM3, cMQ, cM2, cL, cR = 0, 16, 32, 48, 64, 80, 96
            for i in range(NT):
                full = x_tm[:, i * D:(i + 1) * D]
                h0 = x_tm[:, i * D:i * D + 512]
                h1 = x_tm[:, i * D + 512:(i + 1) * D]
                w = list(xwaits_fn(i))
                if i == 0:
                    w += [("s", lnlast["s"])]
                P.op("vector", red(stat[:, cS + i:cS + i + 1], full, ALU.add),
                     wait=w)
                P.op("vector", tt(sq[:, 0:512], h0, h0, ALU.mult))
                P.op("vector", tt(sq[:, 512:1024], h1, h1, ALU.mult))
                P.op("vector", red(stat[:, cQ + i:cQ + i + 1], sq[:], ALU.add))
            vw = P.bump("vector", "v")
            for i in range(NT):
                P.op("scalar", act(stat[:, cM3 + i:cM3 + i + 1],
                                   stat[:, cS + i:cS + i + 1],
                                   AF.Identity, scale=-1.0 / D),
                     wait=[("v", vw)] if i == 0 else ())
            for i in range(NT):
                P.op("scalar", act(stat[:, cMQ + i:cMQ + i + 1],
                                   stat[:, cQ + i:cQ + i + 1],
                                   AF.Identity, scale=1.0 / D, bias=cc("eps")))
            for i in range(NT):
                P.op("scalar", act(stat[:, cM2 + i:cM2 + i + 1],
                                   stat[:, cS + i:cS + i + 1],
                                   AF.Square, scale=1.0 / D))
            for i in range(NT):
                P.op("scalar", act(stat[:, cL + i:cL + i + 1],
                                   stat[:, cM2 + i:cM2 + i + 1],
                                   AF.Ln, scale=-1.0,
                                   bias=stat[:, cMQ + i:cMQ + i + 1]))
            for i in range(NT):
                P.op("scalar", act(stat[:, cR + i:cR + i + 1],
                                   stat[:, cL + i:cL + i + 1],
                                   AF.Exp, scale=cc("neghalf")))
            sw = P.bump("scalar", "s")
            lnlast["s"] = sw
            for i in range(NT):
                P.op("vector", ts(ln_tm[:, i * D:(i + 1) * D],
                                  x_tm[:, i * D:(i + 1) * D],
                                  stat[:, cM3 + i:cM3 + i + 1],
                                  stat[:, cR + i:cR + i + 1],
                                  ALU.add, ALU.mult),
                     wait=[("s", sw)] if i == 0 else ())
                va = P.bump("vector", "v")
                for half in range(2):
                    for cq in range(4):
                        c = half * 4 + cq
                        P.op("tensor",
                             tpz(ps_tr[:, cq * 128:(cq + 1) * 128],
                                 ln_tm[:, i * D + c * 128: i * D + (c + 1) * 128]),
                             wait=[("v", va), ("s", P.now("s"))] if cq == 0 else [])
                    tw = P.bump("tensor", "t")
                    for cq in range(4):
                        c = half * 4 + cq
                        P.op("scalar",
                             cpy(ln_fm[:, c * S + i * 128: c * S + (i + 1) * 128],
                                 ps_tr[:, cq * 128:(cq + 1) * 128]),
                             wait=[("t", tw)] if cq == 0 else [])
                    se = P.bump("scalar", "s")
                    P.op("tensor", None, wait=[("s", se)])
            return P.now("s")

        zs = open_scope("zscope", [("zfm", [128, NCH * S], BF16)])
        zfm = zs["zfm"]
        lt = open_scope("lntm", [("ln_tm", [128, NT * D], BF16)])
        ps_tr = scope["lntm"].enter_context(nc.psum_tensor("ps_tr1", [128, 512], BF16))
        LN1_S = walk_ln_phase(lt["ln_tm"], ps_tr, lambda i: [("din", PRO_DIN)])
        close_scope("lntm")
        if debug == "ln1":
            P.op("gpsimd", _ld(bass.AP(xout, 0, [[16384, 128], [1, 16384]]),
                               ln_fm[:]),
                 wait=[("s", LN1_S)], inc=[("dout", 16)])
            P.op("sync", None, wait=[("dout", P.now("dout"))])
            close_all_scopes()
            return _emit_and_finish(nc, P, sems)
        cbufs = open_scope("chain", [("wabs", [128, D], F32), ("wsgn", [128, D], BF16),
                                     ("wacc", [128, D], F32), ("wis", [128, D], F32),
                                     ("wsc", [128, D], F32), ("wlo", [128, D], F32)],
)
        wabs, wsgn, wacc, wis, wsc, wlo = (cbufs[n] for n in
                                           ("wabs", "wsgn", "wacc", "wis", "wsc", "wlo"))
        wsetA = (wabs[:], wsgn[:], wacc[:], wis[:], wsc[:], wlo[:], "w2A")
        wb = open_scope("w2st", [("w2st", [128, D], F32),
                                 ("w2qs", [128, D], FP8)])
        w2st, w2qs = wb["w2st"], wb["w2qs"]
        att = open_scope("attn", [("qh", [128, 2 * S], BF16), ("kh", [128, 2 * S], BF16),
                                  ("wstr", [128, 3 * 1024], BF16),
                                  ("vpair", [128, NT * 130], BF16),
                                  ("probsT", [128, 4 * GQ], BF16),
                                  ("zstage", [64, S], BF16),
                                  ("recst", [128, GQ], BF16),
                                  ("recsb", [64, GQ], BF16)])
        qh, kh, wstr, vpair, probsT, zstage, recst, recsb = (
            att[n] for n in ("qh", "kh", "wstr", "vpair", "probsT", "zstage",
                             "recst", "recsb"))
        ps_s = scope["attn"].enter_context(nc.psum_tensor("ps_s", [128, 2048], F32))
        T_LN1 = P.now("t")
        zevacv = [0]
        wpend = [None]
        P.op("vector", _ms(qh[64:128, 0:S], 0.0),
             wait=[("din", PRO_DIN), ("t", T_LN1)])
        P.op("vector", _ms(qh[0:64, S:2 * S], 0.0))
        P.op("vector", _ms(kh[64:128, 0:S], 0.0))
        P.op("vector", _ms(kh[0:64, S:2 * S], 0.0))
        P.op("vector", _ms(vpair[:], 1.0))
        P.op("vector", _ms(recst[:], 0.0))
        INIT_V = P.bump("vector", "v")

        for p in range(NP):
            base = 0
            for idx, wsrc in enumerate((wqp, wkp, wvp)):
                P.op("gpsimd",
                     _ld(bass.AP(wstr, base + idx * 1024,
                                 [[3 * 1024, 128], [128, NCH], [1, 128]]),
                         bass.AP(wsrc, p * D * 128,
                                 [[128, 128], [128 * 128, NCH], [1, 128]])),
                     wait=[("t", P.now("t"))] if p >= 1 else [("t", T_LN1)],
                     inc=[("dg", 16)])
            DG_P = P.now("dg")

            evs = {}
            for which, wof, dst, scale in (("q", 0, qh, True), ("k", 1024, kh, False)):
                for g4 in range(4):
                    ps = ps_a if g4 % 2 == 0 else ps_b
                    for c in range(NCH):
                        P.op("tensor",
                             mm(ps[:, 0:512],
                                wstr[:, base + wof + c * 128: base + wof + (c + 1) * 128],
                                ln_fm[:, c * S + g4 * 512: c * S + (g4 + 1) * 512],
                                start=(c == 0), stop=(c == NCH - 1)),
                             wait=([("dg", DG_P), ("s", P.now("s"))] if c == 0 else []))
                    tw = P.bump("tensor", "t")
                    col = g4 * 512
                    nm = "eighth" if scale else "one"
                    P.op("scalar", act(dst[0:64, col:col + 512], ps[0:64, 0:512],
                                       AF.Copy, scale=cc(nm, 0, 64)),
                         wait=[("t", tw)])
                    P.op("scalar", act(dst[64:128, S + col:S + col + 512],
                                       ps[64:128, 0:512], AF.Copy, scale=cc(nm, 64, 128)))
                    P.bump("scalar", "s")
            for i in range(NT):
                ps = ps_a if i % 2 == 0 else ps_b
                for c in range(NCH):
                    P.op("tensor",
                         mm(ps[:, 0:128],
                            ln_fm[:, c * S + i * 128: c * S + (i + 1) * 128],
                            wstr[:, base + 2048 + c * 128: base + 2048 + (c + 1) * 128],
                            start=(c == 0), stop=(c == NCH - 1)),
                         wait=[("s", P.now("s"))] if c == 0 else [])
                tw = P.bump("tensor", "t")
                P.op("scalar",
                     act(bass.AP(vpair, i * 130, [[NT * 130, 128], [65, 2], [1, 64]]),
                         bass.AP(ps, 0, [[512, 128], [64, 2], [1, 64]]),
                         AF.Copy),
                     wait=[("t", tw)])
                P.bump("scalar", "s")
            VS = P.now("s")

            for hloc in range(2):
                qcol = hloc * S
                for g in range(NG):
                    njc = 4 * g + 4
                    for jb in range(0, njc, 4):
                        jhi = jb + 4
                        diag = (jhi == njc)
                        for j in range(jb, jhi):
                            if j == jb:
                                jw = ([("s", VS), ("v", INIT_V)] if jb == 0
                                      else [("s", P.now("s"))])
                            else:
                                jw = []
                            P.op("tensor",
                                 mm(ps_s[:, (j - jb) * 512:(j - jb + 1) * 512],
                                    kh[:, qcol + j * 128: qcol + (j + 1) * 128],
                                    qh[:, qcol + g * GQ: qcol + (g + 1) * GQ],
                                    start=True, stop=True),
                                 wait=jw)
                        tw = P.bump("tensor", "t")
                        P.op("scalar",
                             act(probsT[:, 0: 4 * GQ],
                                 ps_s[:, 0:2048], AF.Exp,
                                 bias=cc("negu")),
                             wait=[("t", tw)])
                        se = P.bump("scalar", "s")
                        pvw = [("s", se)]
                        if diag:
                            for dj in range(4):
                                o = 128 * dj
                                if o > 0:
                                    P.op("vector",
                                         _ms(probsT[:, dj * GQ: dj * GQ + o], 0.0),
                                         wait=[("s", se)])
                                P.op("vector",
                                     tt(probsT[:, dj * GQ + o: dj * GQ + o + 128],
                                        probsT[:, dj * GQ + o: dj * GQ + o + 128],
                                        trimask[:], ALU.mult),
                                     wait=[("s", se)])
                            VMASK = P.bump("vector", "v")
                            pvw = [("v", VMASK)]
                        for j in range(jb, jhi):
                            w = []
                            if j == jb:
                                w = list(pvw)
                            if j == 0:
                                w.append(("v", zevacv[0]))
                            P.op("tensor",
                                 mm(ps_z[0:65, 0:GQ],
                                    bass.AP(vpair, j * 130 + hloc * 65,
                                            [[NT * 130, 128], [1, 65]]),
                                    probsT[:, (j - jb) * GQ:(j - jb + 1) * GQ],
                                    start=(j == 0), stop=(j == njc - 1)),
                                 wait=w)
                    TPV = P.bump("tensor", "t")
                    def _recip(e):
                        with nc.allow_low_precision(reason="recip feeds bf16 mm operand"):
                            return e.reciprocal(recst[0:1, 0:GQ], ps_z[64:65, 0:GQ])
                    P.op("vector", _recip, wait=[("t", TPV)])
                    VREC = P.bump("vector", "v")
                    P.op("tensor", mm(ps_r[0:64, 0:GQ], bcast64[:], recst[:, 0:GQ],
                                      start=True, stop=True),
                         wait=[("v", VREC), ("s", P.now("s"))])
                    TREC = P.bump("tensor", "t")
                    P.op("vector", cpy(recsb[0:64, 0:GQ], ps_r[0:64, 0:GQ]),
                         wait=[("t", TREC)])
                    SREC = P.bump("vector", "v")
                    if hloc == 0:
                        zdst = zfm[0:64, p * S + g * GQ: p * S + (g + 1) * GQ]
                    else:
                        zdst = zstage[0:64, g * GQ:(g + 1) * GQ]
                    zw = []
                    if hloc == 1 and g == 0:
                        zw.append(("dz", P.now("dz")))
                    P.op("vector", tt(zdst, ps_z[0:64, 0:GQ], recsb[0:64, 0:GQ],
                                      ALU.mult), wait=zw)
                    zevacv[0] = P.bump("vector", "v")
                if hloc == 1:
                    P.op("sync", _ld(zfm[64:128, p * S:(p + 1) * S], zstage[0:64, 0:S]),
                         wait=[("v", P.now("v"))], inc=[("dz", 16)])
            if p == 0:
                P.op("sync", _ld(w2st[:], b2bc_in[:]),
                     wait=[("t", T_LN1)], inc=[("din", 16)])
                walk_quant(w2st[:], D, True, True,
                           [("din", P.now("din"))],
                           [bqbc[:, 0:512], bqbc[:, 512:1024]], he=True)
                qlastb["w2A"] = qlast["v"]
            for c in range(4 * p, 4 * p + 4):
                lw = ([("t", T_LN1), ("s", P.now("s"))] if c == 0
                      else [("s", P.now("s")), ("dwr", 16 * max(c - 1, 0))])
                P.op("sync", _ld(w2st[:],
                                 bass.AP(w2t, c * 128 * D, [[D, 128], [1, D]])),
                     wait=lw, inc=[("din", 16)])
                din_c = P.now("din")
                fin = walk_quant2(wsetA, w2st[:], D, False, True,
                                  [("din", din_c)],
                                  [w2qs[:, 0:512], w2qs[:, 512:D]], he=True)
                if c >= 1:
                    P.op("vector", None, wait=[("dwr", 16 * c)])
                vend = fin()
                P.op("gpsimd", _ld(bass.AP(w2q_dram, c * 128 * D,
                                           [[D, 128], [1, D]]),
                                   w2qs[:]),
                     wait=[("v", vend)], inc=[("dwr", 16)])
        W2_DWR = P.now("dwr")
        if debug in ("w2q", "w2qb"):
            cofs = 0 if debug == "w2q" else 16
            for cx in range(16):
                c = cx + cofs
                P.op("sync", _ld(w2qs[:],
                                 bass.AP(w2q_dram, c * 128 * D, [[D, 128], [1, D]])),
                     wait=[("dwr", W2_DWR), ("dg", P.now("dg"))], inc=[("din", 16)])
                P.op("gpsimd", _ld(bass.AP(xout, cx * 1024,
                                           [[16384, 128], [1, 1024]]),
                                   w2qs[:]),
                     wait=[("din", P.now("din"))], inc=[("dg", 16)])
            P.op("sync", None, wait=[("dg", P.now("dg"))])
            close_all_scopes()
            return _emit_and_finish(nc, P, sems)
        if debug == "bq":
            P.op("sync", _ld(bass.AP(xout, 0, [[D, 128], [1, D]]), bqbc[:]),
                 wait=[("v", P.now("v"))], inc=[("dout", 16)])
            P.op("sync", None, wait=[("dout", P.now("dout"))])
            close_all_scopes()
            return _emit_and_finish(nc, P, sems)
        ATT_DZ = P.now("dz")
        ATT_V = P.now("v")
        close_scope("attn")
        close_scope("w2st")
        close_scope("chain")
        ob = open_scope("outp", [("woutb", [128, NCH * D], BF16),
                                 ("attn_t", [128, D], F32)])
        woutb, attn_t = ob["woutb"], ob["attn_t"]

        P.op("gpsimd", _ld(bass.AP(woutb, 0, [[NCH * D, 128], [D, NCH], [1, D]]),
                           bass.AP(wop, 0, [[D, 128], [128 * D, NCH], [1, D]])),
             wait=[("t", P.now("t")), ("v", ATT_V), ("s", P.now("s"))],
             inc=[("dg", 16)])
        WOUT_DG = P.now("dg")

        op_evacv = {}
        S_ATT = P.now("s")
        for i in range(NT):
            for half in range(2):
                ps = ps_a if half == 0 else ps_b
                for c in range(NCH):
                    if c == 0:
                        if half in op_evacv:
                            mmw = [("v", op_evacv[half])]
                        else:
                            mmw = [("dg", WOUT_DG), ("dz", ATT_DZ),
                                   ("v", ATT_V), ("s", S_ATT)]
                    else:
                        mmw = []
                    P.op("tensor",
                         mm(ps[:, 0:512],
                            zfm[:, c * S + i * 128: c * S + (i + 1) * 128],
                            woutb[:, c * D + half * 512: c * D + half * 512 + 512],
                            start=(c == 0), stop=(c == NCH - 1)),
                         wait=mmw)
                tw = P.bump("tensor", "t")
                P.op("vector",
                     sttf(attn_t[:, half * 512: half * 512 + 512], ps[:, 0:512],
                          1.0, boutbc[:, half * 512: half * 512 + 512],
                          ALU.mult, ALU.add),
                     wait=[("t", tw)])
                op_evacv[half] = P.bump("vector", "v")
            sw, _ = walk_ln_stats_apply(attn_t, 0, None, [])
            P.op("vector", ts(attn_t[:], attn_t[:], stat[:, 115:116], stat[:, 119:120],
                              ALU.add, ALU.mult), wait=[("s", sw)])
            P.op("vector", tt(x_tm[:, i * D:(i + 1) * D],
                              x_tm[:, i * D:(i + 1) * D], attn_t[:], ALU.add))
            P.bump("vector", "v")
        OUTPROJ_V = P.now("v")
        close_scope("outp")
        close_scope("zscope")

        if debug == "x2":
            for i in range(NT):
                P.op("sync", _ld(bass.AP(xout, i * 128 * D, [[D, 128], [1, D]]),
                                 x_tm[:, i * D:(i + 1) * D]),
                     wait=[("v", OUTPROJ_V)], inc=[("dout", 16)])
            P.op("sync", None, wait=[("dout", P.now("dout"))])
            debug = "STOP"

        if debug == "STOP":
            close_all_scopes()
            return _emit_and_finish(nc, P, sems)
        lt2 = open_scope("lntm2", [("ln_tm", [128, NT * D], BF16)])
        ps_tr = scope["lntm2"].enter_context(nc.psum_tensor("ps_tr2", [128, 512], BF16))
        T_OUTP = P.now("t")
        LN2_S = walk_ln_phase(lt2["ln_tm"], ps_tr,
                              lambda i: [("v", OUTPROJ_V), ("t", T_OUTP)])
        close_scope("lntm2")
        if debug == "ln2":
            P.op("gpsimd", _ld(bass.AP(xout, 0, [[16384, 128], [1, 16384]]),
                               ln_fm[:]),
                 wait=[("s", LN2_S)], inc=[("dout", 16)])
            P.op("sync", None, wait=[("dout", P.now("dout"))])
            close_all_scopes()
            return _emit_and_finish(nc, P, sems)
        fbufs = open_scope("ffn", [("hq8", [128, NHF * 1024], FP8),
                                   ("w1str", [128, 2 * 1024], BF16),
                                   ("w2qs", [128, 2 * D], FP8),
                                   ("xr_sb", [128, 1024], F32),
                                   ("ffo", [128, 2 * D], F32),
                                   ("ffq", [128, D], F16)])
        hq8, w1str, w2qs, xr_sb, ffo, ffq = (
            fbufs[n] for n in ("hq8", "w1str", "w2qs", "xr_sb", "ffo", "ffq"))
        cb2 = open_scope("chain2", [("wabs", [128, D], F32), ("wsgn", [128, D], BF16),
                                    ("wacc", [128, D], F32), ("wis", [128, D], F32),
                                    ("wsc", [128, D], F32), ("wlo", [128, D], F32)])
        wabs, wsgn, wacc, wis, wsc, wlo = (cb2[n] for n in
                                           ("wabs", "wsgn", "wacc", "wis", "wsc", "wlo"))

        first_w2_load = [True]
        hpend = [None]
        ffn2_evac = [0]
        ffn1_evacv = {}
        w2_tc = []
        for sg in range(2):
            ffn1_evacv = {}
            col0 = sg * 1024
            for m in range(NHF):
                slot = (m % 2) * 1024
                P.op("gpsimd",
                     _ld(bass.AP(w1str, slot, [[2 * 1024, 128], [128, NCH], [1, 128]]),
                         bass.AP(w1t, m * NCH * 128 * 128,
                                 [[128, 128], [128 * 128, NCH], [1, 128]])),
                     wait=[("t", P.now("t"))] if (sg > 0 or m >= 2)
                     else [("t", P.now("t")), ("s", LN2_S)],
                     inc=[("dg", 16)])
                dgm = P.now("dg")
                for g2 in range(2):
                    ps = ps_a if g2 == 0 else ps_b
                    for c in range(NCH):
                        P.op("tensor",
                             mm(ps[:, 0:512],
                                w1str[:, slot + c * 128: slot + (c + 1) * 128],
                                ln_fm[:, c * S + col0 + g2 * 512:
                                      c * S + col0 + g2 * 512 + 512],
                                start=(c == 0), stop=(c == NCH - 1)),
                             wait=([("dg", dgm),
                                    ("v", ffn1_evacv.get(g2, ffn2_evac[0]))]
                                   if c == 0 else []))
                    tw = P.bump("tensor", "t")
                    hk = m * 2 + g2
                    o = (hk % 2) * 512
                    hb = (wabs[:, o:o + 512], wsgn[:, o:o + 512],
                          wacc[:, o:o + 512], wis[:, o:o + 512],
                          wsc[:, o:o + 512], wlo[:, o:o + 512], f"h{hk % 2}")
                    P.op("vector", ts(xr_sb[:, o:o + 512], ps[:, 0:512],
                                      b1_sb[:, m:m + 1], 0.0, ALU.add, ALU.max),
                         wait=[("t", tw), ("v", qlastb.get(f"h{hk % 2}", 0))])
                    ffn1_evacv[g2] = P.bump("vector", "v")
                    fin = walk_quant2(
                        hb, xr_sb[:, o:o + 512], 512, False, False, [],
                        [hq8[:, m * 1024 + g2 * 512: m * 1024 + g2 * 512 + 512]],
                        ext=True)
                    if hpend[0] is not None:
                        hpend[0]()
                    hpend[0] = fin
            if hpend[0] is not None:
                hpend[0]()
                hpend[0] = None
            HQ_V = P.now("v")
            if debug and debug.startswith("ffacc") and sg == 0:
                nacc = int(debug[5:])
                for c in range(nacc):
                    slot = (c % 2) * D
                    P.op("sync", _ld(w2qs[:, slot:slot + D],
                                     bass.AP(w2q_dram, c * 128 * D,
                                             [[D, 128], [1, D]])),
                         wait=[("v", HQ_V), ("dwr", W2_DWR),
                               ("t", P.now("t"))], inc=[("din", 16)])
                    P.op("tensor", mm(ps_a[:, 0:512], hq8[:, c * 1024: c * 1024 + 128],
                                      w2qs[:, slot:slot + 512],
                                      start=(c == 0), stop=(c == nacc - 1)),
                         wait=[("din", P.now("din")), ("v", HQ_V),
                               ("s", P.now("s"))])
                    P.bump("tensor", "t")
                P.op("scalar", cpy(ffo[:, 0:512], ps_a[:, 0:512]),
                     wait=[("t", P.now("t"))])
                s1 = P.bump("scalar", "s")
                P.op("sync", _ld(bass.AP(xout, 0, [[D, 128], [1, 512]]),
                                 ffo[:, 0:512]),
                     wait=[("s", s1)], inc=[("dout", 16)])
                P.op("sync", None, wait=[("dout", P.now("dout"))])
                close_all_scopes()
                return _emit_and_finish(nc, P, sems)
            if debug in ("hqa", "hqb") and sg == 0:
                hofs = 0 if debug == "hqa" else 16384
                P.op("gpsimd", _ld(bass.AP(xout, 0, [[16384, 128], [1, 16384]]),
                                   hq8[:, hofs:hofs + 16384]),
                     wait=[("v", HQ_V)], inc=[("dout", 16)])
                P.op("sync", None, wait=[("dout", P.now("dout"))])
                close_all_scopes()
                return _emit_and_finish(nc, P, sems)
            for stp in range(4):
                sts = (col0 // 128) + stp * 2
                psl = (ps_a, ps_b, ps_z, ps_r)
                for c in range(NHF):
                    slot = (c % 2) * D
                    lw = [("v", HQ_V)]
                    if first_w2_load[0]:
                        lw.append(("dwr", W2_DWR))
                        first_w2_load[0] = False
                    if len(w2_tc) >= 2:
                        lw.append(("t", w2_tc[-2]))
                    P.op("sync", _ld(w2qs[:, slot:slot + D],
                                     bass.AP(w2q_dram, c * 128 * D, [[D, 128], [1, D]])),
                         wait=lw, inc=[("din", 16)])
                    dinc = P.now("din")
                    if debug == "ffo1" and sg == 0 and stp == 0 and c == 0:
                        P.op("gpsimd", _ld(bass.AP(xout, 128 * D, [[D, 128], [1, D]]),
                                           w2qs[:, 0:D]),
                             wait=[("din", dinc)], inc=[("dg", 16)])
                        P.op("gpsimd", _ld(bass.AP(xout, 256 * D, [[D, 128], [1, 128]]),
                                           hq8[:, 0:128]),
                             wait=[("v", HQ_V)], inc=[("dg", 16)])
                        P.op("tensor", mm(ps_a[:, 0:512], hq8[:, 0:128],
                                          w2qs[:, 0:512], start=True, stop=True),
                             wait=[("din", dinc), ("v", HQ_V), ("s", P.now("s"))])
                        t1 = P.bump("tensor", "t")
                        P.op("scalar", cpy(ffo[:, 0:512], ps_a[:, 0:512]),
                             wait=[("t", t1)])
                        s1 = P.bump("scalar", "s")
                        P.op("sync", _ld(bass.AP(xout, 384 * D, [[D, 128], [1, 512]]),
                                         ffo[:, 0:512]),
                             wait=[("s", s1)], inc=[("dout", 16)])
                        P.op("sync", None, wait=[("dout", P.now("dout")),
                                                 ("dg", P.now("dg"))])
                        close_all_scopes()
                        return _emit_and_finish(nc, P, sems)
                    for sti in range(2):
                        for half in range(2):
                            ps = psl[sti * 2 + half]
                            P.op("tensor",
                                 mm(ps[:, 0:512],
                                    hq8[:, c * 1024 + (stp * 2 + sti) * 128:
                                        c * 1024 + (stp * 2 + sti + 1) * 128],
                                    w2qs[:, slot + half * 512: slot + half * 512 + 512],
                                    start=(c == 0), stop=(c == NHF - 1)),
                                 wait=([("din", dinc)]
                                       + ([("v", ffn2_evac[0])]
                                          if c == 0 else []))
                                 if (sti == 0 and half == 0) or c == 0 else [])
                    w2_tc.append(P.bump("tensor", "t"))
                tws = w2_tc[-1]
                if debug == "ffo2" and sg == 0 and stp == 0:
                    P.op("scalar", cpy(ffo[:, 0:512], ps_a[:, 0:512]),
                         wait=[("t", tws)])
                    s1 = P.bump("scalar", "s")
                    P.op("sync", _ld(bass.AP(xout, 0, [[D, 128], [1, 512]]),
                                     ffo[:, 0:512]),
                         wait=[("s", s1)], inc=[("dout", 16)])
                    P.op("sync", None, wait=[("dout", P.now("dout"))])
                    close_all_scopes()
                    return _emit_and_finish(nc, P, sems)
                for sti in range(2):
                    for half in range(2):
                        ps = psl[sti * 2 + half]
                        P.op("vector",
                             sttf(ffo[:, sti * D + half * 512:
                                      sti * D + half * 512 + 512], ps[:, 0:512],
                                  1.0, bqbc[:, half * 512: half * 512 + 512],
                                  ALU.mult, ALU.add),
                             wait=[("t", tws)] if (sti == 0 and half == 0) else ())
                ffn2_evac[0] = P.bump("vector", "v")
                vffo = ffn2_evac[0]
                for sti in range(2):
                    i = sts + sti
                    fslc = ffo[:, sti * D:(sti + 1) * D]
                    if debug == "ffo0" and sti == 0:
                        P.op("sync", _ld(bass.AP(xout, 0, [[D, 128], [1, D]]),
                                         fslc),
                             wait=[("v", vffo)], inc=[("dout", 16)])
                        P.op("sync", None, wait=[("dout", P.now("dout"))])
                        close_all_scopes()
                        return _emit_and_finish(nc, P, sems)
                    qv = walk_quant(fslc, D, True, True, [("v", vffo)],
                                    [ffq[:, 0:512], ffq[:, 512:1024]], he=True)
                    if debug == "ffx" and i == 5:
                        P.op("sync", _ld(bass.AP(xout, 0, [[D, 128], [1, D]]),
                                         fslc),
                             wait=[("v", qv)], inc=[("dout", 16)])
                        P.op("gpsimd", _ld(bass.AP(xout, 128 * D, [[D, 128], [1, D]]),
                                           ffq[:]),
                             wait=[("v", qv)], inc=[("dg", 16)])
                        P.op("sync", None, wait=[("dout", P.now("dout")),
                                                 ("dg", P.now("dg"))])
                        close_all_scopes()
                        return _emit_and_finish(nc, P, sems)
                    sw, _ = walk_ln_stats_apply(ffq, 0, None, [])
                    P.op("vector", ts(fslc, ffq[:], stat[:, 115:116], stat[:, 119:120],
                                      ALU.add, ALU.mult), wait=[("s", sw)])
                    P.op("vector", tt(x_tm[:, i * D:(i + 1) * D],
                                      x_tm[:, i * D:(i + 1) * D], fslc, ALU.add))
                    vres = P.bump("vector", "v")
                    P.op("sync", _ld(bass.AP(xout, i * 128 * D, [[D, 128], [1, D]]),
                                     x_tm[:, i * D:(i + 1) * D]),
                         wait=[("v", vres)], inc=[("dout", 16)])

        P.op("sync", None, wait=[("dout", P.now("dout"))])
        close_scope("chain2")
        close_scope("ffn")

        return _emit_and_finish(nc, P, sems)


def _emit_and_finish(nc, P, sems):
    with nc.Block() as blk:
        def make_body(eng_name):
            def body(eng):
                for waits, emit, incs in P.ops[eng_name]:
                    for sname, val in waits:
                        eng.wait_ge(sems[sname], val)
                    if emit is None:
                        assert not incs
                        continue
                    inst = emit(eng)
                    for sname, amt in incs:
                        inst.then_inc(sems[sname], amt)
            return body
        blk.sync(make_body("sync"))
        blk.tensor(make_body("tensor"))
        blk.vector(make_body("vector"))
        blk.scalar(make_body("scalar"))
        blk.gpsimd(make_body("gpsimd"))
    mybir.codegen_inst_isa_subclasses(nc)
    return nc


def _host_inputs(L0_attn_resid, L0_attn_W_Q, L0_attn_W_K, L0_attn_W_V,
                 L0_attn_W_out, L0_attn_b_out, L0_ffn_w1, L0_ffn_b1,
                 L0_ffn_w2, L0_ffn_b2):
    bf = ml_dtypes.bfloat16
    x = np.ascontiguousarray(np.asarray(L0_attn_resid, dtype=np.float32))
    wq = np.asarray(L0_attn_W_Q, dtype=np.float32).reshape(D, NH * DH)
    wk = np.asarray(L0_attn_W_K, dtype=np.float32).reshape(D, NH * DH)
    wv = np.asarray(L0_attn_W_V, dtype=np.float32).reshape(D, NH * DH)
    wqp = np.ascontiguousarray(wq.reshape(D, NP, 128).transpose(1, 0, 2))
    wkp = np.ascontiguousarray(wk.reshape(D, NP, 128).transpose(1, 0, 2))
    wvp = np.ascontiguousarray(wv.reshape(D, NP, 128).transpose(1, 0, 2))
    wo = np.asarray(L0_attn_W_out, dtype=np.float32).transpose(1, 0, 2).reshape(NH * DH, D)
    wop = np.ascontiguousarray(wo.reshape(NCH, 128, D))
    w1 = np.asarray(L0_ffn_w1, dtype=np.float32)
    w1t = np.ascontiguousarray(w1.reshape(NHF, 128, NCH, 128).transpose(0, 2, 3, 1))
    w2 = np.asarray(L0_ffn_w2, dtype=np.float32)
    w2t = np.ascontiguousarray(w2.T.reshape(NHF, 128, D))
    b1 = np.asarray(L0_ffn_b1, dtype=np.float32)
    b1in = np.ascontiguousarray(b1.reshape(NHF, 128).T)
    boutbc = np.ascontiguousarray(
        np.broadcast_to(np.asarray(L0_attn_b_out, dtype=np.float32), (128, D)))
    b2bc = np.ascontiguousarray(
        np.broadcast_to(np.asarray(L0_ffn_b2, dtype=np.float32), (128, D)))
    cstv = np.zeros((128, NCST), dtype=np.float32)
    for k, v in CST.items():
        cstv[:, CSTI[k]] = v
    identv = np.eye(128, dtype=np.float32).astype(bf)
    tri = np.triu(np.ones((128, 128), dtype=np.float32)).astype(bf)
    bc64 = np.zeros((128, 64), dtype=np.float32)
    bc64[0, :] = 1.0
    bc64 = bc64.astype(bf)
    common = dict(wqp=wqp, wkp=wkp, wvp=wvp, wop=wop, w1t=w1t, w2t=w2t,
                  b1in=b1in, boutbc_in=boutbc, b2bc_in=b2bc, cst_in=cstv,
                  ident_in=identv, trimask_in=tri, bcast64_in=bc64)
    in_maps = []
    for b in range(8):
        m = dict(common)
        m["xin"] = np.ascontiguousarray(x[b])
        in_maps.append(m)
    return in_maps


def kernel(L0_attn_resid, L0_attn_W_Q, L0_attn_W_K, L0_attn_W_V, L0_attn_W_out,
           L0_attn_b_out, L0_ffn_w1, L0_ffn_b1, L0_ffn_w2, L0_ffn_b2):
    in_maps = _host_inputs(L0_attn_resid, L0_attn_W_Q, L0_attn_W_K, L0_attn_W_V,
                           L0_attn_W_out, L0_attn_b_out, L0_ffn_w1, L0_ffn_b1,
                           L0_ffn_w2, L0_ffn_b2)
    nc = build_nc()
    res = run_bass_kernel_spmd(nc, in_maps, core_ids=list(range(8)))
    out = np.stack([np.asarray(r["xout"]) for r in res.results], axis=0)
    return out


# version 08
# speedup vs baseline: 1.0908x; 1.0293x over previous
import contextlib
import numpy as np
import ml_dtypes
import concourse.bass as bass
import concourse.mybir as mybir
from concourse.bass_utils import run_bass_kernel_spmd
from concourse.dve_op import DveOp
from concourse.dve_spec import Spec, Src0, Src1, lower, C0, C1, C2, eq, select
import concourse.dve_ops as dve_ops

F32 = mybir.dt.float32
F16 = mybir.dt.float16
BF16 = mybir.dt.bfloat16
FP8 = mybir.dt.float8e4
AF = mybir.ActivationFunctionType
ALU = mybir.AluOpType
AX = mybir.AxisListType
LN2 = 0.6931471805599453

S = 2048
D = 1024
NT = 16
NCH = 8
NH = 16
NP = 8
DH = 64
HF = 4096
NHF = 32
GQ = 512
NG = S // GQ
USOFT = 12.0
EPS = 1e-5

_OPS = {}


def _fdt(x):
    return np.float64 if isinstance(x, np.ndarray) and x.dtype == np.float64 else np.float32


def _reg(name, spec_body, reference, row):
    spec = Spec(body=spec_body)
    sha = lower(spec, ver="v3", perf_en=False, name=name).sha("v3")
    op = DveOp(name, spec, subdim=False, uops_sha={"v3": sha}, reference=reference)
    if not any(o.name == name for o in dve_ops.OPS):
        dve_ops.OPS.append(op)
        dve_ops._SUB_OPCODE_FOR_NAME[name] = row
        dve_ops.CUSTOM_DVE_SPECS[name] = spec
        dve_ops.CUSTOM_DVE_REFS[name] = op.ref
    return op


def _register_ops():
    if _OPS:
        return
    b4 = ((Src0 >= C0) + (Src0 >= C1)) + ((Src0 >= C2) + (Src0 >= C0 * C2))

    def r4(in0, in1, s0, s1, imm2):
        t = _fdt(in0); x = in0.astype(t)
        return ((x >= s0).astype(t) + (x >= s1).astype(t)
                + (x >= imm2).astype(t) + (x >= s0 * imm2).astype(t))

    def r4b(in0, in1, s0, s1, imm2):
        return r4(in0, in1, s0, s1, imm2) + in1.astype(_fdt(in0))

    def r1b(in0, in1, s0, s1, imm2):
        t = _fdt(in0); x = in0.astype(t)
        return (x >= s0).astype(t) + in1.astype(t)

    t = Src0 * Src1 + C0
    t = t - (t >= C1) * C1
    t = t - (t >= C2) * C2

    def rmp2(in0, in1, s0, s1, imm2):
        tt = _fdt(in0)
        x = in0.astype(tt) * in1.astype(tt) + s0
        x = x - (x >= s1).astype(tt) * s1
        return x - (x >= imm2).astype(tt) * imm2

    p = Src0 + C0
    p = p - (p >= C1) * C1
    p = p - (p >= C2) * C2

    def rp2(in0, in1, s0, s1, imm2):
        tt = _fdt(in0)
        x = in0.astype(tt) + s0
        x = x - (x >= s1).astype(tt) * s1
        return x - (x >= imm2).astype(tt) * imm2

    q = Src0
    q = q - (q >= C0) * C0
    qw = (q - C1) * Src1

    def rpw(in0, in1, s0, s1, imm2):
        tt = _fdt(in0)
        x = in0.astype(tt)
        x = x - (x >= s0).astype(tt) * s0
        return (x - s1) * in1.astype(tt)

    qe = Src0
    qe = qe - (qe >= C0) * C0
    qwe = (qe - C1 + eq(Src0, C0)) * Src1

    def rpwe(in0, in1, s0, s1, imm2):
        tt = _fdt(in0)
        x = in0.astype(tt)
        r = x - (x >= s0).astype(tt) * s0
        return (r - s1 + (x == s0).astype(tt)) * in1.astype(tt)

    b0t = Src0 >= C0
    qwt = (((Src0 - b0t * C0) - C1) + (b0t - (Src0 >= C2))) * Src1

    def rpwt(in0, in1, s0, s1, imm2):
        tt = _fdt(in0)
        x = in0.astype(tt)
        r = x - (x >= s0).astype(tt) * s0
        return (r - s1 + (x >= s0).astype(tt) * (x < imm2).astype(tt)) \
            * in1.astype(tt)

    r_ = Src0
    r_ = r_ - (r_ >= C0) * C0
    r_ = r_ - (r_ >= C2) * C2
    rw = (r_ - C1) * Src1

    def rp2w(in0, in1, s0, s1, imm2):
        tt = _fdt(in0)
        x = in0.astype(tt)
        x = x - (x >= s0).astype(tt) * s0
        x = x - (x >= imm2).astype(tt) * imm2
        return (x - s1) * in1.astype(tt)

    _OPS["LAD4"] = _reg("QLAD4", b4, r4, 30)
    _OPS["LAD4B"] = _reg("QLAD4B", b4 + Src1, r4b, 31)
    _OPS["LAD1B"] = _reg("QLAD1B", (Src0 >= C0) + Src1, r1b, 1)
    _OPS["MULPEEL2"] = _reg("QMULPEEL2", t, rmp2, 2)
    _OPS["PEEL2"] = _reg("QPEEL2", p, rp2, 3)
    _OPS["PEELW"] = _reg("QPEELW", qw, rpw, 4)
    _OPS["PEEL2W"] = _reg("QPEEL2W", rw, rp2w, 5)
    _OPS["PEELWE"] = _reg("QPEELWE", qwe, rpwe, 6)

    hv1 = select(Src0 >= C0, Src1 * C2, Src1)
    hv2 = select(Src0 >= C1, hv1 * C2, hv1)

    def rhalve2(in0, in1, s0, s1, imm2):
        tt = _fdt(in0)
        x = in0.astype(tt)
        t = np.where(x >= s0, in1.astype(tt) * imm2, in1.astype(tt))
        return np.where(x >= s1, t * imm2, t)

    sp = Src0 * C0 + C1
    sp = sp - (sp >= C2) * C2

    def rscpeel(in0, in1, s0, s1, imm2):
        tt = _fdt(in0)
        u = in0.astype(tt) * s0 + s1
        return u - (u >= imm2).astype(tt) * imm2

    _OPS["HALVE2"] = _reg("QHALVE2", hv2, rhalve2, 7)
    _OPS["SCPEEL"] = _reg("QSCPEEL", sp, rscpeel, 8)
    _OPS["PEELWET"] = _reg("QPEELWET", qwt, rpwt, 9)

    pc = Src0 + C0
    pc = pc - (pc >= C1) * C1
    pc = pc - (pc >= C2) * C1

    def rp2c(in0, in1, s0, s1, imm2):
        tt = _fdt(in0)
        x = in0.astype(tt) + s0
        x = x - (x >= s1).astype(tt) * s1
        return x - (x >= imm2).astype(tt) * s1

    _OPS["PEEL2C"] = _reg("QPEEL2C", pc, rp2c, 10)


_register_ops()

_HI = float(np.float32(LN2))
_LO = float(np.float32(LN2 - _HI))
_B1_8 = float(np.float32(9 * _HI))
_B2_8 = float(np.float32(9 * LN2 - _B1_8))
_B1_16 = float(np.float32(24 * _HI))
_B2_16 = float(np.float32(24 * LN2 - _B1_16))
CST = {
    "is8": 9 * LN2, "sc8": -9 * LN2, "is16": 24 * LN2, "sc16": -24 * LN2,
    "pln2": LN2, "nln2": -LN2, "eighth": 0.125, "negu": -USOFT, "eps": EPS,
    "neghalf": -0.5, "one": 1.0,
    "nhi": -_HI, "phi": _HI, "nlo": -_LO, "plo": _LO,
    "i8h": _B1_8, "i8l": _B2_8, "s8h": -_B1_8, "s8l": -_B2_8,
    "i16h": _B1_16, "i16l": _B2_16, "s16h": -_B1_16, "s16l": -_B2_16,
}
CSTI = {k: i for i, k in enumerate(CST)}
NCST = len(CST)

ENGINES = ("sync", "tensor", "vector", "scalar", "gpsimd")
SEMS = ("din", "dg", "dz", "dwr", "dout", "t", "v", "s", "g")


class Prog:

    def __init__(self):
        self.counts = {sn: 0 for sn in SEMS}
        self.ops = {e: [] for e in ENGINES}
        self.known = {e: {} for e in ENGINES}

    def now(self, name):
        return self.counts[name]

    def op(self, eng, emit, wait=(), inc=()):
        w = []
        for sname, val in wait:
            if val <= 0:
                continue
            if self.known[eng].get(sname, 0) >= val:
                continue
            self.known[eng][sname] = val
            w.append((sname, val))
        incs = []
        for sname, amt in inc:
            self.counts[sname] += amt
            incs.append((sname, amt))
            self.known[eng][sname] = max(self.known[eng].get(sname, 0),
                                         self.counts[sname])
        self.ops[eng].append([w, emit, incs])

    def bump(self, eng, sname, amt=1):
        self.counts[sname] += amt
        self.ops[eng][-1][2].append((sname, amt))
        self.known[eng][sname] = max(self.known[eng].get(sname, 0),
                                     self.counts[sname])
        return self.counts[sname]


def build_nc(debug=None):
    nc = bass.Bass("TRN2", target_bir_lowering=False, detect_race_conditions=False)

    xin = nc.dram_tensor("xin", [S, D], F32, kind="ExternalInput")
    wqp = nc.dram_tensor("wqp", [NP, D, 128], F32, kind="ExternalInput")
    wkp = nc.dram_tensor("wkp", [NP, D, 128], F32, kind="ExternalInput")
    wvp = nc.dram_tensor("wvp", [NP, D, 128], F32, kind="ExternalInput")
    wop = nc.dram_tensor("wop", [NCH, 128, D], F32, kind="ExternalInput")
    w1t = nc.dram_tensor("w1t", [NHF, NCH, 128, 128], F32, kind="ExternalInput")
    w2t = nc.dram_tensor("w2t", [NHF, 128, D], F32, kind="ExternalInput")
    b1in = nc.dram_tensor("b1in", [128, NHF], F32, kind="ExternalInput")
    boutbc_in = nc.dram_tensor("boutbc_in", [128, D], F32, kind="ExternalInput")
    b2bc_in = nc.dram_tensor("b2bc_in", [128, D], F32, kind="ExternalInput")
    cst_in = nc.dram_tensor("cst_in", [128, NCST], F32, kind="ExternalInput")
    ident_in = nc.dram_tensor("ident_in", [128, 128], BF16, kind="ExternalInput")
    trimask_in = nc.dram_tensor("trimask_in", [128, 128], BF16, kind="ExternalInput")
    bcast64_in = nc.dram_tensor("bcast64_in", [128, 64], BF16, kind="ExternalInput")
    xout = nc.dram_tensor("xout", [S, D], F32, kind="ExternalOutput")
    w2q_dram = nc.dram_tensor("w2q_dram", [NHF, 128, D], FP8, kind="Internal")

    P = Prog()

    with contextlib.ExitStack() as est:
        E = est.enter_context
        sems = {sn: E(nc.semaphore(sn)) for sn in SEMS}

        x_tm = E(nc.sbuf_tensor("x_tm", [128, NT * D], F32))
        ln_fm = E(nc.sbuf_tensor("ln_fm", [128, NCH * S], BF16))
        cstb = E(nc.sbuf_tensor("cstb", [128, NCST], F32))
        ident = E(nc.sbuf_tensor("ident", [128, 128], BF16))
        trimask = E(nc.sbuf_tensor("trimask", [128, 128], BF16))
        bcast64 = E(nc.sbuf_tensor("bcast64", [128, 64], BF16))
        b1_sb = E(nc.sbuf_tensor("b1_sb", [128, NHF], F32))
        boutbc = E(nc.sbuf_tensor("boutbc", [128, D], F32))

        bqbc = E(nc.sbuf_tensor("bqbc", [128, D], F16))
        stat = E(nc.sbuf_tensor("stat", [128, 128], F32))
        scratch = E(nc.sbuf_tensor("scratch", [128, 2048], F32))
        sq = scratch[:, 0:1024]

        scope = {}

        def open_scope(name, specs):
            stk = contextlib.ExitStack()
            scope[name] = stk
            return {nm: stk.enter_context(nc.sbuf_tensor(f"{name}_{nm}", sh, dt))
                    for nm, sh, dt in specs}

        def close_scope(name):
            scope.pop(name).close()

        def close_all_scopes():
            for name in reversed(list(scope)):
                scope.pop(name).close()

        ps_a = E(nc.psum_tensor("ps_a", [128, 512], F32))
        ps_b = E(nc.psum_tensor("ps_b", [128, 512], F32))
        ps_z = E(nc.psum_tensor("ps_z", [128, 512], F32))
        ps_r = E(nc.psum_tensor("ps_r", [128, 512], F32))

        def cc(name, lo=0, hi=128):
            return cstb[lo:hi, CSTI[name]:CSTI[name] + 1]

        def _ld(dst_ap, src_ap):
            return lambda e, d=dst_ap, s=src_ap: e.dma_start(d, s)

        def _ms(ap, val):
            return lambda e, a=ap, vv=val: e.memset(a, vv)

        def ts(out, in0, s1, s2, op0, op1=None):
            if op1 is None:
                return lambda e: e.tensor_scalar(out, in0, s1, None, op0)
            return lambda e: e.tensor_scalar(out, in0, s1, s2, op0, op1)

        def tt(out, in0, in1, op):
            return lambda e: e.tensor_tensor(out, in0, in1, op)

        def sttf(out, in0, sc, in1, op0, op1, accum=None):
            return lambda e: e.scalar_tensor_tensor(out, in0, sc, in1, op0, op1,
                                                    accum_out=accum)

        def act(out, in_, func, bias=0.0, scale=1.0):
            return lambda e: e.activation(out, in_, func, bias=bias, scale=scale)

        def cdve(opn, out, in0, in1=None, **kw):
            if in1 is None:
                return lambda e: e._custom_dve(_OPS[opn], out=out, in0=in0, **kw)
            return lambda e: e._custom_dve(_OPS[opn], out=out, in0=in0, in1=in1, **kw)

        def mm(out, lhsT, rhs, start, stop):
            return lambda e: e.matmul(out, lhsT, rhs, start=start, stop=stop)

        def tpz(out, in_):
            return lambda e: e.transpose(out, in_, ident[:])

        def red(out, in_, op):
            return lambda e: e.tensor_reduce(out, in_, AX.X, op)

        def cpy(out, in_):
            return lambda e: (e.tensor_copy(out, in_) if hasattr(e, "tensor_copy")
                              else e.copy(out, in_))

        P.op("sync", _ld(bass.AP(x_tm, 0, [[NT * D, 128], [D, NT], [1, D]]),
                         bass.AP(xin, 0, [[D, 128], [128 * D, NT], [1, D]])),
             inc=[("din", 16)])
        for src, dst in ((cst_in, cstb), (b1in, b1_sb), (boutbc_in, boutbc),
                         (ident_in, ident), (trimask_in, trimask),
                         (bcast64_in, bcast64)):
            P.op("sync", _ld(dst[:], src[:]), inc=[("din", 16)])
        PRO_DIN = P.now("din")

        if debug == "x0":
            for i in range(NT):
                P.op("sync", _ld(bass.AP(xout, i * 128 * D, [[D, 128], [1, D]]),
                                 x_tm[:, i * D:(i + 1) * D]),
                     wait=[("din", PRO_DIN)], inc=[("dout", 16)])
            P.op("sync", None, wait=[("dout", P.now("dout"))])
            return _emit_and_finish(nc, P, sems)

        qlast = {"v": 0}
        lnlast = {"s": 0}

        L16 = [("LAD4", dict(s0=2.0 ** -13, s1=2.0 ** -12, imm2=8.0)),
               ("LAD4B", dict(s0=2.0 ** -11, s1=2.0 ** -8, imm2=4.0)),
               ("LAD4B", dict(s0=2.0 ** -7, s1=2.0 ** -4, imm2=2.0)),
               ("LAD4B", dict(s0=2.0 ** -2, s1=2.0 ** -5, imm2=2.0 ** -1)),
               ("LAD1B", dict(s0=1.0))]
        L8 = [("LAD4", dict(s0=2.0 ** -5, s1=2.0 ** -4, imm2=4.0)),
              ("LAD4B", dict(s0=2.0 ** -2, s1=1.0, imm2=2.0))]
        L8X = L8 + [("LAD4B", dict(s0=8.0, s1=16.0, imm2=32.0))]

        ETA = 2.0 ** -22
        CATCH = 2.0 - ETA

        qlastb = {}

        def walk_quant2(bufs, x_ap, n, man16, signed, xwaits, out_slices,
                        ext=False, he=False):
            wabs_, wsgn_, wacc_, wis_, wsc_, wlo_, key = bufs
            lads = L16 if man16 else (L8X if ext else L8)
            ih, il = ("i16h", "i16l") if man16 else ("i8h", "i8l")
            sh, sl_ = ("s16h", "s16l") if man16 else ("s8h", "s8l")
            guard = [("v", qlastb.get(key, 0))]
            if signed:
                P.op("scalar", act(wabs_, x_ap, AF.Abs),
                     wait=list(xwaits) + guard)
                P.op("scalar", act(wsgn_, x_ap, AF.Sign))
                sg = P.bump("scalar", "s")
                ax_ap = wabs_
                lw = [("s", sg)]
            else:
                ax_ap = x_ap
                lw = list(xwaits) + guard
            for i, (opn, kw) in enumerate(lads):
                if i == 0:
                    P.op("vector", cdve(opn, wacc_, ax_ap, **kw), wait=lw)
                else:
                    P.op("vector", cdve(opn, wacc_, ax_ap, wacc_, **kw))
            vw = P.bump("vector", "v")
            P.op("scalar", act(wis_, wacc_, AF.Exp,
                               bias=cc(ih), scale=cc("nhi")), wait=[("v", vw)])
            P.op("scalar", act(wlo_, wacc_, AF.Exp,
                               bias=cc(il), scale=cc("nlo")))
            P.op("scalar", act(wsc_, wacc_, AF.Exp,
                               bias=cc(sh), scale=cc("phi")))
            P.op("scalar", act(wacc_, wacc_, AF.Exp,
                               bias=cc(sl_), scale=cc("plo")))
            sw = P.bump("scalar", "s")
            P.op("gpsimd", tt(wis_, wis_, wlo_, ALU.mult), wait=[("s", sw)])
            P.op("gpsimd", tt(wsc_, wsc_, wacc_, ALU.mult))
            gw = P.bump("gpsimd", "g")

            def finish():
                nh = (n + 511) // 512
                for hh in range(nh):
                    lo, hi = hh * 512, min(n, hh * 512 + 512)
                    m = hi - lo
                    sl = slice(lo, hi)
                    w = [("g", gw)] if hh == 0 else ()
                    if man16:
                        P.op("vector", cdve("MULPEEL2", scratch[:, 0:m], ax_ap[:, sl],
                                            wis_[:, sl], s0=0.5, s1=2048.0, imm2=1024.0),
                             wait=w)
                        P.op("vector", cdve("PEEL2", scratch[:, 512:512 + m], scratch[:, 0:m],
                                            s0=0.0, s1=512.0, imm2=256.0))
                        P.op("vector", cdve("PEEL2", scratch[:, 0:m], scratch[:, 512:512 + m],
                                            s0=0.0, s1=128.0, imm2=64.0))
                        P.op("vector", cdve("PEEL2", scratch[:, 512:512 + m], scratch[:, 0:m],
                                            s0=0.0, s1=32.0, imm2=16.0))
                        P.op("vector", cdve("PEEL2", scratch[:, 0:m], scratch[:, 512:512 + m],
                                            s0=0.0, s1=8.0, imm2=4.0))
                        if he:
                            P.op("vector", cdve("PEEL2C", scratch[:, 512:512 + m],
                                                scratch[:, 0:m], s0=0.0, s1=2.0,
                                                imm2=CATCH))
                            P.op("vector", cdve("PEELWET", scratch[:, 1024:1024 + m],
                                                scratch[:, 512:512 + m], wsc_[:, sl],
                                                s0=1.0, s1=0.5, imm2=1.0 + ETA))
                        else:
                            P.op("vector", cdve("PEEL2W", scratch[:, 1024:1024 + m],
                                                scratch[:, 0:m],
                                                wsc_[:, sl], s0=2.0, s1=0.5, imm2=1.0))
                    else:
                        P.op("vector", cdve("MULPEEL2", scratch[:, 0:m], ax_ap[:, sl],
                                            wis_[:, sl], s0=0.5, s1=16.0, imm2=8.0),
                             wait=w)
                        P.op("vector", cdve("PEEL2", scratch[:, 512:512 + m], scratch[:, 0:m],
                                            s0=0.0, s1=4.0, imm2=2.0))
                        if he:
                            P.op("vector", cdve("PEEL2C", scratch[:, 0:m],
                                                scratch[:, 512:512 + m], s0=0.0, s1=2.0,
                                                imm2=CATCH))
                            P.op("vector", cdve("PEELWET", scratch[:, 1024:1024 + m],
                                                scratch[:, 0:m], wsc_[:, sl],
                                                s0=1.0, s1=0.5, imm2=1.0 + ETA))
                        else:
                            P.op("vector", cdve("PEELW", scratch[:, 1024:1024 + m],
                                                scratch[:, 512:512 + m],
                                                wsc_[:, sl], s0=1.0, s1=0.5))
                    if signed:
                        P.op("vector", tt(scratch[:, 1536:1536 + m], ax_ap[:, sl],
                                          scratch[:, 1024:1024 + m], ALU.subtract))
                        P.op("vector", tt(out_slices[hh], scratch[:, 1536:1536 + m],
                                          wsgn_[:, sl], ALU.mult))
                    else:
                        P.op("vector", tt(out_slices[hh], x_ap[:, sl],
                                          scratch[:, 1024:1024 + m], ALU.subtract))
                vend = P.bump("vector", "v")
                qlastb[key] = vend
                qlast["v"] = vend
                return vend
            return finish

        def walk_quant(x_ap, n, man16, signed, xwaits, out_slices, ext=False,
                       he=False):
            bufs = (wabs[:, 0:n], wsgn[:, 0:n], wacc[:, 0:n], wis[:, 0:n],
                    wsc[:, 0:n], wlo[:, 0:n], "serial")
            qlastb["serial"] = qlast["v"]
            return walk_quant2(bufs, x_ap, n, man16, signed, xwaits,
                               out_slices, ext=ext, he=he)()

        def walk_ln_stats_apply(src_t, src_ofs, dst_ap, xwaits):
            full = src_t[:, src_ofs:src_ofs + D]
            h0 = src_t[:, src_ofs:src_ofs + 512]
            h1 = src_t[:, src_ofs + 512:src_ofs + 1024]
            P.op("vector", red(stat[:, 112:113], full, ALU.add),
                 wait=list(xwaits) + [("s", lnlast["s"])])
            P.op("vector", tt(sq[:, 0:512], h0, h0, ALU.mult))
            P.op("vector", tt(sq[:, 512:1024], h1, h1, ALU.mult))
            P.op("vector", red(stat[:, 113:114], sq[:], ALU.add))
            vw = P.bump("vector", "v")
            P.op("scalar", act(stat[:, 115:116], stat[:, 112:113], AF.Identity,
                               scale=-1.0 / D), wait=[("v", vw)])
            P.op("scalar", act(stat[:, 116:117], stat[:, 113:114], AF.Identity,
                               scale=1.0 / D, bias=cc("eps")))
            P.op("scalar", act(stat[:, 122:123], stat[:, 112:113], AF.Identity))
            P.op("scalar", act(stat[:, 123:124], stat[:, 112:113], AF.Identity))
            P.op("scalar", act(stat[:, 124:125], stat[:, 112:113], AF.Identity))
            P.op("scalar", act(stat[:, 117:118], stat[:, 112:113], AF.Square,
                               scale=1.0 / D))
            P.op("scalar", act(stat[:, 118:119], stat[:, 117:118], AF.Ln, scale=-1.0,
                               bias=stat[:, 116:117]))
            P.op("scalar", act(stat[:, 119:120], stat[:, 118:119], AF.Exp,
                               scale=cc("neghalf")))
            sw = P.bump("scalar", "s")
            va = None
            if dst_ap is not None:
                P.op("vector", ts(dst_ap, full, stat[:, 115:116], stat[:, 119:120],
                                  ALU.add, ALU.mult), wait=[("s", sw)])
                va = P.bump("vector", "v")
            lnlast["s"] = sw
            return sw, va

        def walk_ln_phase(ln_tm, ps_tr, xwaits_fn):
            cS, cQ, cM3, cMQ, cM2, cL, cR = 0, 16, 32, 48, 64, 80, 96
            for i in range(NT):
                full = x_tm[:, i * D:(i + 1) * D]
                h0 = x_tm[:, i * D:i * D + 512]
                h1 = x_tm[:, i * D + 512:(i + 1) * D]
                w = list(xwaits_fn(i))
                if i == 0:
                    w += [("s", lnlast["s"])]
                P.op("vector", red(stat[:, cS + i:cS + i + 1], full, ALU.add),
                     wait=w)
                P.op("vector", tt(sq[:, 0:512], h0, h0, ALU.mult))
                P.op("vector", tt(sq[:, 512:1024], h1, h1, ALU.mult))
                P.op("vector", red(stat[:, cQ + i:cQ + i + 1], sq[:], ALU.add))
            vw = P.bump("vector", "v")
            for i in range(NT):
                P.op("scalar", act(stat[:, cM3 + i:cM3 + i + 1],
                                   stat[:, cS + i:cS + i + 1],
                                   AF.Identity, scale=-1.0 / D),
                     wait=[("v", vw)] if i == 0 else ())
            for i in range(NT):
                P.op("scalar", act(stat[:, cMQ + i:cMQ + i + 1],
                                   stat[:, cQ + i:cQ + i + 1],
                                   AF.Identity, scale=1.0 / D, bias=cc("eps")))
            for i in range(NT):
                P.op("scalar", act(stat[:, cM2 + i:cM2 + i + 1],
                                   stat[:, cS + i:cS + i + 1],
                                   AF.Square, scale=1.0 / D))
            for i in range(NT):
                P.op("scalar", act(stat[:, cL + i:cL + i + 1],
                                   stat[:, cM2 + i:cM2 + i + 1],
                                   AF.Ln, scale=-1.0,
                                   bias=stat[:, cMQ + i:cMQ + i + 1]))
            for i in range(NT):
                P.op("scalar", act(stat[:, cR + i:cR + i + 1],
                                   stat[:, cL + i:cL + i + 1],
                                   AF.Exp, scale=cc("neghalf")))
            sw = P.bump("scalar", "s")
            lnlast["s"] = sw
            for i in range(NT):
                P.op("vector", ts(ln_tm[:, i * D:(i + 1) * D],
                                  x_tm[:, i * D:(i + 1) * D],
                                  stat[:, cM3 + i:cM3 + i + 1],
                                  stat[:, cR + i:cR + i + 1],
                                  ALU.add, ALU.mult),
                     wait=[("s", sw)] if i == 0 else ())
                va = P.bump("vector", "v")
                for half in range(2):
                    for cq in range(4):
                        c = half * 4 + cq
                        P.op("tensor",
                             tpz(ps_tr[:, cq * 128:(cq + 1) * 128],
                                 ln_tm[:, i * D + c * 128: i * D + (c + 1) * 128]),
                             wait=[("v", va), ("s", P.now("s"))] if cq == 0 else [])
                    tw = P.bump("tensor", "t")
                    for cq in range(4):
                        c = half * 4 + cq
                        P.op("scalar",
                             cpy(ln_fm[:, c * S + i * 128: c * S + (i + 1) * 128],
                                 ps_tr[:, cq * 128:(cq + 1) * 128]),
                             wait=[("t", tw)] if cq == 0 else [])
                    se = P.bump("scalar", "s")
                    P.op("tensor", None, wait=[("s", se)])
            return P.now("s")

        zs = open_scope("zscope", [("zfm", [128, NCH * S], BF16)])
        zfm = zs["zfm"]
        lt = open_scope("lntm", [("ln_tm", [128, NT * D], BF16)])
        ps_tr = scope["lntm"].enter_context(nc.psum_tensor("ps_tr1", [128, 512], BF16))
        LN1_S = walk_ln_phase(lt["ln_tm"], ps_tr, lambda i: [("din", PRO_DIN)])
        close_scope("lntm")
        if debug == "ln1":
            P.op("gpsimd", _ld(bass.AP(xout, 0, [[16384, 128], [1, 16384]]),
                               ln_fm[:]),
                 wait=[("s", LN1_S)], inc=[("dout", 16)])
            P.op("sync", None, wait=[("dout", P.now("dout"))])
            close_all_scopes()
            return _emit_and_finish(nc, P, sems)
        cbufs = open_scope("chain", [("wabs", [128, D], F32), ("wsgn", [128, D], BF16),
                                     ("wacc", [128, D], F32), ("wis", [128, D], F32),
                                     ("wsc", [128, D], F32), ("wlo", [128, D], F32)],
)
        wabs, wsgn, wacc, wis, wsc, wlo = (cbufs[n] for n in
                                           ("wabs", "wsgn", "wacc", "wis", "wsc", "wlo"))
        wsetA = (wabs[:], wsgn[:], wacc[:], wis[:], wsc[:], wlo[:], "w2A")
        wb = open_scope("w2st", [("w2st", [128, D], F32),
                                 ("w2qs", [128, D], FP8)])
        w2st, w2qs = wb["w2st"], wb["w2qs"]
        att = open_scope("attn", [("qh", [128, 2 * S], BF16), ("kh", [128, 2 * S], BF16),
                                  ("wstr", [128, 3 * 1024], BF16),
                                  ("vpair", [128, NT * 130], BF16),
                                  ("probsT", [128, 4 * GQ], BF16),
                                  ("zstage", [64, S], BF16),
                                  ("recst", [128, GQ], BF16),
                                  ("recsb", [64, GQ], BF16)])
        qh, kh, wstr, vpair, probsT, zstage, recst, recsb = (
            att[n] for n in ("qh", "kh", "wstr", "vpair", "probsT", "zstage",
                             "recst", "recsb"))
        ps_s = scope["attn"].enter_context(nc.psum_tensor("ps_s", [128, 2048], F32))
        T_LN1 = P.now("t")
        zevacv = [0]
        wpend = [None]
        P.op("vector", _ms(qh[64:128, 0:S], 0.0),
             wait=[("din", PRO_DIN), ("t", T_LN1)])
        P.op("vector", _ms(qh[0:64, S:2 * S], 0.0))
        P.op("vector", _ms(kh[64:128, 0:S], 0.0))
        P.op("vector", _ms(kh[0:64, S:2 * S], 0.0))
        P.op("vector", _ms(vpair[:], 1.0))
        P.op("vector", _ms(recst[:], 0.0))
        INIT_V = P.bump("vector", "v")

        for p in range(NP):
            base = 0
            for idx, wsrc in enumerate((wqp, wkp, wvp)):
                P.op("gpsimd",
                     _ld(bass.AP(wstr, base + idx * 1024,
                                 [[3 * 1024, 128], [128, NCH], [1, 128]]),
                         bass.AP(wsrc, p * D * 128,
                                 [[128, 128], [128 * 128, NCH], [1, 128]])),
                     wait=[("t", P.now("t"))] if p >= 1 else [("t", T_LN1)],
                     inc=[("dg", 16)])
            DG_P = P.now("dg")

            evs = {}
            for which, wof, dst, scale in (("q", 0, qh, True), ("k", 1024, kh, False)):
                for g4 in range(4):
                    ps = ps_a if g4 % 2 == 0 else ps_b
                    for c in range(NCH):
                        P.op("tensor",
                             mm(ps[:, 0:512],
                                wstr[:, base + wof + c * 128: base + wof + (c + 1) * 128],
                                ln_fm[:, c * S + g4 * 512: c * S + (g4 + 1) * 512],
                                start=(c == 0), stop=(c == NCH - 1)),
                             wait=([("dg", DG_P), ("s", P.now("s"))] if c == 0 else []))
                    tw = P.bump("tensor", "t")
                    col = g4 * 512
                    nm = "eighth" if scale else "one"
                    P.op("scalar", act(dst[0:64, col:col + 512], ps[0:64, 0:512],
                                       AF.Copy, scale=cc(nm, 0, 64)),
                         wait=[("t", tw)])
                    P.op("scalar", act(dst[64:128, S + col:S + col + 512],
                                       ps[64:128, 0:512], AF.Copy, scale=cc(nm, 64, 128)))
                    P.bump("scalar", "s")
            for i in range(NT):
                ps = ps_a if i % 2 == 0 else ps_b
                for c in range(NCH):
                    P.op("tensor",
                         mm(ps[:, 0:128],
                            ln_fm[:, c * S + i * 128: c * S + (i + 1) * 128],
                            wstr[:, base + 2048 + c * 128: base + 2048 + (c + 1) * 128],
                            start=(c == 0), stop=(c == NCH - 1)),
                         wait=[("s", P.now("s"))] if c == 0 else [])
                tw = P.bump("tensor", "t")
                P.op("scalar",
                     act(bass.AP(vpair, i * 130, [[NT * 130, 128], [65, 2], [1, 64]]),
                         bass.AP(ps, 0, [[512, 128], [64, 2], [1, 64]]),
                         AF.Copy),
                     wait=[("t", tw)])
                P.bump("scalar", "s")
            VS = P.now("s")

            for hloc in range(2):
                qcol = hloc * S
                for g in range(NG):
                    njc = 4 * g + 4
                    for jb in range(0, njc, 4):
                        jhi = jb + 4
                        diag = (jhi == njc)
                        for j in range(jb, jhi):
                            if j == jb:
                                jw = ([("s", VS), ("v", INIT_V)] if jb == 0
                                      else [("s", P.now("s"))])
                            else:
                                jw = []
                            P.op("tensor",
                                 mm(ps_s[:, (j - jb) * 512:(j - jb + 1) * 512],
                                    kh[:, qcol + j * 128: qcol + (j + 1) * 128],
                                    qh[:, qcol + g * GQ: qcol + (g + 1) * GQ],
                                    start=True, stop=True),
                                 wait=jw)
                        tw = P.bump("tensor", "t")
                        P.op("scalar",
                             act(probsT[:, 0: 4 * GQ],
                                 ps_s[:, 0:2048], AF.Exp,
                                 bias=cc("negu")),
                             wait=[("t", tw)])
                        se = P.bump("scalar", "s")
                        pvw = [("s", se)]
                        if diag:
                            for dj in range(4):
                                o = 128 * dj
                                if o > 0:
                                    P.op("vector",
                                         _ms(probsT[:, dj * GQ: dj * GQ + o], 0.0),
                                         wait=[("s", se)])
                                P.op("vector",
                                     tt(probsT[:, dj * GQ + o: dj * GQ + o + 128],
                                        probsT[:, dj * GQ + o: dj * GQ + o + 128],
                                        trimask[:], ALU.mult),
                                     wait=[("s", se)])
                            VMASK = P.bump("vector", "v")
                            pvw = [("v", VMASK)]
                        for j in range(jb, jhi):
                            w = []
                            if j == jb:
                                w = list(pvw)
                            if j == 0:
                                w.append(("v", zevacv[0]))
                            P.op("tensor",
                                 mm(ps_z[0:65, 0:GQ],
                                    bass.AP(vpair, j * 130 + hloc * 65,
                                            [[NT * 130, 128], [1, 65]]),
                                    probsT[:, (j - jb) * GQ:(j - jb + 1) * GQ],
                                    start=(j == 0), stop=(j == njc - 1)),
                                 wait=w)
                    TPV = P.bump("tensor", "t")
                    def _recip(e):
                        with nc.allow_low_precision(reason="recip feeds bf16 mm operand"):
                            return e.reciprocal(recst[0:1, 0:GQ], ps_z[64:65, 0:GQ])
                    P.op("vector", _recip, wait=[("t", TPV)])
                    VREC = P.bump("vector", "v")
                    P.op("tensor", mm(ps_r[0:64, 0:GQ], bcast64[:], recst[:, 0:GQ],
                                      start=True, stop=True),
                         wait=[("v", VREC), ("s", P.now("s"))])
                    TREC = P.bump("tensor", "t")
                    P.op("vector", cpy(recsb[0:64, 0:GQ], ps_r[0:64, 0:GQ]),
                         wait=[("t", TREC)])
                    SREC = P.bump("vector", "v")
                    if hloc == 0:
                        zdst = zfm[0:64, p * S + g * GQ: p * S + (g + 1) * GQ]
                    else:
                        zdst = zstage[0:64, g * GQ:(g + 1) * GQ]
                    zw = []
                    if hloc == 1 and g == 0:
                        zw.append(("dz", P.now("dz")))
                    P.op("vector", tt(zdst, ps_z[0:64, 0:GQ], recsb[0:64, 0:GQ],
                                      ALU.mult), wait=zw)
                    zevacv[0] = P.bump("vector", "v")
                if hloc == 1:
                    P.op("sync", _ld(zfm[64:128, p * S:(p + 1) * S], zstage[0:64, 0:S]),
                         wait=[("v", P.now("v"))], inc=[("dz", 16)])
            if p == 0:
                P.op("sync", _ld(w2st[:], b2bc_in[:]),
                     wait=[("t", T_LN1)], inc=[("din", 16)])
                walk_quant(w2st[:], D, True, True,
                           [("din", P.now("din"))],
                           [bqbc[:, 0:512], bqbc[:, 512:1024]], he=True)
                qlastb["wh0"] = qlast["v"]
                qlastb["wh1"] = qlast["v"]
            for c in range(4 * p, 4 * p + 4):
                lw = ([("t", T_LN1), ("s", P.now("s"))] if c == 0
                      else [("s", P.now("s")), ("dwr", 16 * max(c - 1, 0))])
                P.op("sync", _ld(w2st[:],
                                 bass.AP(w2t, c * 128 * D, [[D, 128], [1, D]])),
                     wait=lw, inc=[("din", 16)])
                din_c = P.now("din")
                for h2 in range(2):
                    o = h2 * 512
                    wbset = (wabs[:, o:o + 512], wsgn[:, o:o + 512],
                             wacc[:, o:o + 512], wis[:, o:o + 512],
                             wsc[:, o:o + 512], wlo[:, o:o + 512], f"wh{h2}")
                    fin = walk_quant2(wbset, w2st[:, o:o + 512], 512, False,
                                      True, [("din", din_c)],
                                      [w2qs[:, o:o + 512]], he=True)

                    def wfin(f=fin, c=c, h2=h2):
                        if h2 == 0 and c >= 1:
                            P.op("vector", None, wait=[("dwr", 16 * c)])
                        vend = f()
                        if h2 == 1:
                            P.op("gpsimd",
                                 _ld(bass.AP(w2q_dram, c * 128 * D,
                                             [[D, 128], [1, D]]), w2qs[:]),
                                 wait=[("v", vend)], inc=[("dwr", 16)])
                    if wpend[0] is not None:
                        wpend[0]()
                    wpend[0] = wfin
        wpend[0]()
        wpend[0] = None
        W2_DWR = P.now("dwr")
        if debug in ("w2q", "w2qb"):
            cofs = 0 if debug == "w2q" else 16
            for cx in range(16):
                c = cx + cofs
                P.op("sync", _ld(w2qs[:],
                                 bass.AP(w2q_dram, c * 128 * D, [[D, 128], [1, D]])),
                     wait=[("dwr", W2_DWR), ("dg", P.now("dg"))], inc=[("din", 16)])
                P.op("gpsimd", _ld(bass.AP(xout, cx * 1024,
                                           [[16384, 128], [1, 1024]]),
                                   w2qs[:]),
                     wait=[("din", P.now("din"))], inc=[("dg", 16)])
            P.op("sync", None, wait=[("dg", P.now("dg"))])
            close_all_scopes()
            return _emit_and_finish(nc, P, sems)
        if debug == "bq":
            P.op("sync", _ld(bass.AP(xout, 0, [[D, 128], [1, D]]), bqbc[:]),
                 wait=[("v", P.now("v"))], inc=[("dout", 16)])
            P.op("sync", None, wait=[("dout", P.now("dout"))])
            close_all_scopes()
            return _emit_and_finish(nc, P, sems)
        ATT_DZ = P.now("dz")
        ATT_V = P.now("v")
        close_scope("attn")
        close_scope("w2st")
        close_scope("chain")
        ob = open_scope("outp", [("woutb", [128, NCH * D], BF16),
                                 ("attn_t", [128, D], F32)])
        woutb, attn_t = ob["woutb"], ob["attn_t"]

        P.op("gpsimd", _ld(bass.AP(woutb, 0, [[NCH * D, 128], [D, NCH], [1, D]]),
                           bass.AP(wop, 0, [[D, 128], [128 * D, NCH], [1, D]])),
             wait=[("t", P.now("t")), ("v", ATT_V), ("s", P.now("s"))],
             inc=[("dg", 16)])
        WOUT_DG = P.now("dg")

        op_evacv = {}
        S_ATT = P.now("s")
        for i in range(NT):
            for half in range(2):
                ps = ps_a if half == 0 else ps_b
                for c in range(NCH):
                    if c == 0:
                        if half in op_evacv:
                            mmw = [("v", op_evacv[half])]
                        else:
                            mmw = [("dg", WOUT_DG), ("dz", ATT_DZ),
                                   ("v", ATT_V), ("s", S_ATT)]
                    else:
                        mmw = []
                    P.op("tensor",
                         mm(ps[:, 0:512],
                            zfm[:, c * S + i * 128: c * S + (i + 1) * 128],
                            woutb[:, c * D + half * 512: c * D + half * 512 + 512],
                            start=(c == 0), stop=(c == NCH - 1)),
                         wait=mmw)
                tw = P.bump("tensor", "t")
                P.op("vector",
                     sttf(attn_t[:, half * 512: half * 512 + 512], ps[:, 0:512],
                          1.0, boutbc[:, half * 512: half * 512 + 512],
                          ALU.mult, ALU.add),
                     wait=[("t", tw)])
                op_evacv[half] = P.bump("vector", "v")
            sw, _ = walk_ln_stats_apply(attn_t, 0, None, [])
            P.op("vector", ts(attn_t[:], attn_t[:], stat[:, 115:116], stat[:, 119:120],
                              ALU.add, ALU.mult), wait=[("s", sw)])
            P.op("vector", tt(x_tm[:, i * D:(i + 1) * D],
                              x_tm[:, i * D:(i + 1) * D], attn_t[:], ALU.add))
            P.bump("vector", "v")
        OUTPROJ_V = P.now("v")
        close_scope("outp")
        close_scope("zscope")

        if debug == "x2":
            for i in range(NT):
                P.op("sync", _ld(bass.AP(xout, i * 128 * D, [[D, 128], [1, D]]),
                                 x_tm[:, i * D:(i + 1) * D]),
                     wait=[("v", OUTPROJ_V)], inc=[("dout", 16)])
            P.op("sync", None, wait=[("dout", P.now("dout"))])
            debug = "STOP"

        if debug == "STOP":
            close_all_scopes()
            return _emit_and_finish(nc, P, sems)
        lt2 = open_scope("lntm2", [("ln_tm", [128, NT * D], BF16)])
        ps_tr = scope["lntm2"].enter_context(nc.psum_tensor("ps_tr2", [128, 512], BF16))
        T_OUTP = P.now("t")
        LN2_S = walk_ln_phase(lt2["ln_tm"], ps_tr,
                              lambda i: [("v", OUTPROJ_V), ("t", T_OUTP)])
        close_scope("lntm2")
        if debug == "ln2":
            P.op("gpsimd", _ld(bass.AP(xout, 0, [[16384, 128], [1, 16384]]),
                               ln_fm[:]),
                 wait=[("s", LN2_S)], inc=[("dout", 16)])
            P.op("sync", None, wait=[("dout", P.now("dout"))])
            close_all_scopes()
            return _emit_and_finish(nc, P, sems)
        fbufs = open_scope("ffn", [("hq8", [128, NHF * 1024], FP8),
                                   ("w1str", [128, 2 * 1024], BF16),
                                   ("w2qs", [128, 2 * D], FP8),
                                   ("xr_sb", [128, 1024], F32),
                                   ("ffo", [128, 2 * D], F32),
                                   ("ffq", [128, D], F16)])
        hq8, w1str, w2qs, xr_sb, ffo, ffq = (
            fbufs[n] for n in ("hq8", "w1str", "w2qs", "xr_sb", "ffo", "ffq"))
        cb2 = open_scope("chain2", [("wabs", [128, D], F32), ("wsgn", [128, D], BF16),
                                    ("wacc", [128, D], F32), ("wis", [128, D], F32),
                                    ("wsc", [128, D], F32), ("wlo", [128, D], F32)])
        wabs, wsgn, wacc, wis, wsc, wlo = (cb2[n] for n in
                                           ("wabs", "wsgn", "wacc", "wis", "wsc", "wlo"))

        first_w2_load = [True]
        hpend = [None]
        ffn2_evac = [0]
        ffn1_evacv = {}
        w2_tc = []
        for sg in range(2):
            ffn1_evacv = {}
            col0 = sg * 1024
            for m in range(NHF):
                slot = (m % 2) * 1024
                P.op("gpsimd",
                     _ld(bass.AP(w1str, slot, [[2 * 1024, 128], [128, NCH], [1, 128]]),
                         bass.AP(w1t, m * NCH * 128 * 128,
                                 [[128, 128], [128 * 128, NCH], [1, 128]])),
                     wait=[("t", P.now("t"))] if (sg > 0 or m >= 2)
                     else [("t", P.now("t")), ("s", LN2_S)],
                     inc=[("dg", 16)])
                dgm = P.now("dg")
                for g2 in range(2):
                    ps = ps_a if g2 == 0 else ps_b
                    for c in range(NCH):
                        P.op("tensor",
                             mm(ps[:, 0:512],
                                w1str[:, slot + c * 128: slot + (c + 1) * 128],
                                ln_fm[:, c * S + col0 + g2 * 512:
                                      c * S + col0 + g2 * 512 + 512],
                                start=(c == 0), stop=(c == NCH - 1)),
                             wait=([("dg", dgm),
                                    ("v", ffn1_evacv.get(g2, ffn2_evac[0]))]
                                   if c == 0 else []))
                    tw = P.bump("tensor", "t")
                    hk = m * 2 + g2
                    o = (hk % 2) * 512
                    hb = (wabs[:, o:o + 512], wsgn[:, o:o + 512],
                          wacc[:, o:o + 512], wis[:, o:o + 512],
                          wsc[:, o:o + 512], wlo[:, o:o + 512], f"h{hk % 2}")
                    P.op("vector", ts(xr_sb[:, o:o + 512], ps[:, 0:512],
                                      b1_sb[:, m:m + 1], 0.0, ALU.add, ALU.max),
                         wait=[("t", tw), ("v", qlastb.get(f"h{hk % 2}", 0))])
                    ffn1_evacv[g2] = P.bump("vector", "v")
                    fin = walk_quant2(
                        hb, xr_sb[:, o:o + 512], 512, False, False, [],
                        [hq8[:, m * 1024 + g2 * 512: m * 1024 + g2 * 512 + 512]],
                        ext=True)
                    if hpend[0] is not None:
                        hpend[0]()
                    hpend[0] = fin
            if hpend[0] is not None:
                hpend[0]()
                hpend[0] = None
            HQ_V = P.now("v")
            if debug and debug.startswith("ffacc") and sg == 0:
                nacc = int(debug[5:])
                for c in range(nacc):
                    slot = (c % 2) * D
                    P.op("sync", _ld(w2qs[:, slot:slot + D],
                                     bass.AP(w2q_dram, c * 128 * D,
                                             [[D, 128], [1, D]])),
                         wait=[("v", HQ_V), ("dwr", W2_DWR),
                               ("t", P.now("t"))], inc=[("din", 16)])
                    P.op("tensor", mm(ps_a[:, 0:512], hq8[:, c * 1024: c * 1024 + 128],
                                      w2qs[:, slot:slot + 512],
                                      start=(c == 0), stop=(c == nacc - 1)),
                         wait=[("din", P.now("din")), ("v", HQ_V),
                               ("s", P.now("s"))])
                    P.bump("tensor", "t")
                P.op("scalar", cpy(ffo[:, 0:512], ps_a[:, 0:512]),
                     wait=[("t", P.now("t"))])
                s1 = P.bump("scalar", "s")
                P.op("sync", _ld(bass.AP(xout, 0, [[D, 128], [1, 512]]),
                                 ffo[:, 0:512]),
                     wait=[("s", s1)], inc=[("dout", 16)])
                P.op("sync", None, wait=[("dout", P.now("dout"))])
                close_all_scopes()
                return _emit_and_finish(nc, P, sems)
            if debug in ("hqa", "hqb") and sg == 0:
                hofs = 0 if debug == "hqa" else 16384
                P.op("gpsimd", _ld(bass.AP(xout, 0, [[16384, 128], [1, 16384]]),
                                   hq8[:, hofs:hofs + 16384]),
                     wait=[("v", HQ_V)], inc=[("dout", 16)])
                P.op("sync", None, wait=[("dout", P.now("dout"))])
                close_all_scopes()
                return _emit_and_finish(nc, P, sems)
            for stp in range(4):
                sts = (col0 // 128) + stp * 2
                psl = (ps_a, ps_b, ps_z, ps_r)
                for c in range(NHF):
                    slot = (c % 2) * D
                    lw = [("v", HQ_V)]
                    if first_w2_load[0]:
                        lw.append(("dwr", W2_DWR))
                        first_w2_load[0] = False
                    if len(w2_tc) >= 2:
                        lw.append(("t", w2_tc[-2]))
                    P.op("sync", _ld(w2qs[:, slot:slot + D],
                                     bass.AP(w2q_dram, c * 128 * D, [[D, 128], [1, D]])),
                         wait=lw, inc=[("din", 16)])
                    dinc = P.now("din")
                    if debug == "ffo1" and sg == 0 and stp == 0 and c == 0:
                        P.op("gpsimd", _ld(bass.AP(xout, 128 * D, [[D, 128], [1, D]]),
                                           w2qs[:, 0:D]),
                             wait=[("din", dinc)], inc=[("dg", 16)])
                        P.op("gpsimd", _ld(bass.AP(xout, 256 * D, [[D, 128], [1, 128]]),
                                           hq8[:, 0:128]),
                             wait=[("v", HQ_V)], inc=[("dg", 16)])
                        P.op("tensor", mm(ps_a[:, 0:512], hq8[:, 0:128],
                                          w2qs[:, 0:512], start=True, stop=True),
                             wait=[("din", dinc), ("v", HQ_V), ("s", P.now("s"))])
                        t1 = P.bump("tensor", "t")
                        P.op("scalar", cpy(ffo[:, 0:512], ps_a[:, 0:512]),
                             wait=[("t", t1)])
                        s1 = P.bump("scalar", "s")
                        P.op("sync", _ld(bass.AP(xout, 384 * D, [[D, 128], [1, 512]]),
                                         ffo[:, 0:512]),
                             wait=[("s", s1)], inc=[("dout", 16)])
                        P.op("sync", None, wait=[("dout", P.now("dout")),
                                                 ("dg", P.now("dg"))])
                        close_all_scopes()
                        return _emit_and_finish(nc, P, sems)
                    for sti in range(2):
                        for half in range(2):
                            ps = psl[sti * 2 + half]
                            P.op("tensor",
                                 mm(ps[:, 0:512],
                                    hq8[:, c * 1024 + (stp * 2 + sti) * 128:
                                        c * 1024 + (stp * 2 + sti + 1) * 128],
                                    w2qs[:, slot + half * 512: slot + half * 512 + 512],
                                    start=(c == 0), stop=(c == NHF - 1)),
                                 wait=([("din", dinc)]
                                       + ([("v", ffn2_evac[0])]
                                          if c == 0 else []))
                                 if (sti == 0 and half == 0) or c == 0 else [])
                    w2_tc.append(P.bump("tensor", "t"))
                tws = w2_tc[-1]
                if debug == "ffo2" and sg == 0 and stp == 0:
                    P.op("scalar", cpy(ffo[:, 0:512], ps_a[:, 0:512]),
                         wait=[("t", tws)])
                    s1 = P.bump("scalar", "s")
                    P.op("sync", _ld(bass.AP(xout, 0, [[D, 128], [1, 512]]),
                                     ffo[:, 0:512]),
                         wait=[("s", s1)], inc=[("dout", 16)])
                    P.op("sync", None, wait=[("dout", P.now("dout"))])
                    close_all_scopes()
                    return _emit_and_finish(nc, P, sems)
                for sti in range(2):
                    for half in range(2):
                        ps = psl[sti * 2 + half]
                        P.op("vector",
                             sttf(ffo[:, sti * D + half * 512:
                                      sti * D + half * 512 + 512], ps[:, 0:512],
                                  1.0, bqbc[:, half * 512: half * 512 + 512],
                                  ALU.mult, ALU.add),
                             wait=[("t", tws)] if (sti == 0 and half == 0) else ())
                ffn2_evac[0] = P.bump("vector", "v")
                vffo = ffn2_evac[0]
                for sti in range(2):
                    i = sts + sti
                    fslc = ffo[:, sti * D:(sti + 1) * D]
                    if debug == "ffo0" and sti == 0:
                        P.op("sync", _ld(bass.AP(xout, 0, [[D, 128], [1, D]]),
                                         fslc),
                             wait=[("v", vffo)], inc=[("dout", 16)])
                        P.op("sync", None, wait=[("dout", P.now("dout"))])
                        close_all_scopes()
                        return _emit_and_finish(nc, P, sems)
                    qv = walk_quant(fslc, D, True, True, [("v", vffo)],
                                    [ffq[:, 0:512], ffq[:, 512:1024]], he=True)
                    if debug == "ffx" and i == 5:
                        P.op("sync", _ld(bass.AP(xout, 0, [[D, 128], [1, D]]),
                                         fslc),
                             wait=[("v", qv)], inc=[("dout", 16)])
                        P.op("gpsimd", _ld(bass.AP(xout, 128 * D, [[D, 128], [1, D]]),
                                           ffq[:]),
                             wait=[("v", qv)], inc=[("dg", 16)])
                        P.op("sync", None, wait=[("dout", P.now("dout")),
                                                 ("dg", P.now("dg"))])
                        close_all_scopes()
                        return _emit_and_finish(nc, P, sems)
                    sw, _ = walk_ln_stats_apply(ffq, 0, None, [])
                    P.op("vector", ts(fslc, ffq[:], stat[:, 115:116], stat[:, 119:120],
                                      ALU.add, ALU.mult), wait=[("s", sw)])
                    P.op("vector", tt(x_tm[:, i * D:(i + 1) * D],
                                      x_tm[:, i * D:(i + 1) * D], fslc, ALU.add))
                    vres = P.bump("vector", "v")
                    P.op("sync", _ld(bass.AP(xout, i * 128 * D, [[D, 128], [1, D]]),
                                     x_tm[:, i * D:(i + 1) * D]),
                         wait=[("v", vres)], inc=[("dout", 16)])

        P.op("sync", None, wait=[("dout", P.now("dout"))])
        close_scope("chain2")
        close_scope("ffn")

        return _emit_and_finish(nc, P, sems)


def _emit_and_finish(nc, P, sems):
    with nc.Block() as blk:
        def make_body(eng_name):
            def body(eng):
                for waits, emit, incs in P.ops[eng_name]:
                    for sname, val in waits:
                        eng.wait_ge(sems[sname], val)
                    if emit is None:
                        assert not incs
                        continue
                    inst = emit(eng)
                    for sname, amt in incs:
                        inst.then_inc(sems[sname], amt)
            return body
        blk.sync(make_body("sync"))
        blk.tensor(make_body("tensor"))
        blk.vector(make_body("vector"))
        blk.scalar(make_body("scalar"))
        blk.gpsimd(make_body("gpsimd"))
    mybir.codegen_inst_isa_subclasses(nc)
    return nc


def _host_inputs(L0_attn_resid, L0_attn_W_Q, L0_attn_W_K, L0_attn_W_V,
                 L0_attn_W_out, L0_attn_b_out, L0_ffn_w1, L0_ffn_b1,
                 L0_ffn_w2, L0_ffn_b2):
    bf = ml_dtypes.bfloat16
    x = np.ascontiguousarray(np.asarray(L0_attn_resid, dtype=np.float32))
    wq = np.asarray(L0_attn_W_Q, dtype=np.float32).reshape(D, NH * DH)
    wk = np.asarray(L0_attn_W_K, dtype=np.float32).reshape(D, NH * DH)
    wv = np.asarray(L0_attn_W_V, dtype=np.float32).reshape(D, NH * DH)
    wqp = np.ascontiguousarray(wq.reshape(D, NP, 128).transpose(1, 0, 2))
    wkp = np.ascontiguousarray(wk.reshape(D, NP, 128).transpose(1, 0, 2))
    wvp = np.ascontiguousarray(wv.reshape(D, NP, 128).transpose(1, 0, 2))
    wo = np.asarray(L0_attn_W_out, dtype=np.float32).transpose(1, 0, 2).reshape(NH * DH, D)
    wop = np.ascontiguousarray(wo.reshape(NCH, 128, D))
    w1 = np.asarray(L0_ffn_w1, dtype=np.float32)
    w1t = np.ascontiguousarray(w1.reshape(NHF, 128, NCH, 128).transpose(0, 2, 3, 1))
    w2 = np.asarray(L0_ffn_w2, dtype=np.float32)
    w2t = np.ascontiguousarray(w2.T.reshape(NHF, 128, D))
    b1 = np.asarray(L0_ffn_b1, dtype=np.float32)
    b1in = np.ascontiguousarray(b1.reshape(NHF, 128).T)
    boutbc = np.ascontiguousarray(
        np.broadcast_to(np.asarray(L0_attn_b_out, dtype=np.float32), (128, D)))
    b2bc = np.ascontiguousarray(
        np.broadcast_to(np.asarray(L0_ffn_b2, dtype=np.float32), (128, D)))
    cstv = np.zeros((128, NCST), dtype=np.float32)
    for k, v in CST.items():
        cstv[:, CSTI[k]] = v
    identv = np.eye(128, dtype=np.float32).astype(bf)
    tri = np.triu(np.ones((128, 128), dtype=np.float32)).astype(bf)
    bc64 = np.zeros((128, 64), dtype=np.float32)
    bc64[0, :] = 1.0
    bc64 = bc64.astype(bf)
    common = dict(wqp=wqp, wkp=wkp, wvp=wvp, wop=wop, w1t=w1t, w2t=w2t,
                  b1in=b1in, boutbc_in=boutbc, b2bc_in=b2bc, cst_in=cstv,
                  ident_in=identv, trimask_in=tri, bcast64_in=bc64)
    in_maps = []
    for b in range(8):
        m = dict(common)
        m["xin"] = np.ascontiguousarray(x[b])
        in_maps.append(m)
    return in_maps


def kernel(L0_attn_resid, L0_attn_W_Q, L0_attn_W_K, L0_attn_W_V, L0_attn_W_out,
           L0_attn_b_out, L0_ffn_w1, L0_ffn_b1, L0_ffn_w2, L0_ffn_b2):
    in_maps = _host_inputs(L0_attn_resid, L0_attn_W_Q, L0_attn_W_K, L0_attn_W_V,
                           L0_attn_W_out, L0_attn_b_out, L0_ffn_w1, L0_ffn_b1,
                           L0_ffn_w2, L0_ffn_b2)
    nc = build_nc()
    res = run_bass_kernel_spmd(nc, in_maps, core_ids=list(range(8)))
    out = np.stack([np.asarray(r["xout"]) for r in res.results], axis=0)
    return out


# version 09
# speedup vs baseline: 1.0937x; 1.0027x over previous
import contextlib
import numpy as np
import ml_dtypes
import concourse.bass as bass
import concourse.mybir as mybir
from concourse.bass_utils import run_bass_kernel_spmd
from concourse.dve_op import DveOp
from concourse.dve_spec import Spec, Src0, Src1, lower, C0, C1, C2, eq, select
import concourse.dve_ops as dve_ops

F32 = mybir.dt.float32
F16 = mybir.dt.float16
BF16 = mybir.dt.bfloat16
FP8 = mybir.dt.float8e4
AF = mybir.ActivationFunctionType
ALU = mybir.AluOpType
AX = mybir.AxisListType
LN2 = 0.6931471805599453

S = 2048
D = 1024
NT = 16
NCH = 8
NH = 16
NP = 8
DH = 64
HF = 4096
NHF = 32
GQ = 512
NG = S // GQ
USOFT = 12.0
EPS = 1e-5

_OPS = {}


def _fdt(x):
    return np.float64 if isinstance(x, np.ndarray) and x.dtype == np.float64 else np.float32


def _reg(name, spec_body, reference, row):
    spec = Spec(body=spec_body)
    sha = lower(spec, ver="v3", perf_en=False, name=name).sha("v3")
    op = DveOp(name, spec, subdim=False, uops_sha={"v3": sha}, reference=reference)
    if not any(o.name == name for o in dve_ops.OPS):
        dve_ops.OPS.append(op)
        dve_ops._SUB_OPCODE_FOR_NAME[name] = row
        dve_ops.CUSTOM_DVE_SPECS[name] = spec
        dve_ops.CUSTOM_DVE_REFS[name] = op.ref
    return op


def _register_ops():
    if _OPS:
        return
    b4 = ((Src0 >= C0) + (Src0 >= C1)) + ((Src0 >= C2) + (Src0 >= C0 * C2))

    def r4(in0, in1, s0, s1, imm2):
        t = _fdt(in0); x = in0.astype(t)
        return ((x >= s0).astype(t) + (x >= s1).astype(t)
                + (x >= imm2).astype(t) + (x >= s0 * imm2).astype(t))

    def r4b(in0, in1, s0, s1, imm2):
        return r4(in0, in1, s0, s1, imm2) + in1.astype(_fdt(in0))

    def r1b(in0, in1, s0, s1, imm2):
        t = _fdt(in0); x = in0.astype(t)
        return (x >= s0).astype(t) + in1.astype(t)

    t = Src0 * Src1 + C0
    t = t - (t >= C1) * C1
    t = t - (t >= C2) * C2

    def rmp2(in0, in1, s0, s1, imm2):
        tt = _fdt(in0)
        x = in0.astype(tt) * in1.astype(tt) + s0
        x = x - (x >= s1).astype(tt) * s1
        return x - (x >= imm2).astype(tt) * imm2

    p = Src0 + C0
    p = p - (p >= C1) * C1
    p = p - (p >= C2) * C2

    def rp2(in0, in1, s0, s1, imm2):
        tt = _fdt(in0)
        x = in0.astype(tt) + s0
        x = x - (x >= s1).astype(tt) * s1
        return x - (x >= imm2).astype(tt) * imm2

    q = Src0
    q = q - (q >= C0) * C0
    qw = (q - C1) * Src1

    def rpw(in0, in1, s0, s1, imm2):
        tt = _fdt(in0)
        x = in0.astype(tt)
        x = x - (x >= s0).astype(tt) * s0
        return (x - s1) * in1.astype(tt)

    qe = Src0
    qe = qe - (qe >= C0) * C0
    qwe = (qe - C1 + eq(Src0, C0)) * Src1

    def rpwe(in0, in1, s0, s1, imm2):
        tt = _fdt(in0)
        x = in0.astype(tt)
        r = x - (x >= s0).astype(tt) * s0
        return (r - s1 + (x == s0).astype(tt)) * in1.astype(tt)

    b0t = Src0 >= C0
    qwt = (((Src0 - b0t * C0) - C1) + (b0t - (Src0 >= C2))) * Src1

    def rpwt(in0, in1, s0, s1, imm2):
        tt = _fdt(in0)
        x = in0.astype(tt)
        r = x - (x >= s0).astype(tt) * s0
        return (r - s1 + (x >= s0).astype(tt) * (x < imm2).astype(tt)) \
            * in1.astype(tt)

    r_ = Src0
    r_ = r_ - (r_ >= C0) * C0
    r_ = r_ - (r_ >= C2) * C2
    rw = (r_ - C1) * Src1

    def rp2w(in0, in1, s0, s1, imm2):
        tt = _fdt(in0)
        x = in0.astype(tt)
        x = x - (x >= s0).astype(tt) * s0
        x = x - (x >= imm2).astype(tt) * imm2
        return (x - s1) * in1.astype(tt)

    _OPS["LAD4"] = _reg("QLAD4", b4, r4, 30)
    _OPS["LAD4B"] = _reg("QLAD4B", b4 + Src1, r4b, 31)
    _OPS["LAD1B"] = _reg("QLAD1B", (Src0 >= C0) + Src1, r1b, 1)
    _OPS["MULPEEL2"] = _reg("QMULPEEL2", t, rmp2, 2)
    _OPS["PEEL2"] = _reg("QPEEL2", p, rp2, 3)
    _OPS["PEELW"] = _reg("QPEELW", qw, rpw, 4)
    _OPS["PEEL2W"] = _reg("QPEEL2W", rw, rp2w, 5)
    _OPS["PEELWE"] = _reg("QPEELWE", qwe, rpwe, 6)

    hv1 = select(Src0 >= C0, Src1 * C2, Src1)
    hv2 = select(Src0 >= C1, hv1 * C2, hv1)

    def rhalve2(in0, in1, s0, s1, imm2):
        tt = _fdt(in0)
        x = in0.astype(tt)
        t = np.where(x >= s0, in1.astype(tt) * imm2, in1.astype(tt))
        return np.where(x >= s1, t * imm2, t)

    sp = Src0 * C0 + C1
    sp = sp - (sp >= C2) * C2

    def rscpeel(in0, in1, s0, s1, imm2):
        tt = _fdt(in0)
        u = in0.astype(tt) * s0 + s1
        return u - (u >= imm2).astype(tt) * imm2

    _OPS["HALVE2"] = _reg("QHALVE2", hv2, rhalve2, 7)
    _OPS["SCPEEL"] = _reg("QSCPEEL", sp, rscpeel, 8)
    _OPS["PEELWET"] = _reg("QPEELWET", qwt, rpwt, 9)

    pc = Src0 + C0
    pc = pc - (pc >= C1) * C1
    pc = pc - (pc >= C2) * C1

    def rp2c(in0, in1, s0, s1, imm2):
        tt = _fdt(in0)
        x = in0.astype(tt) + s0
        x = x - (x >= s1).astype(tt) * s1
        return x - (x >= imm2).astype(tt) * s1

    _OPS["PEEL2C"] = _reg("QPEEL2C", pc, rp2c, 10)


_register_ops()

_HI = float(np.float32(LN2))
_LO = float(np.float32(LN2 - _HI))
_B1_8 = float(np.float32(9 * _HI))
_B2_8 = float(np.float32(9 * LN2 - _B1_8))
_B1_16 = float(np.float32(24 * _HI))
_B2_16 = float(np.float32(24 * LN2 - _B1_16))
CST = {
    "is8": 9 * LN2, "sc8": -9 * LN2, "is16": 24 * LN2, "sc16": -24 * LN2,
    "pln2": LN2, "nln2": -LN2, "eighth": 0.125, "negu": -USOFT, "eps": EPS,
    "neghalf": -0.5, "one": 1.0,
    "nhi": -_HI, "phi": _HI, "nlo": -_LO, "plo": _LO,
    "i8h": _B1_8, "i8l": _B2_8, "s8h": -_B1_8, "s8l": -_B2_8,
    "i16h": _B1_16, "i16l": _B2_16, "s16h": -_B1_16, "s16l": -_B2_16,
}
CSTI = {k: i for i, k in enumerate(CST)}
NCST = len(CST)

ENGINES = ("sync", "tensor", "vector", "scalar", "gpsimd")
SEMS = ("din", "dg", "dz", "dwr", "dout", "t", "v", "s", "g")


class Prog:

    def __init__(self):
        self.counts = {sn: 0 for sn in SEMS}
        self.ops = {e: [] for e in ENGINES}
        self.known = {e: {} for e in ENGINES}

    def now(self, name):
        return self.counts[name]

    def op(self, eng, emit, wait=(), inc=()):
        w = []
        for sname, val in wait:
            if val <= 0:
                continue
            if self.known[eng].get(sname, 0) >= val:
                continue
            self.known[eng][sname] = val
            w.append((sname, val))
        incs = []
        for sname, amt in inc:
            self.counts[sname] += amt
            incs.append((sname, amt))
            self.known[eng][sname] = max(self.known[eng].get(sname, 0),
                                         self.counts[sname])
        self.ops[eng].append([w, emit, incs])

    def bump(self, eng, sname, amt=1):
        self.counts[sname] += amt
        self.ops[eng][-1][2].append((sname, amt))
        self.known[eng][sname] = max(self.known[eng].get(sname, 0),
                                     self.counts[sname])
        return self.counts[sname]


def build_nc(debug=None):
    nc = bass.Bass("TRN2", target_bir_lowering=False, detect_race_conditions=False)

    xin = nc.dram_tensor("xin", [S, D], F32, kind="ExternalInput")
    wqp = nc.dram_tensor("wqp", [NP, D, 128], F32, kind="ExternalInput")
    wkp = nc.dram_tensor("wkp", [NP, D, 128], F32, kind="ExternalInput")
    wvp = nc.dram_tensor("wvp", [NP, D, 128], F32, kind="ExternalInput")
    wop = nc.dram_tensor("wop", [NCH, 128, D], F32, kind="ExternalInput")
    w1t = nc.dram_tensor("w1t", [NHF, NCH, 128, 128], F32, kind="ExternalInput")
    w2t = nc.dram_tensor("w2t", [NHF, 128, D], F32, kind="ExternalInput")
    b1in = nc.dram_tensor("b1in", [128, NHF], F32, kind="ExternalInput")
    boutbc_in = nc.dram_tensor("boutbc_in", [128, D], F32, kind="ExternalInput")
    b2bc_in = nc.dram_tensor("b2bc_in", [128, D], F32, kind="ExternalInput")
    cst_in = nc.dram_tensor("cst_in", [128, NCST], F32, kind="ExternalInput")
    ident_in = nc.dram_tensor("ident_in", [128, 128], BF16, kind="ExternalInput")
    trimask_in = nc.dram_tensor("trimask_in", [128, 128], BF16, kind="ExternalInput")
    bcast64_in = nc.dram_tensor("bcast64_in", [128, 64], BF16, kind="ExternalInput")
    xout = nc.dram_tensor("xout", [S, D], F32, kind="ExternalOutput")
    w2q_dram = nc.dram_tensor("w2q_dram", [NHF, 128, D], FP8, kind="Internal")

    P = Prog()

    with contextlib.ExitStack() as est:
        E = est.enter_context
        sems = {sn: E(nc.semaphore(sn)) for sn in SEMS}

        x_tm = E(nc.sbuf_tensor("x_tm", [128, NT * D], F32))
        ln_fm = E(nc.sbuf_tensor("ln_fm", [128, NCH * S], BF16))
        cstb = E(nc.sbuf_tensor("cstb", [128, NCST], F32))
        ident = E(nc.sbuf_tensor("ident", [128, 128], BF16))
        trimask = E(nc.sbuf_tensor("trimask", [128, 128], BF16))
        bcast64 = E(nc.sbuf_tensor("bcast64", [128, 64], BF16))
        b1_sb = E(nc.sbuf_tensor("b1_sb", [128, NHF], F32))
        boutbc = E(nc.sbuf_tensor("boutbc", [128, D], F32))

        bqbc = E(nc.sbuf_tensor("bqbc", [128, D], F16))
        stat = E(nc.sbuf_tensor("stat", [128, 128], F32))
        scratch = E(nc.sbuf_tensor("scratch", [128, 2048], F32))
        sq = scratch[:, 0:1024]

        scope = {}

        def open_scope(name, specs):
            stk = contextlib.ExitStack()
            scope[name] = stk
            return {nm: stk.enter_context(nc.sbuf_tensor(f"{name}_{nm}", sh, dt))
                    for nm, sh, dt in specs}

        def close_scope(name):
            scope.pop(name).close()

        def close_all_scopes():
            for name in reversed(list(scope)):
                scope.pop(name).close()

        ps_a = E(nc.psum_tensor("ps_a", [128, 512], F32))
        ps_b = E(nc.psum_tensor("ps_b", [128, 512], F32))
        ps_z = E(nc.psum_tensor("ps_z", [128, 512], F32))
        ps_r = E(nc.psum_tensor("ps_r", [128, 512], F32))

        def cc(name, lo=0, hi=128):
            return cstb[lo:hi, CSTI[name]:CSTI[name] + 1]

        def _ld(dst_ap, src_ap):
            return lambda e, d=dst_ap, s=src_ap: e.dma_start(d, s)

        def _ms(ap, val):
            return lambda e, a=ap, vv=val: e.memset(a, vv)

        def ts(out, in0, s1, s2, op0, op1=None):
            if op1 is None:
                return lambda e: e.tensor_scalar(out, in0, s1, None, op0)
            return lambda e: e.tensor_scalar(out, in0, s1, s2, op0, op1)

        def tt(out, in0, in1, op):
            return lambda e: e.tensor_tensor(out, in0, in1, op)

        def sttf(out, in0, sc, in1, op0, op1, accum=None):
            return lambda e: e.scalar_tensor_tensor(out, in0, sc, in1, op0, op1,
                                                    accum_out=accum)

        def act(out, in_, func, bias=0.0, scale=1.0):
            return lambda e: e.activation(out, in_, func, bias=bias, scale=scale)

        def cdve(opn, out, in0, in1=None, **kw):
            if in1 is None:
                return lambda e: e._custom_dve(_OPS[opn], out=out, in0=in0, **kw)
            return lambda e: e._custom_dve(_OPS[opn], out=out, in0=in0, in1=in1, **kw)

        def mm(out, lhsT, rhs, start, stop):
            return lambda e: e.matmul(out, lhsT, rhs, start=start, stop=stop)

        def tpz(out, in_):
            return lambda e: e.transpose(out, in_, ident[:])

        def red(out, in_, op):
            return lambda e: e.tensor_reduce(out, in_, AX.X, op)

        def cpy(out, in_):
            return lambda e: (e.tensor_copy(out, in_) if hasattr(e, "tensor_copy")
                              else e.copy(out, in_))

        P.op("sync", _ld(bass.AP(x_tm, 0, [[NT * D, 128], [D, NT], [1, D]]),
                         bass.AP(xin, 0, [[D, 128], [128 * D, NT], [1, D]])),
             inc=[("din", 16)])
        for src, dst in ((cst_in, cstb), (b1in, b1_sb), (boutbc_in, boutbc),
                         (ident_in, ident), (trimask_in, trimask),
                         (bcast64_in, bcast64)):
            P.op("sync", _ld(dst[:], src[:]), inc=[("din", 16)])
        PRO_DIN = P.now("din")

        if debug == "x0":
            for i in range(NT):
                P.op("sync", _ld(bass.AP(xout, i * 128 * D, [[D, 128], [1, D]]),
                                 x_tm[:, i * D:(i + 1) * D]),
                     wait=[("din", PRO_DIN)], inc=[("dout", 16)])
            P.op("sync", None, wait=[("dout", P.now("dout"))])
            return _emit_and_finish(nc, P, sems)

        qlast = {"v": 0}
        lnlast = {"s": 0}

        L16 = [("LAD4", dict(s0=2.0 ** -13, s1=2.0 ** -12, imm2=8.0)),
               ("LAD4B", dict(s0=2.0 ** -11, s1=2.0 ** -8, imm2=4.0)),
               ("LAD4B", dict(s0=2.0 ** -7, s1=2.0 ** -4, imm2=2.0)),
               ("LAD4B", dict(s0=2.0 ** -2, s1=2.0 ** -5, imm2=2.0 ** -1)),
               ("LAD1B", dict(s0=1.0))]
        L8 = [("LAD4", dict(s0=2.0 ** -5, s1=2.0 ** -4, imm2=4.0)),
              ("LAD4B", dict(s0=2.0 ** -2, s1=1.0, imm2=2.0))]
        L8X = L8 + [("LAD4B", dict(s0=8.0, s1=16.0, imm2=32.0))]

        ETA = 2.0 ** -22
        CATCH = 2.0 - ETA

        qlastb = {}

        def walk_quant2(bufs, x_ap, n, man16, signed, xwaits, out_slices,
                        ext=False, he=False):
            wabs_, wsgn_, wacc_, wis_, wsc_, wlo_, key = bufs
            lads = L16 if man16 else (L8X if ext else L8)
            ih, il = ("i16h", "i16l") if man16 else ("i8h", "i8l")
            sh, sl_ = ("s16h", "s16l") if man16 else ("s8h", "s8l")
            guard = [("v", qlastb.get(key, 0))]
            if signed:
                P.op("scalar", act(wabs_, x_ap, AF.Abs),
                     wait=list(xwaits) + guard)
                P.op("scalar", act(wsgn_, x_ap, AF.Sign))
                sg = P.bump("scalar", "s")
                ax_ap = wabs_
                lw = [("s", sg)]
            else:
                ax_ap = x_ap
                lw = list(xwaits) + guard
            for i, (opn, kw) in enumerate(lads):
                if i == 0:
                    P.op("vector", cdve(opn, wacc_, ax_ap, **kw), wait=lw)
                else:
                    P.op("vector", cdve(opn, wacc_, ax_ap, wacc_, **kw))
            vw = P.bump("vector", "v")
            P.op("scalar", act(wis_, wacc_, AF.Exp,
                               bias=cc(ih), scale=cc("nhi")), wait=[("v", vw)])
            P.op("scalar", act(wlo_, wacc_, AF.Exp,
                               bias=cc(il), scale=cc("nlo")))
            P.op("scalar", act(wsc_, wacc_, AF.Exp,
                               bias=cc(sh), scale=cc("phi")))
            P.op("scalar", act(wacc_, wacc_, AF.Exp,
                               bias=cc(sl_), scale=cc("plo")))
            sw = P.bump("scalar", "s")
            P.op("gpsimd", tt(wis_, wis_, wlo_, ALU.mult), wait=[("s", sw)])
            P.op("gpsimd", tt(wsc_, wsc_, wacc_, ALU.mult))
            gw = P.bump("gpsimd", "g")

            def finish():
                nh = (n + 511) // 512
                for hh in range(nh):
                    lo, hi = hh * 512, min(n, hh * 512 + 512)
                    m = hi - lo
                    sl = slice(lo, hi)
                    w = [("g", gw)] if hh == 0 else ()
                    if man16:
                        P.op("vector", cdve("MULPEEL2", scratch[:, 0:m], ax_ap[:, sl],
                                            wis_[:, sl], s0=0.5, s1=2048.0, imm2=1024.0),
                             wait=w)
                        P.op("vector", cdve("PEEL2", scratch[:, 512:512 + m], scratch[:, 0:m],
                                            s0=0.0, s1=512.0, imm2=256.0))
                        P.op("vector", cdve("PEEL2", scratch[:, 0:m], scratch[:, 512:512 + m],
                                            s0=0.0, s1=128.0, imm2=64.0))
                        P.op("vector", cdve("PEEL2", scratch[:, 512:512 + m], scratch[:, 0:m],
                                            s0=0.0, s1=32.0, imm2=16.0))
                        P.op("vector", cdve("PEEL2", scratch[:, 0:m], scratch[:, 512:512 + m],
                                            s0=0.0, s1=8.0, imm2=4.0))
                        if he:
                            P.op("vector", cdve("PEEL2C", scratch[:, 512:512 + m],
                                                scratch[:, 0:m], s0=0.0, s1=2.0,
                                                imm2=CATCH))
                            P.op("vector", cdve("PEELWET", scratch[:, 1024:1024 + m],
                                                scratch[:, 512:512 + m], wsc_[:, sl],
                                                s0=1.0, s1=0.5, imm2=1.0 + ETA))
                        else:
                            P.op("vector", cdve("PEEL2W", scratch[:, 1024:1024 + m],
                                                scratch[:, 0:m],
                                                wsc_[:, sl], s0=2.0, s1=0.5, imm2=1.0))
                    else:
                        P.op("vector", cdve("MULPEEL2", scratch[:, 0:m], ax_ap[:, sl],
                                            wis_[:, sl], s0=0.5, s1=16.0, imm2=8.0),
                             wait=w)
                        P.op("vector", cdve("PEEL2", scratch[:, 512:512 + m], scratch[:, 0:m],
                                            s0=0.0, s1=4.0, imm2=2.0))
                        if he:
                            P.op("vector", cdve("PEEL2C", scratch[:, 0:m],
                                                scratch[:, 512:512 + m], s0=0.0, s1=2.0,
                                                imm2=CATCH))
                            P.op("vector", cdve("PEELWET", scratch[:, 1024:1024 + m],
                                                scratch[:, 0:m], wsc_[:, sl],
                                                s0=1.0, s1=0.5, imm2=1.0 + ETA))
                        else:
                            P.op("vector", cdve("PEELW", scratch[:, 1024:1024 + m],
                                                scratch[:, 512:512 + m],
                                                wsc_[:, sl], s0=1.0, s1=0.5))
                    if signed:
                        P.op("vector", tt(scratch[:, 1536:1536 + m], ax_ap[:, sl],
                                          scratch[:, 1024:1024 + m], ALU.subtract))
                        P.op("vector", tt(out_slices[hh], scratch[:, 1536:1536 + m],
                                          wsgn_[:, sl], ALU.mult))
                    else:
                        P.op("vector", tt(out_slices[hh], x_ap[:, sl],
                                          scratch[:, 1024:1024 + m], ALU.subtract))
                vend = P.bump("vector", "v")
                qlastb[key] = vend
                qlast["v"] = vend
                return vend
            return finish

        def walk_quant(x_ap, n, man16, signed, xwaits, out_slices, ext=False,
                       he=False):
            bufs = (wabs[:, 0:n], wsgn[:, 0:n], wacc[:, 0:n], wis[:, 0:n],
                    wsc[:, 0:n], wlo[:, 0:n], "serial")
            qlastb["serial"] = qlast["v"]
            return walk_quant2(bufs, x_ap, n, man16, signed, xwaits,
                               out_slices, ext=ext, he=he)()

        def walk_ln_stats_apply(src_t, src_ofs, dst_ap, xwaits):
            full = src_t[:, src_ofs:src_ofs + D]
            h0 = src_t[:, src_ofs:src_ofs + 512]
            h1 = src_t[:, src_ofs + 512:src_ofs + 1024]
            P.op("vector", red(stat[:, 112:113], full, ALU.add),
                 wait=list(xwaits) + [("s", lnlast["s"])])
            P.op("vector", tt(sq[:, 0:512], h0, h0, ALU.mult))
            P.op("vector", tt(sq[:, 512:1024], h1, h1, ALU.mult))
            P.op("vector", red(stat[:, 113:114], sq[:], ALU.add))
            vw = P.bump("vector", "v")
            P.op("scalar", act(stat[:, 115:116], stat[:, 112:113], AF.Identity,
                               scale=-1.0 / D), wait=[("v", vw)])
            P.op("scalar", act(stat[:, 116:117], stat[:, 113:114], AF.Identity,
                               scale=1.0 / D, bias=cc("eps")))
            P.op("scalar", act(stat[:, 122:123], stat[:, 112:113], AF.Identity))
            P.op("scalar", act(stat[:, 123:124], stat[:, 112:113], AF.Identity))
            P.op("scalar", act(stat[:, 124:125], stat[:, 112:113], AF.Identity))
            P.op("scalar", act(stat[:, 117:118], stat[:, 112:113], AF.Square,
                               scale=1.0 / D))
            P.op("scalar", act(stat[:, 118:119], stat[:, 117:118], AF.Ln, scale=-1.0,
                               bias=stat[:, 116:117]))
            P.op("scalar", act(stat[:, 119:120], stat[:, 118:119], AF.Exp,
                               scale=cc("neghalf")))
            sw = P.bump("scalar", "s")
            va = None
            if dst_ap is not None:
                P.op("vector", ts(dst_ap, full, stat[:, 115:116], stat[:, 119:120],
                                  ALU.add, ALU.mult), wait=[("s", sw)])
                va = P.bump("vector", "v")
            lnlast["s"] = sw
            return sw, va

        def walk_ln_phase(ln_tm, ps_tr, xwaits_fn):
            cS, cQ, cM3, cMQ, cM2, cL, cR = 0, 16, 32, 48, 64, 80, 96
            for i in range(NT):
                full = x_tm[:, i * D:(i + 1) * D]
                h0 = x_tm[:, i * D:i * D + 512]
                h1 = x_tm[:, i * D + 512:(i + 1) * D]
                w = list(xwaits_fn(i))
                if i == 0:
                    w += [("s", lnlast["s"])]
                P.op("vector", red(stat[:, cS + i:cS + i + 1], full, ALU.add),
                     wait=w)
                P.op("vector", tt(sq[:, 0:512], h0, h0, ALU.mult))
                P.op("vector", tt(sq[:, 512:1024], h1, h1, ALU.mult))
                P.op("vector", red(stat[:, cQ + i:cQ + i + 1], sq[:], ALU.add))
            vw = P.bump("vector", "v")
            for i in range(NT):
                P.op("scalar", act(stat[:, cM3 + i:cM3 + i + 1],
                                   stat[:, cS + i:cS + i + 1],
                                   AF.Identity, scale=-1.0 / D),
                     wait=[("v", vw)] if i == 0 else ())
            for i in range(NT):
                P.op("scalar", act(stat[:, cMQ + i:cMQ + i + 1],
                                   stat[:, cQ + i:cQ + i + 1],
                                   AF.Identity, scale=1.0 / D, bias=cc("eps")))
            for i in range(NT):
                P.op("scalar", act(stat[:, cM2 + i:cM2 + i + 1],
                                   stat[:, cS + i:cS + i + 1],
                                   AF.Square, scale=1.0 / D))
            for i in range(NT):
                P.op("scalar", act(stat[:, cL + i:cL + i + 1],
                                   stat[:, cM2 + i:cM2 + i + 1],
                                   AF.Ln, scale=-1.0,
                                   bias=stat[:, cMQ + i:cMQ + i + 1]))
            for i in range(NT):
                P.op("scalar", act(stat[:, cR + i:cR + i + 1],
                                   stat[:, cL + i:cL + i + 1],
                                   AF.Exp, scale=cc("neghalf")))
            sw = P.bump("scalar", "s")
            lnlast["s"] = sw
            for i in range(NT):
                P.op("vector", ts(ln_tm[:, i * D:(i + 1) * D],
                                  x_tm[:, i * D:(i + 1) * D],
                                  stat[:, cM3 + i:cM3 + i + 1],
                                  stat[:, cR + i:cR + i + 1],
                                  ALU.add, ALU.mult),
                     wait=[("s", sw)] if i == 0 else ())
                va = P.bump("vector", "v")
                for half in range(2):
                    for cq in range(4):
                        c = half * 4 + cq
                        P.op("tensor",
                             tpz(ps_tr[:, cq * 128:(cq + 1) * 128],
                                 ln_tm[:, i * D + c * 128: i * D + (c + 1) * 128]),
                             wait=[("v", va), ("s", P.now("s"))] if cq == 0 else [])
                    tw = P.bump("tensor", "t")
                    for cq in range(4):
                        c = half * 4 + cq
                        P.op("scalar",
                             cpy(ln_fm[:, c * S + i * 128: c * S + (i + 1) * 128],
                                 ps_tr[:, cq * 128:(cq + 1) * 128]),
                             wait=[("t", tw)] if cq == 0 else [])
                    se = P.bump("scalar", "s")
                    P.op("tensor", None, wait=[("s", se)])
            return P.now("s")

        zs = open_scope("zscope", [("zfm", [128, NCH * S], BF16)])
        zfm = zs["zfm"]
        lt = open_scope("lntm", [("ln_tm", [128, NT * D], BF16)])
        ps_tr = scope["lntm"].enter_context(nc.psum_tensor("ps_tr1", [128, 512], BF16))
        LN1_S = walk_ln_phase(lt["ln_tm"], ps_tr, lambda i: [("din", PRO_DIN)])
        close_scope("lntm")
        if debug == "ln1":
            P.op("gpsimd", _ld(bass.AP(xout, 0, [[16384, 128], [1, 16384]]),
                               ln_fm[:]),
                 wait=[("s", LN1_S)], inc=[("dout", 16)])
            P.op("sync", None, wait=[("dout", P.now("dout"))])
            close_all_scopes()
            return _emit_and_finish(nc, P, sems)
        cbufs = open_scope("chain", [("wabs", [128, D], F32), ("wsgn", [128, D], BF16),
                                     ("wacc", [128, D], F32), ("wis", [128, D], F32),
                                     ("wsc", [128, D], F32), ("wlo", [128, D], F32)],
)
        wabs, wsgn, wacc, wis, wsc, wlo = (cbufs[n] for n in
                                           ("wabs", "wsgn", "wacc", "wis", "wsc", "wlo"))
        wsetA = (wabs[:], wsgn[:], wacc[:], wis[:], wsc[:], wlo[:], "w2A")
        wb = open_scope("w2st", [("w2st", [128, D], F32),
                                 ("w2qs", [128, D], FP8)])
        w2st, w2qs = wb["w2st"], wb["w2qs"]
        att = open_scope("attn", [("qh", [128, 2 * S], BF16), ("kh", [128, 2 * S], BF16),
                                  ("wstr", [128, 3 * 1024], BF16),
                                  ("vpair", [128, NT * 130], BF16),
                                  ("probsT", [128, 4 * GQ], BF16),
                                  ("zstage", [64, S], BF16),
                                  ("recst", [128, GQ], BF16),
                                  ("recsb", [64, GQ], BF16)])
        qh, kh, wstr, vpair, probsT, zstage, recst, recsb = (
            att[n] for n in ("qh", "kh", "wstr", "vpair", "probsT", "zstage",
                             "recst", "recsb"))
        ps_s = scope["attn"].enter_context(nc.psum_tensor("ps_s", [128, 2048], F32))
        T_LN1 = P.now("t")
        zevacv = [0]
        wpend = [None]
        P.op("vector", _ms(qh[64:128, 0:S], 0.0),
             wait=[("din", PRO_DIN), ("t", T_LN1)])
        P.op("vector", _ms(qh[0:64, S:2 * S], 0.0))
        P.op("vector", _ms(kh[64:128, 0:S], 0.0))
        P.op("vector", _ms(kh[0:64, S:2 * S], 0.0))
        P.op("vector", _ms(vpair[:], 1.0))
        P.op("vector", _ms(recst[:], 0.0))
        INIT_V = P.bump("vector", "v")

        for p in range(NP):
            base = 0
            for idx, wsrc in enumerate((wqp, wkp, wvp)):
                P.op("gpsimd",
                     _ld(bass.AP(wstr, base + idx * 1024,
                                 [[3 * 1024, 128], [128, NCH], [1, 128]]),
                         bass.AP(wsrc, p * D * 128,
                                 [[128, 128], [128 * 128, NCH], [1, 128]])),
                     wait=[("t", P.now("t"))] if p >= 1 else [("t", T_LN1)],
                     inc=[("dg", 16)])
            DG_P = P.now("dg")

            evs = {}
            for which, wof, dst, scale in (("q", 0, qh, True), ("k", 1024, kh, False)):
                for g4 in range(4):
                    ps = ps_a if g4 % 2 == 0 else ps_b
                    for c in range(NCH):
                        P.op("tensor",
                             mm(ps[:, 0:512],
                                wstr[:, base + wof + c * 128: base + wof + (c + 1) * 128],
                                ln_fm[:, c * S + g4 * 512: c * S + (g4 + 1) * 512],
                                start=(c == 0), stop=(c == NCH - 1)),
                             wait=([("dg", DG_P), ("s", P.now("s"))] if c == 0 else []))
                    tw = P.bump("tensor", "t")
                    col = g4 * 512
                    nm = "eighth" if scale else "one"
                    P.op("scalar", act(dst[0:64, col:col + 512], ps[0:64, 0:512],
                                       AF.Copy, scale=cc(nm, 0, 64)),
                         wait=[("t", tw)])
                    P.op("scalar", act(dst[64:128, S + col:S + col + 512],
                                       ps[64:128, 0:512], AF.Copy, scale=cc(nm, 64, 128)))
                    P.bump("scalar", "s")
            for i in range(NT):
                ps = ps_a if i % 2 == 0 else ps_b
                for c in range(NCH):
                    P.op("tensor",
                         mm(ps[:, 0:128],
                            ln_fm[:, c * S + i * 128: c * S + (i + 1) * 128],
                            wstr[:, base + 2048 + c * 128: base + 2048 + (c + 1) * 128],
                            start=(c == 0), stop=(c == NCH - 1)),
                         wait=[("s", P.now("s"))] if c == 0 else [])
                tw = P.bump("tensor", "t")
                P.op("scalar",
                     act(bass.AP(vpair, i * 130, [[NT * 130, 128], [65, 2], [1, 64]]),
                         bass.AP(ps, 0, [[512, 128], [64, 2], [1, 64]]),
                         AF.Copy),
                     wait=[("t", tw)])
                P.bump("scalar", "s")
            VS = P.now("s")

            for hloc in range(2):
                qcol = hloc * S
                for g in range(NG):
                    njc = 4 * g + 4
                    for jb in range(0, njc, 4):
                        jhi = jb + 4
                        diag = (jhi == njc)
                        for j in range(jb, jhi):
                            if j == jb:
                                jw = ([("s", VS), ("v", INIT_V)] if jb == 0
                                      else [("s", P.now("s"))])
                            else:
                                jw = []
                            P.op("tensor",
                                 mm(ps_s[:, (j - jb) * 512:(j - jb + 1) * 512],
                                    kh[:, qcol + j * 128: qcol + (j + 1) * 128],
                                    qh[:, qcol + g * GQ: qcol + (g + 1) * GQ],
                                    start=True, stop=True),
                                 wait=jw)
                        tw = P.bump("tensor", "t")
                        P.op("scalar",
                             act(probsT[:, 0: 4 * GQ],
                                 ps_s[:, 0:2048], AF.Exp,
                                 bias=cc("negu")),
                             wait=[("t", tw)])
                        se = P.bump("scalar", "s")
                        pvw = [("s", se)]
                        if diag:
                            for dj in range(4):
                                o = 128 * dj
                                if o > 0:
                                    P.op("vector",
                                         _ms(probsT[:, dj * GQ: dj * GQ + o], 0.0),
                                         wait=[("s", se)])
                                P.op("vector",
                                     tt(probsT[:, dj * GQ + o: dj * GQ + o + 128],
                                        probsT[:, dj * GQ + o: dj * GQ + o + 128],
                                        trimask[:], ALU.mult),
                                     wait=[("s", se)])
                            VMASK = P.bump("vector", "v")
                            pvw = [("v", VMASK)]
                        for j in range(jb, jhi):
                            w = []
                            if j == jb:
                                w = list(pvw)
                            if j == 0:
                                w.append(("v", zevacv[0]))
                            P.op("tensor",
                                 mm(ps_z[0:65, 0:GQ],
                                    bass.AP(vpair, j * 130 + hloc * 65,
                                            [[NT * 130, 128], [1, 65]]),
                                    probsT[:, (j - jb) * GQ:(j - jb + 1) * GQ],
                                    start=(j == 0), stop=(j == njc - 1)),
                                 wait=w)
                    TPV = P.bump("tensor", "t")
                    def _recip(e):
                        with nc.allow_low_precision(reason="recip feeds bf16 mm operand"):
                            return e.reciprocal(recst[0:1, 0:GQ], ps_z[64:65, 0:GQ])
                    P.op("vector", _recip, wait=[("t", TPV)])
                    VREC = P.bump("vector", "v")
                    P.op("tensor", mm(ps_r[0:64, 0:GQ], bcast64[:], recst[:, 0:GQ],
                                      start=True, stop=True),
                         wait=[("v", VREC), ("s", P.now("s"))])
                    TREC = P.bump("tensor", "t")
                    P.op("vector", cpy(recsb[0:64, 0:GQ], ps_r[0:64, 0:GQ]),
                         wait=[("t", TREC)])
                    SREC = P.bump("vector", "v")
                    if hloc == 0:
                        zdst = zfm[0:64, p * S + g * GQ: p * S + (g + 1) * GQ]
                    else:
                        zdst = zstage[0:64, g * GQ:(g + 1) * GQ]
                    zw = []
                    if hloc == 1 and g == 0:
                        zw.append(("dz", P.now("dz")))
                    P.op("vector", tt(zdst, ps_z[0:64, 0:GQ], recsb[0:64, 0:GQ],
                                      ALU.mult), wait=zw)
                    zevacv[0] = P.bump("vector", "v")
                if hloc == 1:
                    P.op("sync", _ld(zfm[64:128, p * S:(p + 1) * S], zstage[0:64, 0:S]),
                         wait=[("v", P.now("v"))], inc=[("dz", 16)])
            if p == 0:
                P.op("sync", _ld(w2st[:], b2bc_in[:]),
                     wait=[("t", T_LN1)], inc=[("din", 16)])
                walk_quant(w2st[:], D, True, True,
                           [("din", P.now("din"))],
                           [bqbc[:, 0:512], bqbc[:, 512:1024]], he=True)
                qlastb["wh0"] = qlast["v"]
                qlastb["wh1"] = qlast["v"]
            for c in range(4 * p, 4 * p + 4):
                lw = ([("t", T_LN1), ("s", P.now("s"))] if c == 0
                      else [("s", P.now("s")), ("dwr", 16 * max(c - 1, 0))])
                P.op("sync", _ld(w2st[:],
                                 bass.AP(w2t, c * 128 * D, [[D, 128], [1, D]])),
                     wait=lw, inc=[("din", 16)])
                din_c = P.now("din")
                for h2 in range(2):
                    o = h2 * 512
                    wbset = (wabs[:, o:o + 512], wsgn[:, o:o + 512],
                             wacc[:, o:o + 512], wis[:, o:o + 512],
                             wsc[:, o:o + 512], wlo[:, o:o + 512], f"wh{h2}")
                    fin = walk_quant2(wbset, w2st[:, o:o + 512], 512, False,
                                      True, [("din", din_c)],
                                      [w2qs[:, o:o + 512]], he=True)

                    def wfin(f=fin, c=c, h2=h2):
                        if h2 == 0 and c >= 1:
                            P.op("vector", None, wait=[("dwr", 16 * c)])
                        vend = f()
                        if h2 == 1:
                            P.op("gpsimd",
                                 _ld(bass.AP(w2q_dram, c * 128 * D,
                                             [[D, 128], [1, D]]), w2qs[:]),
                                 wait=[("v", vend)], inc=[("dwr", 16)])
                    if wpend[0] is not None:
                        wpend[0]()
                    wpend[0] = wfin
        wpend[0]()
        wpend[0] = None
        W2_DWR = P.now("dwr")
        if debug in ("w2q", "w2qb"):
            cofs = 0 if debug == "w2q" else 16
            for cx in range(16):
                c = cx + cofs
                P.op("sync", _ld(w2qs[:],
                                 bass.AP(w2q_dram, c * 128 * D, [[D, 128], [1, D]])),
                     wait=[("dwr", W2_DWR), ("dg", P.now("dg"))], inc=[("din", 16)])
                P.op("gpsimd", _ld(bass.AP(xout, cx * 1024,
                                           [[16384, 128], [1, 1024]]),
                                   w2qs[:]),
                     wait=[("din", P.now("din"))], inc=[("dg", 16)])
            P.op("sync", None, wait=[("dg", P.now("dg"))])
            close_all_scopes()
            return _emit_and_finish(nc, P, sems)
        if debug == "bq":
            P.op("sync", _ld(bass.AP(xout, 0, [[D, 128], [1, D]]), bqbc[:]),
                 wait=[("v", P.now("v"))], inc=[("dout", 16)])
            P.op("sync", None, wait=[("dout", P.now("dout"))])
            close_all_scopes()
            return _emit_and_finish(nc, P, sems)
        ATT_DZ = P.now("dz")
        ATT_V = P.now("v")
        close_scope("attn")
        close_scope("w2st")
        close_scope("chain")
        ob = open_scope("outp", [("woutb", [128, NCH * D], BF16),
                                 ("attn_t", [128, D], F32)])
        woutb, attn_t = ob["woutb"], ob["attn_t"]

        P.op("gpsimd", _ld(bass.AP(woutb, 0, [[NCH * D, 128], [D, NCH], [1, D]]),
                           bass.AP(wop, 0, [[D, 128], [128 * D, NCH], [1, D]])),
             wait=[("t", P.now("t")), ("v", ATT_V), ("s", P.now("s"))],
             inc=[("dg", 16)])
        WOUT_DG = P.now("dg")

        op_evacv = {}
        S_ATT = P.now("s")
        for i in range(NT):
            for half in range(2):
                ps = ps_a if half == 0 else ps_b
                for c in range(NCH):
                    if c == 0:
                        if half in op_evacv:
                            mmw = [("v", op_evacv[half])]
                        else:
                            mmw = [("dg", WOUT_DG), ("dz", ATT_DZ),
                                   ("v", ATT_V), ("s", S_ATT)]
                    else:
                        mmw = []
                    P.op("tensor",
                         mm(ps[:, 0:512],
                            zfm[:, c * S + i * 128: c * S + (i + 1) * 128],
                            woutb[:, c * D + half * 512: c * D + half * 512 + 512],
                            start=(c == 0), stop=(c == NCH - 1)),
                         wait=mmw)
                tw = P.bump("tensor", "t")
                P.op("vector",
                     sttf(attn_t[:, half * 512: half * 512 + 512], ps[:, 0:512],
                          1.0, boutbc[:, half * 512: half * 512 + 512],
                          ALU.mult, ALU.add),
                     wait=[("t", tw)])
                op_evacv[half] = P.bump("vector", "v")
            sw, _ = walk_ln_stats_apply(attn_t, 0, None, [])
            P.op("vector", ts(attn_t[:], attn_t[:], stat[:, 115:116], stat[:, 119:120],
                              ALU.add, ALU.mult), wait=[("s", sw)])
            P.op("vector", tt(x_tm[:, i * D:(i + 1) * D],
                              x_tm[:, i * D:(i + 1) * D], attn_t[:], ALU.add))
            P.bump("vector", "v")
        OUTPROJ_V = P.now("v")
        close_scope("outp")
        close_scope("zscope")

        if debug == "x2":
            for i in range(NT):
                P.op("sync", _ld(bass.AP(xout, i * 128 * D, [[D, 128], [1, D]]),
                                 x_tm[:, i * D:(i + 1) * D]),
                     wait=[("v", OUTPROJ_V)], inc=[("dout", 16)])
            P.op("sync", None, wait=[("dout", P.now("dout"))])
            debug = "STOP"

        if debug == "STOP":
            close_all_scopes()
            return _emit_and_finish(nc, P, sems)
        lt2 = open_scope("lntm2", [("ln_tm", [128, NT * D], BF16)])
        ps_tr = scope["lntm2"].enter_context(nc.psum_tensor("ps_tr2", [128, 512], BF16))
        T_OUTP = P.now("t")
        LN2_S = walk_ln_phase(lt2["ln_tm"], ps_tr,
                              lambda i: [("v", OUTPROJ_V), ("t", T_OUTP)])
        close_scope("lntm2")
        if debug == "ln2":
            P.op("gpsimd", _ld(bass.AP(xout, 0, [[16384, 128], [1, 16384]]),
                               ln_fm[:]),
                 wait=[("s", LN2_S)], inc=[("dout", 16)])
            P.op("sync", None, wait=[("dout", P.now("dout"))])
            close_all_scopes()
            return _emit_and_finish(nc, P, sems)
        fbufs = open_scope("ffn", [("hq8", [128, NHF * 1024], FP8),
                                   ("w1str", [128, 2 * 1024], BF16),
                                   ("w2qs", [128, 2 * D], FP8),
                                   ("xr_sb", [128, 1024], F32),
                                   ("ffo", [128, 2 * D], F32),
                                   ("ffq", [128, D], F16)])
        hq8, w1str, w2qs, xr_sb, ffo, ffq = (
            fbufs[n] for n in ("hq8", "w1str", "w2qs", "xr_sb", "ffo", "ffq"))
        cb2 = open_scope("chain2", [("wabs", [128, D], F32), ("wsgn", [128, D], BF16),
                                    ("wacc", [128, D], F32), ("wis", [128, D], F32),
                                    ("wsc", [128, D], F32), ("wlo", [128, D], F32)])
        wabs, wsgn, wacc, wis, wsc, wlo = (cb2[n] for n in
                                           ("wabs", "wsgn", "wacc", "wis", "wsc", "wlo"))

        first_w2_load = [True]
        hpend = [None]
        w1_tc = []
        ffn2_evac = [0]
        ffn1_evacv = {}
        w2_tc = []
        for sg in range(2):
            ffn1_evacv = {}
            col0 = sg * 1024
            for m in range(NHF):
                slot = (m % 2) * 1024
                if len(w1_tc) >= 2:
                    w1w = [("t", w1_tc[-2])]
                elif sg == 0 and m == 0:
                    w1w = [("t", P.now("t")), ("s", LN2_S)]
                else:
                    w1w = [("t", w1_tc[-1] if w1_tc else P.now("t"))]
                P.op("gpsimd",
                     _ld(bass.AP(w1str, slot, [[2 * 1024, 128], [128, NCH], [1, 128]]),
                         bass.AP(w1t, m * NCH * 128 * 128,
                                 [[128, 128], [128 * 128, NCH], [1, 128]])),
                     wait=w1w, inc=[("dg", 16)])
                dgm = P.now("dg")
                for g2 in range(2):
                    ps = ps_a if g2 == 0 else ps_b
                    for c in range(NCH):
                        P.op("tensor",
                             mm(ps[:, 0:512],
                                w1str[:, slot + c * 128: slot + (c + 1) * 128],
                                ln_fm[:, c * S + col0 + g2 * 512:
                                      c * S + col0 + g2 * 512 + 512],
                                start=(c == 0), stop=(c == NCH - 1)),
                             wait=([("dg", dgm),
                                    ("v", ffn1_evacv.get(g2, ffn2_evac[0]))]
                                   if c == 0 else []))
                    tw = P.bump("tensor", "t")
                    hk = m * 2 + g2
                    o = (hk % 2) * 512
                    hb = (wabs[:, o:o + 512], wsgn[:, o:o + 512],
                          wacc[:, o:o + 512], wis[:, o:o + 512],
                          wsc[:, o:o + 512], wlo[:, o:o + 512], f"h{hk % 2}")
                    P.op("vector", ts(xr_sb[:, o:o + 512], ps[:, 0:512],
                                      b1_sb[:, m:m + 1], 0.0, ALU.add, ALU.max),
                         wait=[("t", tw), ("v", qlastb.get(f"h{hk % 2}", 0))])
                    ffn1_evacv[g2] = P.bump("vector", "v")
                    fin = walk_quant2(
                        hb, xr_sb[:, o:o + 512], 512, False, False, [],
                        [hq8[:, m * 1024 + g2 * 512: m * 1024 + g2 * 512 + 512]],
                        ext=True)
                    if hpend[0] is not None:
                        hpend[0]()
                    hpend[0] = fin
                w1_tc.append(P.now("t"))
            if hpend[0] is not None:
                hpend[0]()
                hpend[0] = None
            HQ_V = P.now("v")
            if debug and debug.startswith("ffacc") and sg == 0:
                nacc = int(debug[5:])
                for c in range(nacc):
                    slot = (c % 2) * D
                    P.op("sync", _ld(w2qs[:, slot:slot + D],
                                     bass.AP(w2q_dram, c * 128 * D,
                                             [[D, 128], [1, D]])),
                         wait=[("v", HQ_V), ("dwr", W2_DWR),
                               ("t", P.now("t"))], inc=[("din", 16)])
                    P.op("tensor", mm(ps_a[:, 0:512], hq8[:, c * 1024: c * 1024 + 128],
                                      w2qs[:, slot:slot + 512],
                                      start=(c == 0), stop=(c == nacc - 1)),
                         wait=[("din", P.now("din")), ("v", HQ_V),
                               ("s", P.now("s"))])
                    P.bump("tensor", "t")
                P.op("scalar", cpy(ffo[:, 0:512], ps_a[:, 0:512]),
                     wait=[("t", P.now("t"))])
                s1 = P.bump("scalar", "s")
                P.op("sync", _ld(bass.AP(xout, 0, [[D, 128], [1, 512]]),
                                 ffo[:, 0:512]),
                     wait=[("s", s1)], inc=[("dout", 16)])
                P.op("sync", None, wait=[("dout", P.now("dout"))])
                close_all_scopes()
                return _emit_and_finish(nc, P, sems)
            if debug in ("hqa", "hqb") and sg == 0:
                hofs = 0 if debug == "hqa" else 16384
                P.op("gpsimd", _ld(bass.AP(xout, 0, [[16384, 128], [1, 16384]]),
                                   hq8[:, hofs:hofs + 16384]),
                     wait=[("v", HQ_V)], inc=[("dout", 16)])
                P.op("sync", None, wait=[("dout", P.now("dout"))])
                close_all_scopes()
                return _emit_and_finish(nc, P, sems)
            for stp in range(4):
                sts = (col0 // 128) + stp * 2
                psl = (ps_a, ps_b, ps_z, ps_r)
                for c in range(NHF):
                    slot = (c % 2) * D
                    lw = [("v", HQ_V)]
                    if first_w2_load[0]:
                        lw.append(("dwr", W2_DWR))
                        first_w2_load[0] = False
                    if len(w2_tc) >= 2:
                        lw.append(("t", w2_tc[-2]))
                    P.op("sync", _ld(w2qs[:, slot:slot + D],
                                     bass.AP(w2q_dram, c * 128 * D, [[D, 128], [1, D]])),
                         wait=lw, inc=[("din", 16)])
                    dinc = P.now("din")
                    if debug == "ffo1" and sg == 0 and stp == 0 and c == 0:
                        P.op("gpsimd", _ld(bass.AP(xout, 128 * D, [[D, 128], [1, D]]),
                                           w2qs[:, 0:D]),
                             wait=[("din", dinc)], inc=[("dg", 16)])
                        P.op("gpsimd", _ld(bass.AP(xout, 256 * D, [[D, 128], [1, 128]]),
                                           hq8[:, 0:128]),
                             wait=[("v", HQ_V)], inc=[("dg", 16)])
                        P.op("tensor", mm(ps_a[:, 0:512], hq8[:, 0:128],
                                          w2qs[:, 0:512], start=True, stop=True),
                             wait=[("din", dinc), ("v", HQ_V), ("s", P.now("s"))])
                        t1 = P.bump("tensor", "t")
                        P.op("scalar", cpy(ffo[:, 0:512], ps_a[:, 0:512]),
                             wait=[("t", t1)])
                        s1 = P.bump("scalar", "s")
                        P.op("sync", _ld(bass.AP(xout, 384 * D, [[D, 128], [1, 512]]),
                                         ffo[:, 0:512]),
                             wait=[("s", s1)], inc=[("dout", 16)])
                        P.op("sync", None, wait=[("dout", P.now("dout")),
                                                 ("dg", P.now("dg"))])
                        close_all_scopes()
                        return _emit_and_finish(nc, P, sems)
                    for sti in range(2):
                        for half in range(2):
                            ps = psl[sti * 2 + half]
                            P.op("tensor",
                                 mm(ps[:, 0:512],
                                    hq8[:, c * 1024 + (stp * 2 + sti) * 128:
                                        c * 1024 + (stp * 2 + sti + 1) * 128],
                                    w2qs[:, slot + half * 512: slot + half * 512 + 512],
                                    start=(c == 0), stop=(c == NHF - 1)),
                                 wait=([("din", dinc)]
                                       + ([("v", ffn2_evac[0])]
                                          if c == 0 else []))
                                 if (sti == 0 and half == 0) or c == 0 else [])
                    w2_tc.append(P.bump("tensor", "t"))
                tws = w2_tc[-1]
                if debug == "ffo2" and sg == 0 and stp == 0:
                    P.op("scalar", cpy(ffo[:, 0:512], ps_a[:, 0:512]),
                         wait=[("t", tws)])
                    s1 = P.bump("scalar", "s")
                    P.op("sync", _ld(bass.AP(xout, 0, [[D, 128], [1, 512]]),
                                     ffo[:, 0:512]),
                         wait=[("s", s1)], inc=[("dout", 16)])
                    P.op("sync", None, wait=[("dout", P.now("dout"))])
                    close_all_scopes()
                    return _emit_and_finish(nc, P, sems)
                for sti in range(2):
                    for half in range(2):
                        ps = psl[sti * 2 + half]
                        P.op("vector",
                             sttf(ffo[:, sti * D + half * 512:
                                      sti * D + half * 512 + 512], ps[:, 0:512],
                                  1.0, bqbc[:, half * 512: half * 512 + 512],
                                  ALU.mult, ALU.add),
                             wait=[("t", tws)] if (sti == 0 and half == 0) else ())
                ffn2_evac[0] = P.bump("vector", "v")
                vffo = ffn2_evac[0]
                for sti in range(2):
                    i = sts + sti
                    fslc = ffo[:, sti * D:(sti + 1) * D]
                    if debug == "ffo0" and sti == 0:
                        P.op("sync", _ld(bass.AP(xout, 0, [[D, 128], [1, D]]),
                                         fslc),
                             wait=[("v", vffo)], inc=[("dout", 16)])
                        P.op("sync", None, wait=[("dout", P.now("dout"))])
                        close_all_scopes()
                        return _emit_and_finish(nc, P, sems)
                    qv = walk_quant(fslc, D, True, True, [("v", vffo)],
                                    [ffq[:, 0:512], ffq[:, 512:1024]], he=True)
                    if debug == "ffx" and i == 5:
                        P.op("sync", _ld(bass.AP(xout, 0, [[D, 128], [1, D]]),
                                         fslc),
                             wait=[("v", qv)], inc=[("dout", 16)])
                        P.op("gpsimd", _ld(bass.AP(xout, 128 * D, [[D, 128], [1, D]]),
                                           ffq[:]),
                             wait=[("v", qv)], inc=[("dg", 16)])
                        P.op("sync", None, wait=[("dout", P.now("dout")),
                                                 ("dg", P.now("dg"))])
                        close_all_scopes()
                        return _emit_and_finish(nc, P, sems)
                    sw, _ = walk_ln_stats_apply(ffq, 0, None, [])
                    P.op("vector", ts(fslc, ffq[:], stat[:, 115:116], stat[:, 119:120],
                                      ALU.add, ALU.mult), wait=[("s", sw)])
                    P.op("vector", tt(x_tm[:, i * D:(i + 1) * D],
                                      x_tm[:, i * D:(i + 1) * D], fslc, ALU.add))
                    vres = P.bump("vector", "v")
                    P.op("sync", _ld(bass.AP(xout, i * 128 * D, [[D, 128], [1, D]]),
                                     x_tm[:, i * D:(i + 1) * D]),
                         wait=[("v", vres)], inc=[("dout", 16)])

        P.op("sync", None, wait=[("dout", P.now("dout"))])
        close_scope("chain2")
        close_scope("ffn")

        return _emit_and_finish(nc, P, sems)


def _emit_and_finish(nc, P, sems):
    with nc.Block() as blk:
        def make_body(eng_name):
            def body(eng):
                for waits, emit, incs in P.ops[eng_name]:
                    for sname, val in waits:
                        eng.wait_ge(sems[sname], val)
                    if emit is None:
                        assert not incs
                        continue
                    inst = emit(eng)
                    for sname, amt in incs:
                        inst.then_inc(sems[sname], amt)
            return body
        blk.sync(make_body("sync"))
        blk.tensor(make_body("tensor"))
        blk.vector(make_body("vector"))
        blk.scalar(make_body("scalar"))
        blk.gpsimd(make_body("gpsimd"))
    mybir.codegen_inst_isa_subclasses(nc)
    return nc


def _host_inputs(L0_attn_resid, L0_attn_W_Q, L0_attn_W_K, L0_attn_W_V,
                 L0_attn_W_out, L0_attn_b_out, L0_ffn_w1, L0_ffn_b1,
                 L0_ffn_w2, L0_ffn_b2):
    bf = ml_dtypes.bfloat16
    x = np.ascontiguousarray(np.asarray(L0_attn_resid, dtype=np.float32))
    wq = np.asarray(L0_attn_W_Q, dtype=np.float32).reshape(D, NH * DH)
    wk = np.asarray(L0_attn_W_K, dtype=np.float32).reshape(D, NH * DH)
    wv = np.asarray(L0_attn_W_V, dtype=np.float32).reshape(D, NH * DH)
    wqp = np.ascontiguousarray(wq.reshape(D, NP, 128).transpose(1, 0, 2))
    wkp = np.ascontiguousarray(wk.reshape(D, NP, 128).transpose(1, 0, 2))
    wvp = np.ascontiguousarray(wv.reshape(D, NP, 128).transpose(1, 0, 2))
    wo = np.asarray(L0_attn_W_out, dtype=np.float32).transpose(1, 0, 2).reshape(NH * DH, D)
    wop = np.ascontiguousarray(wo.reshape(NCH, 128, D))
    w1 = np.asarray(L0_ffn_w1, dtype=np.float32)
    w1t = np.ascontiguousarray(w1.reshape(NHF, 128, NCH, 128).transpose(0, 2, 3, 1))
    w2 = np.asarray(L0_ffn_w2, dtype=np.float32)
    w2t = np.ascontiguousarray(w2.T.reshape(NHF, 128, D))
    b1 = np.asarray(L0_ffn_b1, dtype=np.float32)
    b1in = np.ascontiguousarray(b1.reshape(NHF, 128).T)
    boutbc = np.ascontiguousarray(
        np.broadcast_to(np.asarray(L0_attn_b_out, dtype=np.float32), (128, D)))
    b2bc = np.ascontiguousarray(
        np.broadcast_to(np.asarray(L0_ffn_b2, dtype=np.float32), (128, D)))
    cstv = np.zeros((128, NCST), dtype=np.float32)
    for k, v in CST.items():
        cstv[:, CSTI[k]] = v
    identv = np.eye(128, dtype=np.float32).astype(bf)
    tri = np.triu(np.ones((128, 128), dtype=np.float32)).astype(bf)
    bc64 = np.zeros((128, 64), dtype=np.float32)
    bc64[0, :] = 1.0
    bc64 = bc64.astype(bf)
    common = dict(wqp=wqp, wkp=wkp, wvp=wvp, wop=wop, w1t=w1t, w2t=w2t,
                  b1in=b1in, boutbc_in=boutbc, b2bc_in=b2bc, cst_in=cstv,
                  ident_in=identv, trimask_in=tri, bcast64_in=bc64)
    in_maps = []
    for b in range(8):
        m = dict(common)
        m["xin"] = np.ascontiguousarray(x[b])
        in_maps.append(m)
    return in_maps


def kernel(L0_attn_resid, L0_attn_W_Q, L0_attn_W_K, L0_attn_W_V, L0_attn_W_out,
           L0_attn_b_out, L0_ffn_w1, L0_ffn_b1, L0_ffn_w2, L0_ffn_b2):
    in_maps = _host_inputs(L0_attn_resid, L0_attn_W_Q, L0_attn_W_K, L0_attn_W_V,
                           L0_attn_W_out, L0_attn_b_out, L0_ffn_w1, L0_ffn_b1,
                           L0_ffn_w2, L0_ffn_b2)
    nc = build_nc()
    res = run_bass_kernel_spmd(nc, in_maps, core_ids=list(range(8)))
    out = np.stack([np.asarray(r["xout"]) for r in res.results], axis=0)
    return out


# version 11
# speedup vs baseline: 1.1704x; 1.0701x over previous
import contextlib
import numpy as np
import ml_dtypes
import concourse.bass as bass
import concourse.mybir as mybir
from concourse.bass_utils import run_bass_kernel_spmd
from concourse.dve_op import DveOp
from concourse.dve_spec import Spec, Src0, Src1, lower, C0, C1, C2, eq, select
import concourse.dve_ops as dve_ops

F32 = mybir.dt.float32
F16 = mybir.dt.float16
BF16 = mybir.dt.bfloat16
FP8 = mybir.dt.float8e4
AF = mybir.ActivationFunctionType
ALU = mybir.AluOpType
AX = mybir.AxisListType
LN2 = 0.6931471805599453

S = 2048
D = 1024
NT = 16
NCH = 8
NH = 16
NP = 8
DH = 64
HF = 4096
NHF = 32
GQ = 512
NG = S // GQ
USOFT = 12.0
EPS = 1e-5

_OPS = {}


def _fdt(x):
    return np.float64 if isinstance(x, np.ndarray) and x.dtype == np.float64 else np.float32


def _reg(name, spec_body, reference, row):
    spec = Spec(body=spec_body)
    sha = lower(spec, ver="v3", perf_en=False, name=name).sha("v3")
    op = DveOp(name, spec, subdim=False, uops_sha={"v3": sha}, reference=reference)
    if not any(o.name == name for o in dve_ops.OPS):
        dve_ops.OPS.append(op)
        dve_ops._SUB_OPCODE_FOR_NAME[name] = row
        dve_ops.CUSTOM_DVE_SPECS[name] = spec
        dve_ops.CUSTOM_DVE_REFS[name] = op.ref
    return op


def _register_ops():
    if _OPS:
        return
    b4 = ((Src0 >= C0) + (Src0 >= C1)) + ((Src0 >= C2) + (Src0 >= C0 * C2))

    def r4(in0, in1, s0, s1, imm2):
        t = _fdt(in0); x = in0.astype(t)
        return ((x >= s0).astype(t) + (x >= s1).astype(t)
                + (x >= imm2).astype(t) + (x >= s0 * imm2).astype(t))

    def r4b(in0, in1, s0, s1, imm2):
        return r4(in0, in1, s0, s1, imm2) + in1.astype(_fdt(in0))

    def r1b(in0, in1, s0, s1, imm2):
        t = _fdt(in0); x = in0.astype(t)
        return (x >= s0).astype(t) + in1.astype(t)

    t = Src0 * Src1 + C0
    t = t - (t >= C1) * C1
    t = t - (t >= C2) * C2

    def rmp2(in0, in1, s0, s1, imm2):
        tt = _fdt(in0)
        x = in0.astype(tt) * in1.astype(tt) + s0
        x = x - (x >= s1).astype(tt) * s1
        return x - (x >= imm2).astype(tt) * imm2

    p = Src0 + C0
    p = p - (p >= C1) * C1
    p = p - (p >= C2) * C2

    def rp2(in0, in1, s0, s1, imm2):
        tt = _fdt(in0)
        x = in0.astype(tt) + s0
        x = x - (x >= s1).astype(tt) * s1
        return x - (x >= imm2).astype(tt) * imm2

    q = Src0
    q = q - (q >= C0) * C0
    qw = (q - C1) * Src1

    def rpw(in0, in1, s0, s1, imm2):
        tt = _fdt(in0)
        x = in0.astype(tt)
        x = x - (x >= s0).astype(tt) * s0
        return (x - s1) * in1.astype(tt)

    qe = Src0
    qe = qe - (qe >= C0) * C0
    qwe = (qe - C1 + eq(Src0, C0)) * Src1

    def rpwe(in0, in1, s0, s1, imm2):
        tt = _fdt(in0)
        x = in0.astype(tt)
        r = x - (x >= s0).astype(tt) * s0
        return (r - s1 + (x == s0).astype(tt)) * in1.astype(tt)

    b0t = Src0 >= C0
    qwt = (((Src0 - b0t * C0) - C1) + (b0t - (Src0 >= C2))) * Src1

    def rpwt(in0, in1, s0, s1, imm2):
        tt = _fdt(in0)
        x = in0.astype(tt)
        r = x - (x >= s0).astype(tt) * s0
        return (r - s1 + (x >= s0).astype(tt) * (x < imm2).astype(tt)) \
            * in1.astype(tt)

    r_ = Src0
    r_ = r_ - (r_ >= C0) * C0
    r_ = r_ - (r_ >= C2) * C2
    rw = (r_ - C1) * Src1

    def rp2w(in0, in1, s0, s1, imm2):
        tt = _fdt(in0)
        x = in0.astype(tt)
        x = x - (x >= s0).astype(tt) * s0
        x = x - (x >= imm2).astype(tt) * imm2
        return (x - s1) * in1.astype(tt)

    _OPS["LAD4"] = _reg("QLAD4", b4, r4, 30)
    _OPS["LAD4B"] = _reg("QLAD4B", b4 + Src1, r4b, 31)
    _OPS["LAD1B"] = _reg("QLAD1B", (Src0 >= C0) + Src1, r1b, 1)
    _OPS["MULPEEL2"] = _reg("QMULPEEL2", t, rmp2, 2)
    _OPS["PEEL2"] = _reg("QPEEL2", p, rp2, 3)
    _OPS["PEELW"] = _reg("QPEELW", qw, rpw, 4)
    _OPS["PEEL2W"] = _reg("QPEEL2W", rw, rp2w, 5)
    _OPS["PEELWE"] = _reg("QPEELWE", qwe, rpwe, 6)

    hv1 = select(Src0 >= C0, Src1 * C2, Src1)
    hv2 = select(Src0 >= C1, hv1 * C2, hv1)

    def rhalve2(in0, in1, s0, s1, imm2):
        tt = _fdt(in0)
        x = in0.astype(tt)
        t = np.where(x >= s0, in1.astype(tt) * imm2, in1.astype(tt))
        return np.where(x >= s1, t * imm2, t)

    sp = Src0 * C0 + C1
    sp = sp - (sp >= C2) * C2

    def rscpeel(in0, in1, s0, s1, imm2):
        tt = _fdt(in0)
        u = in0.astype(tt) * s0 + s1
        return u - (u >= imm2).astype(tt) * imm2

    _OPS["HALVE2"] = _reg("QHALVE2", hv2, rhalve2, 7)
    _OPS["SCPEEL"] = _reg("QSCPEEL", sp, rscpeel, 8)
    _OPS["PEELWET"] = _reg("QPEELWET", qwt, rpwt, 9)

    pc = Src0 + C0
    pc = pc - (pc >= C1) * C1
    pc = pc - (pc >= C2) * C1

    def rp2c(in0, in1, s0, s1, imm2):
        tt = _fdt(in0)
        x = in0.astype(tt) + s0
        x = x - (x >= s1).astype(tt) * s1
        return x - (x >= imm2).astype(tt) * s1

    _OPS["PEEL2C"] = _reg("QPEEL2C", pc, rp2c, 10)


_register_ops()

_HI = float(np.float32(LN2))
_LO = float(np.float32(LN2 - _HI))
_B1_8 = float(np.float32(9 * _HI))
_B2_8 = float(np.float32(9 * LN2 - _B1_8))
_B1_16 = float(np.float32(24 * _HI))
_B2_16 = float(np.float32(24 * LN2 - _B1_16))
CST = {
    "is8": 9 * LN2, "sc8": -9 * LN2, "is16": 24 * LN2, "sc16": -24 * LN2,
    "pln2": LN2, "nln2": -LN2, "eighth": 0.125, "negu": -USOFT, "eps": EPS,
    "neghalf": -0.5, "one": 1.0,
    "nhi": -_HI, "phi": _HI, "nlo": -_LO, "plo": _LO,
    "i8h": _B1_8, "i8l": _B2_8, "s8h": -_B1_8, "s8l": -_B2_8,
    "i16h": _B1_16, "i16l": _B2_16, "s16h": -_B1_16, "s16l": -_B2_16,
}
CSTI = {k: i for i, k in enumerate(CST)}
NCST = len(CST)

ENGINES = ("sync", "tensor", "vector", "scalar", "gpsimd")
SEMS = ("din", "dg", "dz", "dwr", "dout", "t", "v", "s", "g")


class Prog:

    def __init__(self):
        self.counts = {sn: 0 for sn in SEMS}
        self.ops = {e: [] for e in ENGINES}
        self.known = {e: {} for e in ENGINES}

    def now(self, name):
        return self.counts[name]

    def op(self, eng, emit, wait=(), inc=()):
        w = []
        for sname, val in wait:
            if val <= 0:
                continue
            if self.known[eng].get(sname, 0) >= val:
                continue
            self.known[eng][sname] = val
            w.append((sname, val))
        incs = []
        for sname, amt in inc:
            self.counts[sname] += amt
            incs.append((sname, amt))
            self.known[eng][sname] = max(self.known[eng].get(sname, 0),
                                         self.counts[sname])
        self.ops[eng].append([w, emit, incs])

    def bump(self, eng, sname, amt=1):
        self.counts[sname] += amt
        self.ops[eng][-1][2].append((sname, amt))
        self.known[eng][sname] = max(self.known[eng].get(sname, 0),
                                     self.counts[sname])
        return self.counts[sname]


def build_nc(debug=None):
    nc = bass.Bass("TRN2", target_bir_lowering=False, detect_race_conditions=False)

    xin = nc.dram_tensor("xin", [S, D], F32, kind="ExternalInput")
    wqp = nc.dram_tensor("wqp", [NP, D, 128], F32, kind="ExternalInput")
    wkp = nc.dram_tensor("wkp", [NP, D, 128], F32, kind="ExternalInput")
    wvp = nc.dram_tensor("wvp", [NP, D, 128], F32, kind="ExternalInput")
    wop = nc.dram_tensor("wop", [NCH, 128, D], F32, kind="ExternalInput")
    w1t = nc.dram_tensor("w1t", [NHF, NCH, 128, 128], F32, kind="ExternalInput")
    w2t = nc.dram_tensor("w2t", [NHF, 128, D], F32, kind="ExternalInput")
    b1in = nc.dram_tensor("b1in", [128, NHF], F32, kind="ExternalInput")
    boutbc_in = nc.dram_tensor("boutbc_in", [128, D], F32, kind="ExternalInput")
    b2bc_in = nc.dram_tensor("b2bc_in", [128, D], F32, kind="ExternalInput")
    cst_in = nc.dram_tensor("cst_in", [128, NCST], F32, kind="ExternalInput")
    ident_in = nc.dram_tensor("ident_in", [128, 128], BF16, kind="ExternalInput")
    trimask_in = nc.dram_tensor("trimask_in", [128, 128], BF16, kind="ExternalInput")
    bcast64_in = nc.dram_tensor("bcast64_in", [128, 64], BF16, kind="ExternalInput")
    xout = nc.dram_tensor("xout", [S, D], F32, kind="ExternalOutput")
    w2q_dram = nc.dram_tensor("w2q_dram", [NHF, 128, D], FP8, kind="Internal")

    P = Prog()

    with contextlib.ExitStack() as est:
        E = est.enter_context
        sems = {sn: E(nc.semaphore(sn)) for sn in SEMS}

        x_tm = E(nc.sbuf_tensor("x_tm", [128, NT * D], F32))
        ln_fm = E(nc.sbuf_tensor("ln_fm", [128, NCH * S], BF16))
        cstb = E(nc.sbuf_tensor("cstb", [128, NCST], F32))
        ident = E(nc.sbuf_tensor("ident", [128, 128], BF16))
        trimask = E(nc.sbuf_tensor("trimask", [128, 128], BF16))
        bcast64 = E(nc.sbuf_tensor("bcast64", [128, 64], BF16))
        b1_sb = E(nc.sbuf_tensor("b1_sb", [128, NHF], F32))
        boutbc = E(nc.sbuf_tensor("boutbc", [128, D], F32))

        bqbc = E(nc.sbuf_tensor("bqbc", [128, D], F16))
        stat = E(nc.sbuf_tensor("stat", [128, 128], F32))
        scratch = E(nc.sbuf_tensor("scratch", [128, 2048], F32))
        sq = scratch[:, 0:1024]

        scope = {}

        def open_scope(name, specs):
            stk = contextlib.ExitStack()
            scope[name] = stk
            return {nm: stk.enter_context(nc.sbuf_tensor(f"{name}_{nm}", sh, dt))
                    for nm, sh, dt in specs}

        def close_scope(name):
            scope.pop(name).close()

        def close_all_scopes():
            for name in reversed(list(scope)):
                scope.pop(name).close()

        ps_a = E(nc.psum_tensor("ps_a", [128, 512], F32))
        ps_b = E(nc.psum_tensor("ps_b", [128, 512], F32))
        ps_z = E(nc.psum_tensor("ps_z", [128, 512], F32))
        ps_r = E(nc.psum_tensor("ps_r", [128, 512], F32))

        def cc(name, lo=0, hi=128):
            return cstb[lo:hi, CSTI[name]:CSTI[name] + 1]

        def _ld(dst_ap, src_ap):
            return lambda e, d=dst_ap, s=src_ap: e.dma_start(d, s)

        def _ms(ap, val):
            return lambda e, a=ap, vv=val: e.memset(a, vv)

        def ts(out, in0, s1, s2, op0, op1=None):
            if op1 is None:
                return lambda e: e.tensor_scalar(out, in0, s1, None, op0)
            return lambda e: e.tensor_scalar(out, in0, s1, s2, op0, op1)

        def tt(out, in0, in1, op):
            return lambda e: e.tensor_tensor(out, in0, in1, op)

        def sttf(out, in0, sc, in1, op0, op1, accum=None):
            return lambda e: e.scalar_tensor_tensor(out, in0, sc, in1, op0, op1,
                                                    accum_out=accum)

        def act(out, in_, func, bias=0.0, scale=1.0):
            return lambda e: e.activation(out, in_, func, bias=bias, scale=scale)

        def cdve(opn, out, in0, in1=None, **kw):
            if in1 is None:
                return lambda e: e._custom_dve(_OPS[opn], out=out, in0=in0, **kw)
            return lambda e: e._custom_dve(_OPS[opn], out=out, in0=in0, in1=in1, **kw)

        def mm(out, lhsT, rhs, start, stop):
            return lambda e: e.matmul(out, lhsT, rhs, start=start, stop=stop)

        def tpz(out, in_):
            return lambda e: e.transpose(out, in_, ident[:])

        def red(out, in_, op):
            return lambda e: e.tensor_reduce(out, in_, AX.X, op)

        def cpy(out, in_):
            return lambda e: (e.tensor_copy(out, in_) if hasattr(e, "tensor_copy")
                              else e.copy(out, in_))

        P.op("sync", _ld(bass.AP(x_tm, 0, [[NT * D, 128], [D, NT], [1, D]]),
                         bass.AP(xin, 0, [[D, 128], [128 * D, NT], [1, D]])),
             inc=[("din", 16)])
        for src, dst in ((cst_in, cstb), (b1in, b1_sb), (boutbc_in, boutbc),
                         (ident_in, ident), (trimask_in, trimask),
                         (bcast64_in, bcast64)):
            P.op("sync", _ld(dst[:], src[:]), inc=[("din", 16)])
        PRO_DIN = P.now("din")

        if debug == "x0":
            for i in range(NT):
                P.op("sync", _ld(bass.AP(xout, i * 128 * D, [[D, 128], [1, D]]),
                                 x_tm[:, i * D:(i + 1) * D]),
                     wait=[("din", PRO_DIN)], inc=[("dout", 16)])
            P.op("sync", None, wait=[("dout", P.now("dout"))])
            return _emit_and_finish(nc, P, sems)

        qlast = {"v": 0}
        lnlast = {"s": 0}

        L16 = [("LAD4", dict(s0=2.0 ** -13, s1=2.0 ** -12, imm2=8.0)),
               ("LAD4B", dict(s0=2.0 ** -11, s1=2.0 ** -8, imm2=4.0)),
               ("LAD4B", dict(s0=2.0 ** -7, s1=2.0 ** -4, imm2=2.0)),
               ("LAD4B", dict(s0=2.0 ** -2, s1=2.0 ** -5, imm2=2.0 ** -1)),
               ("LAD1B", dict(s0=1.0))]
        L8 = [("LAD4", dict(s0=2.0 ** -5, s1=2.0 ** -4, imm2=4.0)),
              ("LAD4B", dict(s0=2.0 ** -2, s1=1.0, imm2=2.0))]
        L8X = L8 + [("LAD4B", dict(s0=8.0, s1=16.0, imm2=32.0))]

        ETA = 2.0 ** -22
        CATCH = 2.0 - ETA

        qlastb = {}

        def walk_quant2(bufs, x_ap, n, man16, signed, xwaits, out_slices,
                        ext=False, he=False):
            wabs_, wsgn_, wacc_, wis_, wsc_, wlo_, key = bufs
            lads = L16 if man16 else (L8X if ext else L8)
            ih, il = ("i16h", "i16l") if man16 else ("i8h", "i8l")
            sh, sl_ = ("s16h", "s16l") if man16 else ("s8h", "s8l")
            guard = [("v", qlastb.get(key, 0))]
            if signed:
                P.op("scalar", act(wabs_, x_ap, AF.Abs),
                     wait=list(xwaits) + guard)
                P.op("scalar", act(wsgn_, x_ap, AF.Sign))
                sg = P.bump("scalar", "s")
                ax_ap = wabs_
                lw = [("s", sg)]
            else:
                ax_ap = x_ap
                lw = list(xwaits) + guard
            for i, (opn, kw) in enumerate(lads):
                if i == 0:
                    P.op("vector", cdve(opn, wacc_, ax_ap, **kw), wait=lw)
                else:
                    P.op("vector", cdve(opn, wacc_, ax_ap, wacc_, **kw))
            vw = P.bump("vector", "v")
            P.op("scalar", act(wis_, wacc_, AF.Exp,
                               bias=cc(ih), scale=cc("nhi")), wait=[("v", vw)])
            P.op("scalar", act(wlo_, wacc_, AF.Exp,
                               bias=cc(il), scale=cc("nlo")))
            P.op("scalar", act(wsc_, wacc_, AF.Exp,
                               bias=cc(sh), scale=cc("phi")))
            P.op("scalar", act(wacc_, wacc_, AF.Exp,
                               bias=cc(sl_), scale=cc("plo")))
            sw = P.bump("scalar", "s")
            P.op("gpsimd", tt(wis_, wis_, wlo_, ALU.mult), wait=[("s", sw)])
            P.op("gpsimd", tt(wsc_, wsc_, wacc_, ALU.mult))
            gw = P.bump("gpsimd", "g")

            def finish():
                nh = (n + 511) // 512
                for hh in range(nh):
                    lo, hi = hh * 512, min(n, hh * 512 + 512)
                    m = hi - lo
                    sl = slice(lo, hi)
                    w = [("g", gw)] if hh == 0 else ()
                    if man16:
                        P.op("vector", cdve("MULPEEL2", scratch[:, 0:m], ax_ap[:, sl],
                                            wis_[:, sl], s0=0.5, s1=2048.0, imm2=1024.0),
                             wait=w)
                        P.op("vector", cdve("PEEL2", scratch[:, 512:512 + m], scratch[:, 0:m],
                                            s0=0.0, s1=512.0, imm2=256.0))
                        P.op("vector", cdve("PEEL2", scratch[:, 0:m], scratch[:, 512:512 + m],
                                            s0=0.0, s1=128.0, imm2=64.0))
                        P.op("vector", cdve("PEEL2", scratch[:, 512:512 + m], scratch[:, 0:m],
                                            s0=0.0, s1=32.0, imm2=16.0))
                        P.op("vector", cdve("PEEL2", scratch[:, 0:m], scratch[:, 512:512 + m],
                                            s0=0.0, s1=8.0, imm2=4.0))
                        if he:
                            P.op("vector", cdve("PEEL2C", scratch[:, 512:512 + m],
                                                scratch[:, 0:m], s0=0.0, s1=2.0,
                                                imm2=CATCH))
                            P.op("vector", cdve("PEELWET", scratch[:, 1024:1024 + m],
                                                scratch[:, 512:512 + m], wsc_[:, sl],
                                                s0=1.0, s1=0.5, imm2=1.0 + ETA))
                        else:
                            P.op("vector", cdve("PEEL2W", scratch[:, 1024:1024 + m],
                                                scratch[:, 0:m],
                                                wsc_[:, sl], s0=2.0, s1=0.5, imm2=1.0))
                    else:
                        P.op("vector", cdve("MULPEEL2", scratch[:, 0:m], ax_ap[:, sl],
                                            wis_[:, sl], s0=0.5, s1=16.0, imm2=8.0),
                             wait=w)
                        P.op("vector", cdve("PEEL2", scratch[:, 512:512 + m], scratch[:, 0:m],
                                            s0=0.0, s1=4.0, imm2=2.0))
                        if he:
                            P.op("vector", cdve("PEEL2C", scratch[:, 0:m],
                                                scratch[:, 512:512 + m], s0=0.0, s1=2.0,
                                                imm2=CATCH))
                            P.op("vector", cdve("PEELWET", scratch[:, 1024:1024 + m],
                                                scratch[:, 0:m], wsc_[:, sl],
                                                s0=1.0, s1=0.5, imm2=1.0 + ETA))
                        else:
                            P.op("vector", cdve("PEELW", scratch[:, 1024:1024 + m],
                                                scratch[:, 512:512 + m],
                                                wsc_[:, sl], s0=1.0, s1=0.5))
                    if signed:
                        P.op("vector", tt(scratch[:, 1536:1536 + m], ax_ap[:, sl],
                                          scratch[:, 1024:1024 + m], ALU.subtract))
                        P.op("vector", tt(out_slices[hh], scratch[:, 1536:1536 + m],
                                          wsgn_[:, sl], ALU.mult))
                    else:
                        P.op("vector", tt(out_slices[hh], x_ap[:, sl],
                                          scratch[:, 1024:1024 + m], ALU.subtract))
                vend = P.bump("vector", "v")
                qlastb[key] = vend
                qlast["v"] = vend
                return vend
            return finish

        def walk_quant(x_ap, n, man16, signed, xwaits, out_slices, ext=False,
                       he=False):
            bufs = (wabs[:, 0:n], wsgn[:, 0:n], wacc[:, 0:n], wis[:, 0:n],
                    wsc[:, 0:n], wlo[:, 0:n], "serial")
            qlastb["serial"] = qlast["v"]
            return walk_quant2(bufs, x_ap, n, man16, signed, xwaits,
                               out_slices, ext=ext, he=he)()

        def walk_ln_stats_apply(src_t, src_ofs, dst_ap, xwaits):
            full = src_t[:, src_ofs:src_ofs + D]
            h0 = src_t[:, src_ofs:src_ofs + 512]
            h1 = src_t[:, src_ofs + 512:src_ofs + 1024]
            P.op("vector", red(stat[:, 112:113], full, ALU.add),
                 wait=list(xwaits) + [("s", lnlast["s"])])
            P.op("vector", tt(sq[:, 0:512], h0, h0, ALU.mult))
            P.op("vector", tt(sq[:, 512:1024], h1, h1, ALU.mult))
            P.op("vector", red(stat[:, 113:114], sq[:], ALU.add))
            vw = P.bump("vector", "v")
            P.op("scalar", act(stat[:, 115:116], stat[:, 112:113], AF.Identity,
                               scale=-1.0 / D), wait=[("v", vw)])
            P.op("scalar", act(stat[:, 116:117], stat[:, 113:114], AF.Identity,
                               scale=1.0 / D, bias=cc("eps")))
            P.op("scalar", act(stat[:, 122:123], stat[:, 112:113], AF.Identity))
            P.op("scalar", act(stat[:, 123:124], stat[:, 112:113], AF.Identity))
            P.op("scalar", act(stat[:, 124:125], stat[:, 112:113], AF.Identity))
            P.op("scalar", act(stat[:, 117:118], stat[:, 112:113], AF.Square,
                               scale=1.0 / D))
            P.op("scalar", act(stat[:, 118:119], stat[:, 117:118], AF.Ln, scale=-1.0,
                               bias=stat[:, 116:117]))
            P.op("scalar", act(stat[:, 119:120], stat[:, 118:119], AF.Exp,
                               scale=cc("neghalf")))
            sw = P.bump("scalar", "s")
            va = None
            if dst_ap is not None:
                P.op("vector", ts(dst_ap, full, stat[:, 115:116], stat[:, 119:120],
                                  ALU.add, ALU.mult), wait=[("s", sw)])
                va = P.bump("vector", "v")
            lnlast["s"] = sw
            return sw, va

        def walk_ln_phase(ln_tm, ps_tr, xwaits_fn):
            cS, cQ, cM3, cMQ, cM2, cL, cR = 0, 16, 32, 48, 64, 80, 96
            for i in range(NT):
                full = x_tm[:, i * D:(i + 1) * D]
                h0 = x_tm[:, i * D:i * D + 512]
                h1 = x_tm[:, i * D + 512:(i + 1) * D]
                w = list(xwaits_fn(i))
                if i == 0:
                    w += [("s", lnlast["s"])]
                P.op("vector", red(stat[:, cS + i:cS + i + 1], full, ALU.add),
                     wait=w)
                P.op("vector", tt(sq[:, 0:512], h0, h0, ALU.mult))
                P.op("vector", tt(sq[:, 512:1024], h1, h1, ALU.mult))
                P.op("vector", red(stat[:, cQ + i:cQ + i + 1], sq[:], ALU.add))
            vw = P.bump("vector", "v")
            for i in range(NT):
                P.op("scalar", act(stat[:, cM3 + i:cM3 + i + 1],
                                   stat[:, cS + i:cS + i + 1],
                                   AF.Identity, scale=-1.0 / D),
                     wait=[("v", vw)] if i == 0 else ())
            for i in range(NT):
                P.op("scalar", act(stat[:, cMQ + i:cMQ + i + 1],
                                   stat[:, cQ + i:cQ + i + 1],
                                   AF.Identity, scale=1.0 / D, bias=cc("eps")))
            for i in range(NT):
                P.op("scalar", act(stat[:, cM2 + i:cM2 + i + 1],
                                   stat[:, cS + i:cS + i + 1],
                                   AF.Square, scale=1.0 / D))
            for i in range(NT):
                P.op("scalar", act(stat[:, cL + i:cL + i + 1],
                                   stat[:, cM2 + i:cM2 + i + 1],
                                   AF.Ln, scale=-1.0,
                                   bias=stat[:, cMQ + i:cMQ + i + 1]))
            for i in range(NT):
                P.op("scalar", act(stat[:, cR + i:cR + i + 1],
                                   stat[:, cL + i:cL + i + 1],
                                   AF.Exp, scale=cc("neghalf")))
            sw = P.bump("scalar", "s")
            lnlast["s"] = sw
            for i in range(NT):
                P.op("vector", ts(ln_tm[:, i * D:(i + 1) * D],
                                  x_tm[:, i * D:(i + 1) * D],
                                  stat[:, cM3 + i:cM3 + i + 1],
                                  stat[:, cR + i:cR + i + 1],
                                  ALU.add, ALU.mult),
                     wait=[("s", sw)] if i == 0 else ())
                va = P.bump("vector", "v")
                for half in range(2):
                    for cq in range(4):
                        c = half * 4 + cq
                        P.op("tensor",
                             tpz(ps_tr[:, cq * 128:(cq + 1) * 128],
                                 ln_tm[:, i * D + c * 128: i * D + (c + 1) * 128]),
                             wait=[("v", va), ("s", P.now("s"))] if cq == 0 else [])
                    tw = P.bump("tensor", "t")
                    for cq in range(4):
                        c = half * 4 + cq
                        P.op("scalar",
                             cpy(ln_fm[:, c * S + i * 128: c * S + (i + 1) * 128],
                                 ps_tr[:, cq * 128:(cq + 1) * 128]),
                             wait=[("t", tw)] if cq == 0 else [])
                    se = P.bump("scalar", "s")
                    P.op("tensor", None, wait=[("s", se)])
            return P.now("s")

        zs = open_scope("zscope", [("zfm", [128, NCH * S], BF16)])
        zfm = zs["zfm"]
        lt = open_scope("lntm", [("ln_tm", [128, NT * D], BF16)])
        ps_tr = scope["lntm"].enter_context(nc.psum_tensor("ps_tr1", [128, 512], BF16))
        LN1_S = walk_ln_phase(lt["ln_tm"], ps_tr, lambda i: [("din", PRO_DIN)])
        close_scope("lntm")
        if debug == "ln1":
            P.op("gpsimd", _ld(bass.AP(xout, 0, [[16384, 128], [1, 16384]]),
                               ln_fm[:]),
                 wait=[("s", LN1_S)], inc=[("dout", 16)])
            P.op("sync", None, wait=[("dout", P.now("dout"))])
            close_all_scopes()
            return _emit_and_finish(nc, P, sems)
        cbufs = open_scope("chain", [("wabs", [128, D], F32), ("wsgn", [128, D], BF16),
                                     ("wacc", [128, D], F32), ("wis", [128, D], F32),
                                     ("wsc", [128, D], F32), ("wlo", [128, D], F32)],
)
        wabs, wsgn, wacc, wis, wsc, wlo = (cbufs[n] for n in
                                           ("wabs", "wsgn", "wacc", "wis", "wsc", "wlo"))
        wsetA = (wabs[:], wsgn[:], wacc[:], wis[:], wsc[:], wlo[:], "w2A")
        wb = open_scope("w2st", [("w2st", [128, D], F32),
                                 ("w2qs", [128, D], FP8)])
        w2st, w2qs = wb["w2st"], wb["w2qs"]
        att = open_scope("attn", [("qh", [128, 2 * S], BF16), ("kh", [128, 2 * S], BF16),
                                  ("wstr", [128, 3 * 1024], BF16),
                                  ("vpair", [128, NT * 130], BF16),
                                  ("probsT", [128, 4 * GQ], BF16),
                                  ("zstage", [64, S], BF16),
                                  ("recst", [128, GQ], BF16),
                                  ("recsb", [64, GQ], BF16)])
        qh, kh, wstr, vpair, probsT, zstage, recst, recsb = (
            att[n] for n in ("qh", "kh", "wstr", "vpair", "probsT", "zstage",
                             "recst", "recsb"))
        ps_s = scope["attn"].enter_context(nc.psum_tensor("ps_s", [128, 2048], F32))
        T_LN1 = P.now("t")
        zevacv = [0]
        wpend = [None]
        P.op("vector", _ms(qh[64:128, 0:S], 0.0),
             wait=[("din", PRO_DIN), ("t", T_LN1)])
        P.op("vector", _ms(qh[0:64, S:2 * S], 0.0))
        P.op("vector", _ms(kh[64:128, 0:S], 0.0))
        P.op("vector", _ms(kh[0:64, S:2 * S], 0.0))
        P.op("vector", _ms(vpair[:], 1.0))
        P.op("vector", _ms(recst[:], 0.0))
        INIT_V = P.bump("vector", "v")

        for p in range(NP):
            base = 0
            for idx, wsrc in enumerate((wqp, wkp, wvp)):
                P.op("gpsimd",
                     _ld(bass.AP(wstr, base + idx * 1024,
                                 [[3 * 1024, 128], [128, NCH], [1, 128]]),
                         bass.AP(wsrc, p * D * 128,
                                 [[128, 128], [128 * 128, NCH], [1, 128]])),
                     wait=[("t", P.now("t"))] if p >= 1 else [("t", T_LN1)],
                     inc=[("dg", 16)])
            DG_P = P.now("dg")

            evs = {}
            for which, wof, dst, scale in (("q", 0, qh, True), ("k", 1024, kh, False)):
                for g4 in range(4):
                    ps = ps_a if g4 % 2 == 0 else ps_b
                    for c in range(NCH):
                        P.op("tensor",
                             mm(ps[:, 0:512],
                                wstr[:, base + wof + c * 128: base + wof + (c + 1) * 128],
                                ln_fm[:, c * S + g4 * 512: c * S + (g4 + 1) * 512],
                                start=(c == 0), stop=(c == NCH - 1)),
                             wait=([("dg", DG_P), ("s", P.now("s"))] if c == 0 else []))
                    tw = P.bump("tensor", "t")
                    col = g4 * 512
                    nm = "eighth" if scale else "one"
                    P.op("scalar", act(dst[0:64, col:col + 512], ps[0:64, 0:512],
                                       AF.Copy, scale=cc(nm, 0, 64)),
                         wait=[("t", tw)])
                    P.op("scalar", act(dst[64:128, S + col:S + col + 512],
                                       ps[64:128, 0:512], AF.Copy, scale=cc(nm, 64, 128)))
                    P.bump("scalar", "s")
            for i in range(NT):
                ps = ps_a if i % 2 == 0 else ps_b
                for c in range(NCH):
                    P.op("tensor",
                         mm(ps[:, 0:128],
                            ln_fm[:, c * S + i * 128: c * S + (i + 1) * 128],
                            wstr[:, base + 2048 + c * 128: base + 2048 + (c + 1) * 128],
                            start=(c == 0), stop=(c == NCH - 1)),
                         wait=[("s", P.now("s"))] if c == 0 else [])
                tw = P.bump("tensor", "t")
                P.op("scalar",
                     act(bass.AP(vpair, i * 130, [[NT * 130, 128], [65, 2], [1, 64]]),
                         bass.AP(ps, 0, [[512, 128], [64, 2], [1, 64]]),
                         AF.Copy),
                     wait=[("t", tw)])
                P.bump("scalar", "s")
            VS = P.now("s")

            for hloc in range(2):
                qcol = hloc * S
                for g in range(NG):
                    njc = 4 * g + 4
                    for jb in range(0, njc, 4):
                        jhi = jb + 4
                        diag = (jhi == njc)
                        for j in range(jb, jhi):
                            if j == jb:
                                jw = ([("s", VS), ("v", INIT_V)] if jb == 0
                                      else [("s", P.now("s"))])
                            else:
                                jw = []
                            P.op("tensor",
                                 mm(ps_s[:, (j - jb) * 512:(j - jb + 1) * 512],
                                    kh[:, qcol + j * 128: qcol + (j + 1) * 128],
                                    qh[:, qcol + g * GQ: qcol + (g + 1) * GQ],
                                    start=True, stop=True),
                                 wait=jw)
                        tw = P.bump("tensor", "t")
                        P.op("scalar",
                             act(probsT[:, 0: 4 * GQ],
                                 ps_s[:, 0:2048], AF.Exp,
                                 bias=cc("negu")),
                             wait=[("t", tw)])
                        se = P.bump("scalar", "s")
                        pvw = [("s", se)]
                        if diag:
                            for dj in range(4):
                                o = 128 * dj
                                if o > 0:
                                    P.op("vector",
                                         _ms(probsT[:, dj * GQ: dj * GQ + o], 0.0),
                                         wait=[("s", se)])
                                P.op("vector",
                                     tt(probsT[:, dj * GQ + o: dj * GQ + o + 128],
                                        probsT[:, dj * GQ + o: dj * GQ + o + 128],
                                        trimask[:], ALU.mult),
                                     wait=[("s", se)])
                            VMASK = P.bump("vector", "v")
                            pvw = [("v", VMASK)]
                        for j in range(jb, jhi):
                            w = []
                            if j == jb:
                                w = list(pvw)
                            if j == 0:
                                w.append(("v", zevacv[0]))
                            P.op("tensor",
                                 mm(ps_z[0:65, 0:GQ],
                                    bass.AP(vpair, j * 130 + hloc * 65,
                                            [[NT * 130, 128], [1, 65]]),
                                    probsT[:, (j - jb) * GQ:(j - jb + 1) * GQ],
                                    start=(j == 0), stop=(j == njc - 1)),
                                 wait=w)
                    TPV = P.bump("tensor", "t")
                    def _recip(e):
                        with nc.allow_low_precision(reason="recip feeds bf16 mm operand"):
                            return e.reciprocal(recst[0:1, 0:GQ], ps_z[64:65, 0:GQ])
                    P.op("vector", _recip, wait=[("t", TPV)])
                    VREC = P.bump("vector", "v")
                    P.op("tensor", mm(ps_r[0:64, 0:GQ], bcast64[:], recst[:, 0:GQ],
                                      start=True, stop=True),
                         wait=[("v", VREC), ("s", P.now("s"))])
                    TREC = P.bump("tensor", "t")
                    P.op("vector", cpy(recsb[0:64, 0:GQ], ps_r[0:64, 0:GQ]),
                         wait=[("t", TREC)])
                    SREC = P.bump("vector", "v")
                    if hloc == 0:
                        zdst = zfm[0:64, p * S + g * GQ: p * S + (g + 1) * GQ]
                    else:
                        zdst = zstage[0:64, g * GQ:(g + 1) * GQ]
                    zw = []
                    if hloc == 1 and g == 0:
                        zw.append(("dz", P.now("dz")))
                    P.op("vector", tt(zdst, ps_z[0:64, 0:GQ], recsb[0:64, 0:GQ],
                                      ALU.mult), wait=zw)
                    zevacv[0] = P.bump("vector", "v")
                if hloc == 1:
                    P.op("sync", _ld(zfm[64:128, p * S:(p + 1) * S], zstage[0:64, 0:S]),
                         wait=[("v", P.now("v"))], inc=[("dz", 16)])
            if p == 0:
                P.op("sync", _ld(w2st[:], b2bc_in[:]),
                     wait=[("t", T_LN1)], inc=[("din", 16)])
                walk_quant(w2st[:], D, True, True,
                           [("din", P.now("din"))],
                           [bqbc[:, 0:512], bqbc[:, 512:1024]], he=True)
                qlastb["wh0"] = qlast["v"]
                qlastb["wh1"] = qlast["v"]
            for c in range(4 * p, 4 * p + 4):
                lw = ([("t", T_LN1), ("s", P.now("s"))] if c == 0
                      else [("s", P.now("s")), ("dwr", 16 * max(c - 1, 0))])
                P.op("sync", _ld(w2st[:],
                                 bass.AP(w2t, c * 128 * D, [[D, 128], [1, D]])),
                     wait=lw, inc=[("din", 16)])
                din_c = P.now("din")
                for h2 in range(2):
                    o = h2 * 512
                    wbset = (wabs[:, o:o + 512], wsgn[:, o:o + 512],
                             wacc[:, o:o + 512], wis[:, o:o + 512],
                             wsc[:, o:o + 512], wlo[:, o:o + 512], f"wh{h2}")
                    fin = walk_quant2(wbset, w2st[:, o:o + 512], 512, False,
                                      True, [("din", din_c)],
                                      [w2qs[:, o:o + 512]], he=True)

                    def wfin(f=fin, c=c, h2=h2):
                        if h2 == 0 and c >= 1:
                            P.op("vector", None, wait=[("dwr", 16 * c)])
                        vend = f()
                        if h2 == 1:
                            P.op("gpsimd",
                                 _ld(bass.AP(w2q_dram, c * 128 * D,
                                             [[D, 128], [1, D]]), w2qs[:]),
                                 wait=[("v", vend)], inc=[("dwr", 16)])
                    if wpend[0] is not None:
                        wpend[0]()
                    wpend[0] = wfin
        wpend[0]()
        wpend[0] = None
        W2_DWR = P.now("dwr")
        if debug in ("w2q", "w2qb"):
            cofs = 0 if debug == "w2q" else 16
            for cx in range(16):
                c = cx + cofs
                P.op("sync", _ld(w2qs[:],
                                 bass.AP(w2q_dram, c * 128 * D, [[D, 128], [1, D]])),
                     wait=[("dwr", W2_DWR), ("dg", P.now("dg"))], inc=[("din", 16)])
                P.op("gpsimd", _ld(bass.AP(xout, cx * 1024,
                                           [[16384, 128], [1, 1024]]),
                                   w2qs[:]),
                     wait=[("din", P.now("din"))], inc=[("dg", 16)])
            P.op("sync", None, wait=[("dg", P.now("dg"))])
            close_all_scopes()
            return _emit_and_finish(nc, P, sems)
        if debug == "bq":
            P.op("sync", _ld(bass.AP(xout, 0, [[D, 128], [1, D]]), bqbc[:]),
                 wait=[("v", P.now("v"))], inc=[("dout", 16)])
            P.op("sync", None, wait=[("dout", P.now("dout"))])
            close_all_scopes()
            return _emit_and_finish(nc, P, sems)
        ATT_DZ = P.now("dz")
        ATT_V = P.now("v")
        close_scope("attn")
        close_scope("w2st")
        close_scope("chain")
        ob = open_scope("outp", [("woutb", [128, NCH * D], BF16),
                                 ("attn_t", [128, D], F32)])
        woutb, attn_t = ob["woutb"], ob["attn_t"]

        P.op("gpsimd", _ld(bass.AP(woutb, 0, [[NCH * D, 128], [D, NCH], [1, D]]),
                           bass.AP(wop, 0, [[D, 128], [128 * D, NCH], [1, D]])),
             wait=[("t", P.now("t")), ("v", ATT_V), ("s", P.now("s"))],
             inc=[("dg", 16)])
        WOUT_DG = P.now("dg")

        op_evacv = {}
        S_ATT = P.now("s")
        for i in range(NT):
            for half in range(2):
                ps = ps_a if half == 0 else ps_b
                for c in range(NCH):
                    if c == 0:
                        if half in op_evacv:
                            mmw = [("v", op_evacv[half])]
                        else:
                            mmw = [("dg", WOUT_DG), ("dz", ATT_DZ),
                                   ("v", ATT_V), ("s", S_ATT)]
                    else:
                        mmw = []
                    P.op("tensor",
                         mm(ps[:, 0:512],
                            zfm[:, c * S + i * 128: c * S + (i + 1) * 128],
                            woutb[:, c * D + half * 512: c * D + half * 512 + 512],
                            start=(c == 0), stop=(c == NCH - 1)),
                         wait=mmw)
                tw = P.bump("tensor", "t")
                P.op("vector",
                     sttf(attn_t[:, half * 512: half * 512 + 512], ps[:, 0:512],
                          1.0, boutbc[:, half * 512: half * 512 + 512],
                          ALU.mult, ALU.add),
                     wait=[("t", tw)])
                op_evacv[half] = P.bump("vector", "v")
            sw, _ = walk_ln_stats_apply(attn_t, 0, None, [])
            P.op("vector", ts(attn_t[:], attn_t[:], stat[:, 115:116], stat[:, 119:120],
                              ALU.add, ALU.mult), wait=[("s", sw)])
            P.op("vector", tt(x_tm[:, i * D:(i + 1) * D],
                              x_tm[:, i * D:(i + 1) * D], attn_t[:], ALU.add))
            P.bump("vector", "v")
        OUTPROJ_V = P.now("v")
        close_scope("outp")
        close_scope("zscope")

        if debug == "x2":
            for i in range(NT):
                P.op("sync", _ld(bass.AP(xout, i * 128 * D, [[D, 128], [1, D]]),
                                 x_tm[:, i * D:(i + 1) * D]),
                     wait=[("v", OUTPROJ_V)], inc=[("dout", 16)])
            P.op("sync", None, wait=[("dout", P.now("dout"))])
            debug = "STOP"

        if debug == "STOP":
            close_all_scopes()
            return _emit_and_finish(nc, P, sems)
        lt2 = open_scope("lntm2", [("ln_tm", [128, NT * D], BF16)])
        ps_tr = scope["lntm2"].enter_context(nc.psum_tensor("ps_tr2", [128, 512], BF16))
        T_OUTP = P.now("t")
        LN2_S = walk_ln_phase(lt2["ln_tm"], ps_tr,
                              lambda i: [("v", OUTPROJ_V), ("t", T_OUTP)])
        close_scope("lntm2")
        if debug == "ln2":
            P.op("gpsimd", _ld(bass.AP(xout, 0, [[16384, 128], [1, 16384]]),
                               ln_fm[:]),
                 wait=[("s", LN2_S)], inc=[("dout", 16)])
            P.op("sync", None, wait=[("dout", P.now("dout"))])
            close_all_scopes()
            return _emit_and_finish(nc, P, sems)
        fbufs = open_scope("ffn", [("hq8", [128, NHF * 1024], FP8),
                                   ("w1str", [128, 2 * 1024], BF16),
                                   ("w2qs", [128, 2 * D], FP8),
                                   ("xr_sb", [128, 1024], F32),
                                   ("ffo", [128, 2 * D], F32),
                                   ("ffq", [128, 8 * D], F16)])
        hq8, w1str, w2qs, xr_sb, ffo, ffq = (
            fbufs[n] for n in ("hq8", "w1str", "w2qs", "xr_sb", "ffo", "ffq"))
        cb2 = open_scope("chain2", [("wabs", [128, D], F32), ("wsgn", [128, D], BF16),
                                    ("wacc", [128, D], F32), ("wis", [128, D], F32),
                                    ("wsc", [128, D], F32), ("wlo", [128, D], F32)])
        wabs, wsgn, wacc, wis, wsc, wlo = (cb2[n] for n in
                                           ("wabs", "wsgn", "wacc", "wis", "wsc", "wlo"))

        first_w2_load = [True]
        hpend = [None]
        w1_tc = []
        ffn2_evac = [0]
        ffn1_evacv = {}
        w2_tc = []
        for sg in range(2):
            ffn1_evacv = {}
            col0 = sg * 1024
            for m in range(NHF):
                slot = (m % 2) * 1024
                if len(w1_tc) >= 2:
                    w1w = [("t", w1_tc[-2])]
                elif sg == 0 and m == 0:
                    w1w = [("t", P.now("t")), ("s", LN2_S)]
                else:
                    w1w = [("t", w1_tc[-1] if w1_tc else P.now("t"))]
                P.op("gpsimd",
                     _ld(bass.AP(w1str, slot, [[2 * 1024, 128], [128, NCH], [1, 128]]),
                         bass.AP(w1t, m * NCH * 128 * 128,
                                 [[128, 128], [128 * 128, NCH], [1, 128]])),
                     wait=w1w, inc=[("dg", 16)])
                dgm = P.now("dg")
                for g2 in range(2):
                    ps = ps_a if g2 == 0 else ps_b
                    for c in range(NCH):
                        P.op("tensor",
                             mm(ps[:, 0:512],
                                w1str[:, slot + c * 128: slot + (c + 1) * 128],
                                ln_fm[:, c * S + col0 + g2 * 512:
                                      c * S + col0 + g2 * 512 + 512],
                                start=(c == 0), stop=(c == NCH - 1)),
                             wait=([("dg", dgm),
                                    ("v", ffn1_evacv.get(g2, ffn2_evac[0]))]
                                   if c == 0 else []))
                    tw = P.bump("tensor", "t")
                    hk = m * 2 + g2
                    o = (hk % 2) * 512
                    hb = (wabs[:, o:o + 512], wsgn[:, o:o + 512],
                          wacc[:, o:o + 512], wis[:, o:o + 512],
                          wsc[:, o:o + 512], wlo[:, o:o + 512], f"h{hk % 2}")
                    P.op("vector", ts(xr_sb[:, o:o + 512], ps[:, 0:512],
                                      b1_sb[:, m:m + 1], 0.0, ALU.add, ALU.max),
                         wait=[("t", tw), ("v", qlastb.get(f"h{hk % 2}", 0))])
                    ffn1_evacv[g2] = P.bump("vector", "v")
                    fin = walk_quant2(
                        hb, xr_sb[:, o:o + 512], 512, False, False, [],
                        [hq8[:, m * 1024 + g2 * 512: m * 1024 + g2 * 512 + 512]],
                        ext=True)
                    if hpend[0] is not None:
                        hpend[0]()
                    hpend[0] = fin
                w1_tc.append(P.now("t"))
            if hpend[0] is not None:
                hpend[0]()
                hpend[0] = None
            HQ_V = P.now("v")
            if debug and debug.startswith("ffacc") and sg == 0:
                nacc = int(debug[5:])
                for c in range(nacc):
                    slot = (c % 2) * D
                    P.op("sync", _ld(w2qs[:, slot:slot + D],
                                     bass.AP(w2q_dram, c * 128 * D,
                                             [[D, 128], [1, D]])),
                         wait=[("v", HQ_V), ("dwr", W2_DWR),
                               ("t", P.now("t"))], inc=[("din", 16)])
                    P.op("tensor", mm(ps_a[:, 0:512], hq8[:, c * 1024: c * 1024 + 128],
                                      w2qs[:, slot:slot + 512],
                                      start=(c == 0), stop=(c == nacc - 1)),
                         wait=[("din", P.now("din")), ("v", HQ_V),
                               ("s", P.now("s"))])
                    P.bump("tensor", "t")
                P.op("scalar", cpy(ffo[:, 0:512], ps_a[:, 0:512]),
                     wait=[("t", P.now("t"))])
                s1 = P.bump("scalar", "s")
                P.op("sync", _ld(bass.AP(xout, 0, [[D, 128], [1, 512]]),
                                 ffo[:, 0:512]),
                     wait=[("s", s1)], inc=[("dout", 16)])
                P.op("sync", None, wait=[("dout", P.now("dout"))])
                close_all_scopes()
                return _emit_and_finish(nc, P, sems)
            if debug in ("hqa", "hqb") and sg == 0:
                hofs = 0 if debug == "hqa" else 16384
                P.op("gpsimd", _ld(bass.AP(xout, 0, [[16384, 128], [1, 16384]]),
                                   hq8[:, hofs:hofs + 16384]),
                     wait=[("v", HQ_V)], inc=[("dout", 16)])
                P.op("sync", None, wait=[("dout", P.now("dout"))])
                close_all_scopes()
                return _emit_and_finish(nc, P, sems)
            for stp in range(4):
                sts = (col0 // 128) + stp * 2
                psl = (ps_a, ps_b, ps_z, ps_r)
                for c in range(NHF):
                    slot = (c % 2) * D
                    lw = [("v", HQ_V)]
                    if first_w2_load[0]:
                        lw.append(("dwr", W2_DWR))
                        first_w2_load[0] = False
                    if len(w2_tc) >= 2:
                        lw.append(("t", w2_tc[-2]))
                    P.op("sync", _ld(w2qs[:, slot:slot + D],
                                     bass.AP(w2q_dram, c * 128 * D, [[D, 128], [1, D]])),
                         wait=lw, inc=[("din", 16)])
                    dinc = P.now("din")
                    if debug == "ffo1" and sg == 0 and stp == 0 and c == 0:
                        P.op("gpsimd", _ld(bass.AP(xout, 128 * D, [[D, 128], [1, D]]),
                                           w2qs[:, 0:D]),
                             wait=[("din", dinc)], inc=[("dg", 16)])
                        P.op("gpsimd", _ld(bass.AP(xout, 256 * D, [[D, 128], [1, 128]]),
                                           hq8[:, 0:128]),
                             wait=[("v", HQ_V)], inc=[("dg", 16)])
                        P.op("tensor", mm(ps_a[:, 0:512], hq8[:, 0:128],
                                          w2qs[:, 0:512], start=True, stop=True),
                             wait=[("din", dinc), ("v", HQ_V), ("s", P.now("s"))])
                        t1 = P.bump("tensor", "t")
                        P.op("scalar", cpy(ffo[:, 0:512], ps_a[:, 0:512]),
                             wait=[("t", t1)])
                        s1 = P.bump("scalar", "s")
                        P.op("sync", _ld(bass.AP(xout, 384 * D, [[D, 128], [1, 512]]),
                                         ffo[:, 0:512]),
                             wait=[("s", s1)], inc=[("dout", 16)])
                        P.op("sync", None, wait=[("dout", P.now("dout")),
                                                 ("dg", P.now("dg"))])
                        close_all_scopes()
                        return _emit_and_finish(nc, P, sems)
                    for sti in range(2):
                        for half in range(2):
                            ps = psl[sti * 2 + half]
                            P.op("tensor",
                                 mm(ps[:, 0:512],
                                    hq8[:, c * 1024 + (stp * 2 + sti) * 128:
                                        c * 1024 + (stp * 2 + sti + 1) * 128],
                                    w2qs[:, slot + half * 512: slot + half * 512 + 512],
                                    start=(c == 0), stop=(c == NHF - 1)),
                                 wait=([("din", dinc)]
                                       + ([("v", ffn2_evac[0])]
                                          if c == 0 else []))
                                 if (sti == 0 and half == 0) or c == 0 else [])
                    w2_tc.append(P.bump("tensor", "t"))
                tws = w2_tc[-1]
                if debug == "ffo2" and sg == 0 and stp == 0:
                    P.op("scalar", cpy(ffo[:, 0:512], ps_a[:, 0:512]),
                         wait=[("t", tws)])
                    s1 = P.bump("scalar", "s")
                    P.op("sync", _ld(bass.AP(xout, 0, [[D, 128], [1, 512]]),
                                     ffo[:, 0:512]),
                         wait=[("s", s1)], inc=[("dout", 16)])
                    P.op("sync", None, wait=[("dout", P.now("dout"))])
                    close_all_scopes()
                    return _emit_and_finish(nc, P, sems)
                for sti in range(2):
                    for half in range(2):
                        ps = psl[sti * 2 + half]
                        P.op("vector",
                             sttf(ffo[:, sti * D + half * 512:
                                      sti * D + half * 512 + 512], ps[:, 0:512],
                                  1.0, bqbc[:, half * 512: half * 512 + 512],
                                  ALU.mult, ALU.add),
                             wait=[("t", tws)] if (sti == 0 and half == 0) else ())
                ffn2_evac[0] = P.bump("vector", "v")
                vffo = ffn2_evac[0]
                for sti in range(2):
                    ti = stp * 2 + sti
                    fslc = ffo[:, sti * D:(sti + 1) * D]
                    if debug == "ffo0" and sti == 0:
                        P.op("sync", _ld(bass.AP(xout, 0, [[D, 128], [1, D]]),
                                         fslc),
                             wait=[("v", vffo)], inc=[("dout", 16)])
                        P.op("sync", None, wait=[("dout", P.now("dout"))])
                        close_all_scopes()
                        return _emit_and_finish(nc, P, sems)
                    qv = walk_quant(fslc, D, True, True, [("v", vffo)],
                                    [ffq[:, ti * D:ti * D + 512],
                                     ffq[:, ti * D + 512:(ti + 1) * D]], he=True)
                    if debug == "ffx" and sts + sti == 5:
                        P.op("sync", _ld(bass.AP(xout, 0, [[D, 128], [1, D]]),
                                         fslc),
                             wait=[("v", qv)], inc=[("dout", 16)])
                        P.op("gpsimd", _ld(bass.AP(xout, 128 * D, [[D, 128], [1, D]]),
                                           ffq[:, ti * D:(ti + 1) * D]),
                             wait=[("v", qv)], inc=[("dg", 16)])
                        P.op("sync", None, wait=[("dout", P.now("dout")),
                                                 ("dg", P.now("dg"))])
                        close_all_scopes()
                        return _emit_and_finish(nc, P, sems)
            cS, cQ, cM3, cMQ, cM2, cL, cR = 0, 16, 32, 48, 64, 80, 96
            QV = qlast["v"]
            for ti in range(8):
                full = ffq[:, ti * D:(ti + 1) * D]
                h0 = ffq[:, ti * D:ti * D + 512]
                h1 = ffq[:, ti * D + 512:(ti + 1) * D]
                w = [("v", QV), ("s", lnlast["s"])] if ti == 0 else []
                P.op("vector", red(stat[:, cS + ti:cS + ti + 1], full, ALU.add),
                     wait=w)
                P.op("vector", tt(sq[:, 0:512], h0, h0, ALU.mult))
                P.op("vector", tt(sq[:, 512:1024], h1, h1, ALU.mult))
                P.op("vector", red(stat[:, cQ + ti:cQ + ti + 1], sq[:], ALU.add))
            vw = P.bump("vector", "v")
            for ti in range(8):
                P.op("scalar", act(stat[:, cM3 + ti:cM3 + ti + 1],
                                   stat[:, cS + ti:cS + ti + 1],
                                   AF.Identity, scale=-1.0 / D),
                     wait=[("v", vw)] if ti == 0 else ())
            for ti in range(8):
                P.op("scalar", act(stat[:, cMQ + ti:cMQ + ti + 1],
                                   stat[:, cQ + ti:cQ + ti + 1],
                                   AF.Identity, scale=1.0 / D, bias=cc("eps")))
            for ti in range(8):
                P.op("scalar", act(stat[:, cM2 + ti:cM2 + ti + 1],
                                   stat[:, cS + ti:cS + ti + 1],
                                   AF.Square, scale=1.0 / D))
            for ti in range(8):
                P.op("scalar", act(stat[:, cL + ti:cL + ti + 1],
                                   stat[:, cM2 + ti:cM2 + ti + 1],
                                   AF.Ln, scale=-1.0,
                                   bias=stat[:, cMQ + ti:cMQ + ti + 1]))
            for ti in range(8):
                P.op("scalar", act(stat[:, cR + ti:cR + ti + 1],
                                   stat[:, cL + ti:cL + ti + 1],
                                   AF.Exp, scale=cc("neghalf")))
            sw = P.bump("scalar", "s")
            lnlast["s"] = sw
            for ti in range(8):
                i = (col0 // 128) + ti
                P.op("vector", ts(ffo[:, 0:D], ffq[:, ti * D:(ti + 1) * D],
                                  stat[:, cM3 + ti:cM3 + ti + 1],
                                  stat[:, cR + ti:cR + ti + 1],
                                  ALU.add, ALU.mult),
                     wait=[("s", sw)] if ti == 0 else ())
                P.op("vector", tt(x_tm[:, i * D:(i + 1) * D],
                                  x_tm[:, i * D:(i + 1) * D], ffo[:, 0:D], ALU.add))
                vres = P.bump("vector", "v")
                P.op("sync", _ld(bass.AP(xout, i * 128 * D, [[D, 128], [1, D]]),
                                 x_tm[:, i * D:(i + 1) * D]),
                     wait=[("v", vres)], inc=[("dout", 16)])

        P.op("sync", None, wait=[("dout", P.now("dout"))])
        close_scope("chain2")
        close_scope("ffn")

        return _emit_and_finish(nc, P, sems)


def _emit_and_finish(nc, P, sems):
    with nc.Block() as blk:
        def make_body(eng_name):
            def body(eng):
                for waits, emit, incs in P.ops[eng_name]:
                    for sname, val in waits:
                        eng.wait_ge(sems[sname], val)
                    if emit is None:
                        assert not incs
                        continue
                    inst = emit(eng)
                    for sname, amt in incs:
                        inst.then_inc(sems[sname], amt)
            return body
        blk.sync(make_body("sync"))
        blk.tensor(make_body("tensor"))
        blk.vector(make_body("vector"))
        blk.scalar(make_body("scalar"))
        blk.gpsimd(make_body("gpsimd"))
    mybir.codegen_inst_isa_subclasses(nc)
    return nc


def _host_inputs(L0_attn_resid, L0_attn_W_Q, L0_attn_W_K, L0_attn_W_V,
                 L0_attn_W_out, L0_attn_b_out, L0_ffn_w1, L0_ffn_b1,
                 L0_ffn_w2, L0_ffn_b2):
    bf = ml_dtypes.bfloat16
    x = np.ascontiguousarray(np.asarray(L0_attn_resid, dtype=np.float32))
    wq = np.asarray(L0_attn_W_Q, dtype=np.float32).reshape(D, NH * DH)
    wk = np.asarray(L0_attn_W_K, dtype=np.float32).reshape(D, NH * DH)
    wv = np.asarray(L0_attn_W_V, dtype=np.float32).reshape(D, NH * DH)
    wqp = np.ascontiguousarray(wq.reshape(D, NP, 128).transpose(1, 0, 2))
    wkp = np.ascontiguousarray(wk.reshape(D, NP, 128).transpose(1, 0, 2))
    wvp = np.ascontiguousarray(wv.reshape(D, NP, 128).transpose(1, 0, 2))
    wo = np.asarray(L0_attn_W_out, dtype=np.float32).transpose(1, 0, 2).reshape(NH * DH, D)
    wop = np.ascontiguousarray(wo.reshape(NCH, 128, D))
    w1 = np.asarray(L0_ffn_w1, dtype=np.float32)
    w1t = np.ascontiguousarray(w1.reshape(NHF, 128, NCH, 128).transpose(0, 2, 3, 1))
    w2 = np.asarray(L0_ffn_w2, dtype=np.float32)
    w2t = np.ascontiguousarray(w2.T.reshape(NHF, 128, D))
    b1 = np.asarray(L0_ffn_b1, dtype=np.float32)
    b1in = np.ascontiguousarray(b1.reshape(NHF, 128).T)
    boutbc = np.ascontiguousarray(
        np.broadcast_to(np.asarray(L0_attn_b_out, dtype=np.float32), (128, D)))
    b2bc = np.ascontiguousarray(
        np.broadcast_to(np.asarray(L0_ffn_b2, dtype=np.float32), (128, D)))
    cstv = np.zeros((128, NCST), dtype=np.float32)
    for k, v in CST.items():
        cstv[:, CSTI[k]] = v
    identv = np.eye(128, dtype=np.float32).astype(bf)
    tri = np.triu(np.ones((128, 128), dtype=np.float32)).astype(bf)
    bc64 = np.zeros((128, 64), dtype=np.float32)
    bc64[0, :] = 1.0
    bc64 = bc64.astype(bf)
    common = dict(wqp=wqp, wkp=wkp, wvp=wvp, wop=wop, w1t=w1t, w2t=w2t,
                  b1in=b1in, boutbc_in=boutbc, b2bc_in=b2bc, cst_in=cstv,
                  ident_in=identv, trimask_in=tri, bcast64_in=bc64)
    in_maps = []
    for b in range(8):
        m = dict(common)
        m["xin"] = np.ascontiguousarray(x[b])
        in_maps.append(m)
    return in_maps


def kernel(L0_attn_resid, L0_attn_W_Q, L0_attn_W_K, L0_attn_W_V, L0_attn_W_out,
           L0_attn_b_out, L0_ffn_w1, L0_ffn_b1, L0_ffn_w2, L0_ffn_b2):
    in_maps = _host_inputs(L0_attn_resid, L0_attn_W_Q, L0_attn_W_K, L0_attn_W_V,
                           L0_attn_W_out, L0_attn_b_out, L0_ffn_w1, L0_ffn_b1,
                           L0_ffn_w2, L0_ffn_b2)
    nc = build_nc()
    res = run_bass_kernel_spmd(nc, in_maps, core_ids=list(range(8)))
    out = np.stack([np.asarray(r["xout"]) for r in res.results], axis=0)
    return out
